# version 24
# baseline (speedup 1.0000x reference)
import contextlib

import numpy as np

import concourse.bass as bass
import concourse.mybir as mybir
from concourse.bass_utils import run_bass_kernel_spmd

F32 = mybir.dt.float32
BF16 = mybir.dt.bfloat16
AF = mybir.ActivationFunctionType
ALU = mybir.AluOpType
AX = mybir.AxisListType

NCORES = 8
D = 1024
FT = D // 128
NH = 16
HD = 64
NPAIR = 8

ENGS = ("tensor", "vector", "scalar", "gpsimd", "sync")


class Sem:
    __slots__ = ("h", "n", "name")

    def __init__(self, h, name):
        self.h = h
        self.n = 0
        self.name = name

    def tok(self):
        return (self, self.n)


class _Blk:
    def __init__(self):
        self.ops = {e: [] for e in ENGS}
        self.tag = None


class B:

    def __init__(self, nc, T):
        self.nc = nc
        self.T = T
        self.blocks = []
        self.cur = None
        self.stack = contextlib.ExitStack()
        self._sems = {}
        self.inputs = {}
        self.block()

    def sem(self, name):
        s = self._sems.get(name)
        if s is None:
            h = self.stack.enter_context(self.nc.semaphore(name))
            s = Sem(h, name)
            self._sems[name] = s
        return s

    def block(self, tag=None):
        self.cur = _Blk()
        self.cur.tag = tag or getattr(self, "phase_tag", None)
        self.blocks.append(self.cur)

    def esem(self, eng):
        return self.sem("s_" + eng)

    def emit(self, eng, fn, wait=(), inc=(), is_dma=False):
        wait = tuple(wait)
        if eng in ("vector", "scalar", "gpsimd"):
            es = self.esem(eng)
            wait = ((es, es.n),) + wait
            if not is_dma:
                inc = ((es, 1),) + tuple((s, k) for s, k in inc if s is not es)
        assert len(inc) <= 1, "one sem update per instruction"
        for s, k in inc:
            s.n += k
        self.cur.ops[eng].append((fn, tuple(wait), tuple(inc)))

    def wait(self, eng, toks):
        self.cur.ops[eng].append((None, tuple(toks), ()))

    def drain_dmas(self):
        for name, s in self._sems.items():
            if name.startswith("dma_") or name.startswith("s_wld"):
                q = name[4:] if name.startswith("dma_") else "gpsimd"
                if q not in ENGS:
                    q = "sync"
                self.wait(q, (s.tok(),))

    def dma(self, queue, dst, src, wait=(), sem=None):
        s = sem or self.sem("dma_" + queue)
        self.emit(queue, lambda e, d=dst, sr=src: e.dma_start(d, sr),
                  wait=wait, inc=((s, 16),), is_dma=True)
        return s.tok()

    def sbuf(self, name, shape, dt):
        return self.stack.enter_context(self.nc.sbuf_tensor(name, list(shape), dt))

    def psum(self, name, shape, dt):
        return self.stack.enter_context(self.nc.psum_tensor(name, list(shape), dt))

    def inp(self, name, shape, dt=F32):
        if name not in self.inputs:
            self.inputs[name] = self.nc.dram_tensor(name, list(shape), dt,
                                                    kind="ExternalInput")
        return self.inputs[name]

    def finalize(self):
        import contextlib as _ctx
        for bi, blk in enumerate(self.blocks):
            if not any(blk.ops[e] for e in ENGS):
                continue
            scope = (self.nc.named_scope("%s_%d" % (blk.tag, bi))
                     if blk.tag else _ctx.nullcontext())
            with scope, self.nc.Block() as bb:
                for ename in ENGS:
                    entries = blk.ops[ename]
                    if not entries:
                        continue

                    def replay(eng, entries=entries):
                        for fn, waits, incs in entries:
                            for s, v in waits:
                                eng.wait_ge(s.h, v)
                            if fn is None:
                                continue
                            inst = fn(eng)
                            for s, k in incs:
                                inst.then_inc(s.h, k)

                    getattr(bb, ename)(replay)
        self.stack.close()


def _tiles_mk(w):
    K, M = w.shape
    return np.ascontiguousarray(
        w.reshape(K // 128, 128, M // 128, 128).transpose(2, 1, 0, 3))


def prepare_host(inputs, T):
    h = {}
    f32 = np.float32

    n = np.arange(T, dtype=np.float64)
    g = 1.0 - 2.0 ** (-5.0 - np.arange(NH, dtype=np.float64))
    gq = np.zeros((NPAIR, 128, T), f32)
    gk = np.zeros((NPAIR, 128, T), f32)
    for p in range(NPAIR):
        for j, hh in enumerate((2 * p, 2 * p + 1)):
            gq[p, 64 * j:64 * j + 64, :] = (g[hh] ** n)[None, :].astype(f32)
            gk[p, 64 * j:64 * j + 64, :] = (g[hh] ** (-n))[None, :].astype(f32)
    h["gq"] = gq
    h["gk"] = gk
    masks = np.zeros((4, 128, 512), f32)
    ii = np.arange(128)[:, None]
    jj = np.arange(512)[None, :]
    for oi, off in enumerate((0, 128, 256, 384)):
        masks[oi] = (jj >= ii + off).astype(f32)
    h["masks"] = masks
    h["ident"] = np.eye(128, dtype=f32)
    h["ones1"] = np.ones((1, 128), f32)
    h["onesm"] = np.full((128, 1), 1.0 / D, f32)

    cols = []
    colidx = {}

    def addvec(name, v):
        v = np.asarray(v, f32).reshape(-1, 128)
        colidx[name] = len(cols)
        for t in range(v.shape[0]):
            cols.append(v[t])

    for i in range(2):
        pb = inputs["L0_attn_proj_b"][i]
        addvec(f"bq{i}", pb[0:1024])
        addvec(f"bk{i}", pb[1024:2048])
        addvec(f"bv{i}", pb[2048:3072])
        addvec(f"bxo{i}", pb[3072:4096])
        addvec(f"reto_b{i}", inputs["L0_attn_reto_b"][i])
        addvec(f"ffn1_b{i}", inputs["L0_attn_ffn1_b"][i])
        addvec(f"ffn2_b{i}", inputs["L0_attn_ffn2_b"][i])
        addvec(f"ln1_g{i}", inputs["L0_attn_ln1_g"][i])
        addvec(f"ln1_b{i}", inputs["L0_attn_ln1_b"][i])
        addvec(f"ln2_g{i}", inputs["L0_attn_ln2_g"][i])
        addvec(f"ln2_b{i}", inputs["L0_attn_ln2_b"][i])
    addvec("lnf_g", inputs["L0_attn_lnf_g"])
    addvec("lnf_b", inputs["L0_attn_lnf_b"])
    addvec("b1", inputs["L1_attn_b1"])
    addvec("b2", inputs["L1_attn_b2"])
    addvec("b4", inputs["L1_attn_b4"])
    h["cvec"] = np.ascontiguousarray(np.stack(cols, axis=1))
    h["_colidx"] = colidx
    h["b3row"] = np.asarray(inputs["L1_attn_b3"], f32).reshape(1, D)

    for i in range(2):
        pw = inputs["L0_attn_proj_w"][i]
        h[f"wq{i}"] = _tiles_mk(pw[:, 0:1024])
        h[f"wk{i}"] = _tiles_mk(pw[:, 1024:2048])
        h[f"wv{i}"] = _tiles_mk(pw[:, 2048:3072])
        h[f"wxo{i}"] = _tiles_mk(pw[:, 3072:4096])
        h[f"reto{i}"] = _tiles_mk(inputs["L0_attn_reto_w"][i])
        h[f"ffn1_{i}"] = _tiles_mk(inputs["L0_attn_ffn1_w"][i])
        h[f"ffn2_{i}"] = _tiles_mk(inputs["L0_attn_ffn2_w"][i])

    h["wrouter"] = np.ascontiguousarray(
        inputs["L0_ffn_w_router"].reshape(FT, 128, 8))
    h["fc1"] = np.stack([_tiles_mk(inputs["L0_ffn_fc1_w"][e]).transpose(
        1, 0, 2, 3).reshape(128, 16384) for e in range(8)])
    h["fc2"] = np.stack([_tiles_mk(inputs["L0_ffn_fc2_w"][e]) for e in range(8)])
    h["gatew"] = _tiles_mk(inputs["L0_ffn_gate_w"])
    h["upw"] = _tiles_mk(inputs["L0_ffn_up_w"])
    h["downw"] = _tiles_mk(inputs["L0_ffn_down_w"])

    h["w1"] = _tiles_mk(inputs["L1_attn_W1"])
    h["w2"] = _tiles_mk(inputs["L1_attn_W2"])
    h["w3"] = np.ascontiguousarray(inputs["L1_attn_W3"].reshape(FT, 128, D))
    h["w4"] = _tiles_mk(inputs["L1_attn_W4"])

    h["makerT"] = np.ascontiguousarray(
        inputs["L1_ffn_rail_maker"].transpose(3, 0, 1, 2))
    h["upT"] = np.ascontiguousarray(
        inputs["L1_ffn_rail_up_proj"].transpose(3, 0, 1, 2))
    wct = np.ascontiguousarray(inputs["L1_ffn_w_cproj"].T)
    wct = wct.reshape(32, 4, 128, 8, 128).transpose(0, 3, 2, 1, 4)
    h["wcT"] = np.ascontiguousarray(wct)
    return h


class MK:
    def __init__(self, nc, T, colidx):
        self.b = B(nc, T)
        b = self.b
        self.T = T
        self.NB = T // 512
        self.NTT = T // 128
        self.colidx = colidx
        ncol = 320
        self.SX = b.sbuf("SX", (128, FT * T), F32)
        self.A1 = b.sbuf("A1", (128, max(FT * T, 16384)), BF16)
        self.A2 = b.sbuf("A2", (128, max(FT * T, 16384)), BF16)
        self.A3 = b.sbuf("A3", (128, max(FT * T, 16384)), BF16)
        self.PP = b.sbuf("PP", (128, max(5 * T, 8192)), BF16)
        self.PT = b.sbuf("PT", (128, 1024), BF16)
        self.WR = b.sbuf("WR", (128, 4096), BF16)
        self.CB = b.sbuf("CB", (128, ncol), F32)
        self.MB = b.sbuf("MB", (128, 4 * 512), BF16)
        self.ID = b.sbuf("ID", (128, 128), BF16)
        self.O1 = b.sbuf("O1", (1, 128), BF16)
        self.OM = b.sbuf("OM", (128, 1), BF16)
        self.SS = b.sbuf("SS", (128, 64), F32)
        self.PF = b.sbuf("PF", (128, 2048), F32)
        self.pb = [b.psum(f"pb{i}", (128, 512), F32) for i in range(6)]
        self.pq = [b.psum("pq0", (128, 1024), BF16),
                   b.psum("pq1", (128, 1024), BF16)]
        self.s_pe = b.sem("s_pe")
        self.s_act = b.esem("scalar")
        self.s_dve = b.esem("vector")
        self.s_pool = b.esem("gpsimd")
        self._bank_free = {}
        self._uid = 0
        self._consts = {}

    def uid(self):
        self._uid += 1
        return self._uid

    def cv(self, name, tile):
        c = self.colidx[name] + tile
        return self.CB[:, c:c + 1]

    def bank_guard(self, bank, key=None):
        t = self._bank_free.get(key if key is not None else id(bank))
        if not t:
            return ()
        return t if isinstance(t, tuple) and t and isinstance(t[0], tuple) \
            else (t,)

    def bank_done(self, bank, tok, key=None):
        self._bank_free[key if key is not None else id(bank)] = tok

    def esem(self, eng):
        return {"scalar": self.s_act, "vector": self.s_dve,
                "gpsimd": self.s_pool, "tensor": self.s_pe}[eng]

    def const_f32(self, val):
        val = float(val)
        if val not in self._consts:
            ap = self.b.sbuf("c%d" % self.uid(), (128, 1), F32)
            self.b.emit("gpsimd", lambda e, a=ap, v=val: e.memset(a[:], v),
                        inc=((self.s_pool, 1),))
            self.b.nc.const_aps.register(F32, val, ap[:])
            self._consts[val] = ap
        return val

    def emit_init(self, cvec_ncol):
        b = self.b
        T = self.T
        xin = b.inp("xin", (D, T))
        b.dma("sync",
              self.SX[:, 0:FT * T].rearrange("p (ft t) -> p ft t", ft=FT),
              xin.ap().rearrange("(ft p) t -> p ft t", p=128))
        b.dma("sync", self.CB[:, 0:cvec_ncol],
              b.inp("cvec", (128, cvec_ncol)).ap())
        b.dma("gpsimd",
              self.MB[:, 0:4 * 512].rearrange("p (o n) -> p o n", o=4),
              b.inp("masks", (4, 128, 512)).ap().rearrange("o p n -> p o n"))
        b.dma("gpsimd", self.ID[:], b.inp("ident", (128, 128)).ap())
        b.dma("gpsimd", self.O1[:], b.inp("ones1", (1, 128)).ap())
        b.dma("gpsimd", self.OM[:], b.inp("onesm", (128, 1)).ap())
        self.const_f32(1e-5)
        self.const_f32(1.0 / 32.0)
        self.const_f32(-0.5)
        b.drain_dmas()
        b.block()

    def emit_store(self):
        b = self.b
        T = self.T
        out = self.b.nc.dram_tensor("xout", [D, T], F32, kind="ExternalOutput")
        s = b.sem("dma_out")
        b.dma("sync", out.ap().rearrange("(ft p) t -> p ft t", p=128),
              self.SX[:, 0:FT * T].rearrange("p (ft t) -> p ft t", ft=FT),
              wait=getattr(self, "ln_done", ()), sem=s)
        b.emit("sync", lambda e: e.nop(), wait=(s.tok(),))
        b.block()

    def emit_cast(self, dst=None):
        b = self.b
        T = self.T
        dst = dst if dst is not None else self.A1
        engs = ("scalar", "vector", "gpsimd")
        for ft in range(FT):
            eng = engs[ft % 3]
            sl = slice(ft * T, (ft + 1) * T)
            if eng == "scalar":
                b.emit("scalar", lambda e, s=sl: e.activation(
                    dst[:, s], self.SX[:, s], AF.Copy), inc=((self.s_act, 1),))
            else:
                b.emit(eng, lambda e, s=sl: e.tensor_copy(
                    dst[:, s], self.SX[:, s]), inc=((self.esem(eng), 1),))
        b.block()

    def warm(self, bank, wait=()):
        self.b.emit("tensor", lambda e, bank=bank: e.matmul(
            bank[0:1, 0:64], self.O1[0:1, 0:1], self.O1[0:1, 0:64],
            start=True, stop=True), wait=wait, inc=((self.s_pe, 1),))

    def emit_ln(self, kind, gname=None, bname=None, dst=None, castbuf=None,
                upto=99):
        b = self.b
        T = self.T
        cb = castbuf if castbuf is not None else self.A1
        wb = self.pb[5]
        b.block()
        apply_q = []

        def emit_apply(nb, at, mtile, rtile):
            for ft in range(FT):
                s = slice(ft * T + nb * 512, ft * T + nb * 512 + 512)
                eng = ("vector", "gpsimd")[ft % 2]
                sem = self.esem(eng)
                tgt = self.SX if kind == "plain" else dst
                b.emit(eng, lambda e, s=s, tgt=tgt, mtile=mtile: e.tensor_add(
                    tgt[:, s], self.SX[:, s], mtile), wait=(at,),
                    inc=((sem, 1),))
                b.emit(eng, lambda e, s=s, tgt=tgt, rtile=rtile: e.tensor_mul(
                    tgt[:, s], tgt[:, s], rtile), inc=((sem, 1),))
                if kind == "plain":
                    eng2 = ("gpsimd", "vector")[ft % 2]
                    b.emit(eng2, lambda e, s=s: e.tensor_copy(
                        self.A1[:, s], self.SX[:, s]), wait=(sem.tok(),))
                if kind == "affine":
                    et = sem.tok()
                    b.emit("scalar", lambda e, s=s, ft=ft: e.activation(
                        dst[:, s], dst[:, s], AF.Identity,
                        bias=self.cv(bname, ft), scale=self.cv(gname, ft)),
                        wait=(et,), inc=((self.s_act, 1),))
            par = nb % 2
            ldone = (self.s_dve.tok(), self.s_pool.tok(), self.s_act.tok())
            self.bank_done(None, ldone, ("PTm", par))
            self.bank_done(None, ldone, ("PTr", par))

        for nb in range(self.NB):
            par = nb % 2
            pmean = self.pb[0 + 2 * par]
            psq = self.pb[1 + 2 * par]
            srm = self.MB[0:1, par * 1024:par * 1024 + 512]
            srr = self.MB[0:1, par * 1024 + 512:par * 1024 + 1024]
            scr = self.PF[0:1, par * 1024 + 512:par * 1024 + 1024]
            mtile = self.PF[:, par * 1024:par * 1024 + 512]
            rtile = self.PF[:, par * 1024 + 512:par * 1024 + 1024]
            sq = [self.PT[:, 0:512], self.PT[:, 512:1024]]
            sqtok = [None] * FT
            for ft in range(FT):
                s = slice(ft * T + nb * 512, ft * T + nb * 512 + 512)
                ceng = ("scalar", "vector", "gpsimd")[ft % 3]
                csem = self.esem(ceng)
                if ceng == "scalar":
                    b.emit("scalar", lambda e, s=s: e.activation(
                        cb[:, s], self.SX[:, s], AF.Copy), inc=((csem, 1),))
                else:
                    b.emit(ceng, lambda e, s=s: e.tensor_copy(
                        cb[:, s], self.SX[:, s]), inc=((csem, 1),))
                ct = csem.tok()
                if ft >= 2:
                    w2 = (sqtok[ft - 2],) + (self.bank_guard(psq)
                                             if ft == 2 else ())
                    b.emit("tensor", lambda e, q=sq[ft % 2], ft=ft,
                           psq=psq: e.matmul(
                        psq[0:1, 0:512], self.OM[:], q,
                        start=(ft == 2), stop=False),
                        wait=w2, inc=((self.s_pe, 1),))
                    self.bank_done(None, self.s_pe.tok(), ("sq", ft % 2))
                qs = sq[ft % 2]
                b.emit("scalar", lambda e, s=s, q=qs: e.activation(
                    q, cb[:, s], AF.Square),
                    wait=(ct,) + self.bank_guard(qs, ("sq", ft % 2)))
                sqtok[ft] = self.s_act.tok()
                w1 = (ct,) + (self.bank_guard(pmean) if ft == 0 else ())
                b.emit("tensor", lambda e, s=s, ft=ft, pmean=pmean: e.matmul(
                    pmean[0:1, 0:512], self.OM[:], cb[:, s],
                    start=(ft == 0), stop=(ft == FT - 1)),
                    wait=w1, inc=((self.s_pe, 1),))
            for ft in (FT - 2, FT - 1):
                b.emit("tensor", lambda e, q=sq[ft % 2], ft=ft, psq=psq:
                       e.matmul(
                    psq[0:1, 0:512], self.OM[:], q,
                    start=False, stop=(ft == FT - 1)),
                    wait=(sqtok[ft],), inc=((self.s_pe, 1),))
                self.bank_done(None, self.s_pe.tok(), ("sq", ft % 2))
            pet = self.s_pe.tok()
            b.emit("scalar", lambda e, pmean=pmean, scr=scr: e.activation(
                scr, pmean[0:1, 0:512], AF.Square),
                wait=(pet,) + self.bank_guard(None, ("SC", par))
                + self.bank_guard(None, ("PTr", par)))
            sq_t = self.s_act.tok()
            self.warm(wb, (sq_t,))
            b.emit("vector", lambda e, psq=psq, scr=scr: e.tensor_sub(
                psq[0:1, 0:512], psq[0:1, 0:512], scr), wait=(sq_t,))
            self.bank_done(None, self.s_dve.tok(), ("SC", par))
            b.emit("vector", lambda e, pmean=pmean, srm=srm:
                   e.tensor_scalar_mul(srm, pmean[0:1, 0:512], -1.0),
                   wait=self.bank_guard(None, ("SRm", par)))
            dv_t = self.s_dve.tok()
            b.emit("scalar", lambda e, psq=psq: e.activation(
                psq[0:1, 0:512], psq[0:1, 0:512], AF.Ln, bias=1e-5),
                wait=(dv_t,))
            b.emit("scalar", lambda e, psq=psq, srr=srr: e.activation(
                srr, psq[0:1, 0:512], AF.Exp, scale=-0.5),
                wait=self.bank_guard(None, ("SRr", par)))
            rt = (dv_t, self.s_act.tok())
            self.bank_done(pmean, rt[1])
            self.bank_done(psq, rt[1])
            self.warm(wb, (rt[1],))
            b.emit("tensor", lambda e, pmean=pmean, srm=srm: e.matmul(
                pmean[:, 0:512], self.O1[:], srm, start=True, stop=True),
                wait=rt, inc=((self.s_pe, 1),))
            self.bank_done(None, self.s_pe.tok(), ("SRm", par))
            b.emit("tensor", lambda e, psq=psq, srr=srr: e.matmul(
                psq[:, 0:512], self.O1[:], srr, start=True, stop=True),
                wait=rt, inc=((self.s_pe, 1),))
            self.bank_done(None, self.s_pe.tok(), ("SRr", par))
            bt = self.s_pe.tok()
            b.emit("scalar", lambda e, pmean=pmean, mtile=mtile: e.activation(
                mtile, pmean[:, 0:512], AF.Copy),
                wait=(bt,) + self.bank_guard(None, ("PTm", par)))
            b.emit("scalar", lambda e, psq=psq, rtile=rtile: e.activation(
                rtile, psq[:, 0:512], AF.Copy),
                wait=self.bank_guard(None, ("PTr", par)))
            at = self.s_act.tok()
            self.bank_done(pmean, at)
            self.bank_done(psq, at)
            self.warm(wb, (at,))
            apply_q.append((nb, at, mtile, rtile))
            if len(apply_q) > 1:
                emit_apply(*apply_q.pop(0))
        while apply_q:
            emit_apply(*apply_q.pop(0))
        self.ln_done = (self.s_dve.tok(), self.s_pool.tok(),
                        self.s_act.tok())

    def proj(self, wname, Mt, Kt, src_bf, evac, wdram=None, banks=None,
             nb_range=None, src_off=0, src_T=None, mt_range=None,
             slot_range=None, wsem="s_wld", first_wait=()):
        b = self.b
        T = self.T
        sT = src_T if src_T is not None else T
        nbs = list(range(self.NB)) if nb_range is None else list(nb_range)
        mts = list(range(Mt)) if mt_range is None else list(mt_range)
        wdram = wdram if wdram is not None else b.inp(wname, (Mt, 128, Kt, 128))
        wsz = Kt * 128
        srange = list(slot_range) if slot_range is not None else \
            list(range(max(1, min(4, 4096 // wsz))))
        nslot = len(srange)
        assert nslot >= 2 or len(mts) == 1, "weight strip too large for ring"
        slots = [self.WR[:, i * wsz:(i + 1) * wsz] for i in srange]
        banks = banks or self.pb[0:4]
        s_w = b.sem(wsem)
        dtok = [None] * (len(mts) if mt_range is not None else Mt)
        pair_ok = (nslot % 2 == 0 and len(mts) > 1)

        def issue_dma(i):
            if pair_ok:
                if i % 2 == 1:
                    return
                w = (self.bank_guard(None, ("WR", srange[i % nslot]))
                     + self.bank_guard(None, ("WR", srange[(i + 1) % nslot])))
                if i + 1 < len(mts) and mts[i + 1] == mts[i] + 1 \
                        and (i % nslot) + 1 == (i + 1) % nslot:
                    dst = self.WR[:, srange[i % nslot] * wsz:
                                  (srange[i % nslot] + 2) * wsz].rearrange(
                        "p (t kt m) -> p t kt m", t=2, kt=Kt)
                    dtok[i] = b.dma(
                        "gpsimd", dst,
                        wdram.ap()[mts[i]:mts[i] + 2].rearrange(
                            "t p kt m -> p t kt m"),
                        wait=w, sem=s_w)
                    dtok[i + 1] = dtok[i]
                    return
            w = self.bank_guard(None, ("WR", srange[i % nslot]))
            dtok[i] = b.dma(
                "gpsimd",
                slots[i % nslot].rearrange("p (kt m) -> p kt m", kt=Kt),
                wdram.ap()[mts[i]], wait=w, sem=s_w)

        nxt = min(nslot, len(mts))
        for i in range(nxt):
            issue_dma(i)
        for i, mt in enumerate(mts):
            sl = slots[i % nslot]
            for kt in range(Kt):
                for j, nb in enumerate(nbs):
                    bank = banks[j % len(banks)]
                    wt = ()
                    if kt == 0 and j == 0:
                        wt += (s_w.tok(),)
                        if i == 0:
                            wt += tuple(first_wait)
                    if kt == 0:
                        wt += self.bank_guard(bank)
                    rhs = src_bf[:, src_off + kt * sT + nb * 512:
                                 src_off + kt * sT + nb * 512 + 512]
                    b.emit("tensor", lambda e, sl=sl, kt=kt, rhs=rhs, bank=bank,
                           st=(kt == 0), sp=(kt == Kt - 1): e.matmul(
                        bank[:, 0:512], sl[:, kt * 128:kt * 128 + 128], rhs,
                        start=st, stop=sp),
                        wait=wt, inc=((self.s_pe, 1),))
            self.bank_done(None, self.s_pe.tok(), ("WR", srange[i % nslot]))
            lim = i + nslot - (1 if pair_ok else 0)
            while nxt < len(mts) and nxt <= lim:
                issue_dma(nxt)
                nxt += 1
            
            for j, nb in enumerate(nbs):
                evac(mt, nb, banks[j % len(banks)], self.s_pe.tok())

    def emit_l1_attn(self):
        b = self.b
        b.phase_tag = "l1attn"
        T = self.T
        NTT = self.NTT
        NB = self.NB
        SXb = self.SX[:, 0:FT * T].bitcast(BF16)
        P_P = 0
        P_PT = 4 * T
        P_AO = 8 * T
        if not hasattr(self, "xpark"):
            self.xpark = self.b.nc.dram_tensor("xpark", [128, FT * T], F32,
                                               kind="Internal")
        xpark = self.xpark
        pk = b.dma("sync", xpark.ap(), self.SX[:, 0:FT * T],
                   wait=getattr(self, "ln_done", ()))
        b.wait("sync", (pk,))
        dt0 = b.dma("gpsimd", self.MB[0:1, 0:D], b.inp("b3row", (1, D)).ap())
        for half in range(2):
            bk = self.pb[4]
            b.emit("tensor", lambda e, h=half, bk=bk: e.matmul(
                bk[:, 0:512], self.O1[:],
                self.MB[0:1, h * 512:h * 512 + 512],
                start=True, stop=True),
                wait=(dt0,) + self.bank_guard(bk), inc=((self.s_pe, 1),))
            b.emit("scalar", lambda e, h=half, bk=bk: e.activation(
                self.PF[:, h * 512:h * 512 + 512], bk[:, 0:512], AF.Copy),
                wait=(self.s_pe.tok(),), inc=((self.s_act, 1),))
            self.bank_done(bk, self.s_act.tok())
        b.block()
        w3 = b.inp("w3", (FT, 128, D))
        s_w = b.sem("s_wld")
        for vh in range(2):
            wv = None
            for kt in range(FT):
                sl = self.WR[:, kt * 512:(kt + 1) * 512]
                wv = b.dma("gpsimd", sl,
                           w3.ap()[kt, :, vh * 512:vh * 512 + 512], sem=s_w)
            for tt in range(NTT):
                bk = self.pb[tt % 2]
                for kt in range(FT):
                    wt = ()
                    if kt == 0:
                        wt = (wv,) + self.bank_guard(bk)
                        if vh == 0 and tt == 0:
                            wt += self.ln_done
                    b.emit("tensor", lambda e, kt=kt, tt=tt, bk=bk,
                           st=(kt == 0), sp=(kt == FT - 1): e.matmul(
                        bk[:, 0:512],
                        self.A1[:, kt * T + tt * 128:kt * T + tt * 128 + 128],
                        self.WR[:, kt * 512:kt * 512 + 512],
                        start=st, stop=sp), wait=wt, inc=((self.s_pe, 1),))
                pt = self.s_pe.tok()
                b.emit("vector", lambda e, tt=tt, vh=vh, bk=bk: e.tensor_add(
                    self.A2[:, tt * D + vh * 512:tt * D + vh * 512 + 512],
                    bk[:, 0:512], self.PF[:, vh * 512:vh * 512 + 512]),
                    wait=(pt,), inc=((self.s_dve, 1),))
                self.bank_done(bk, self.s_dve.tok())
            b.block()

        def evac_k(mt, nb, bank, tok):
            b.emit("scalar", lambda e, mt=mt, nb=nb, bank=bank: e.activation(
                self.A3[:, mt * T + nb * 512:mt * T + nb * 512 + 512],
                bank[:, 0:512], AF.Identity, bias=self.cv("b2", mt)),
                wait=(tok,), inc=((self.s_act, 1),))
            self.bank_done(bank, self.s_act.tok())

        self.proj("w2", FT, FT, self.A1, evac_k)
        b.block()

        w1 = b.inp("w1", (FT, 128, FT, 128))
        w4 = b.inp("w4", (FT, 128, FT, 128))
        for ns in range(NB):
            def evac_q(mt, nb, bank, tok):
                qg = self.bank_guard(None, "PPq") if mt == 0 else ()
                b.emit("scalar", lambda e, mt=mt, bank=bank: e.activation(
                    self.PP[:, mt * 512:mt * 512 + 512],
                    bank[:, 0:512], AF.Identity, bias=self.cv("b1", mt)),
                    wait=(tok,) + qg, inc=((self.s_act, 1),))
                self.bank_done(bank, self.s_act.tok())

            self.proj("w1", FT, FT, self.A1, evac_q, wdram=w1,
                      banks=self.pb[4:6], nb_range=(ns,), slot_range=(0, 1),
                      wsem="s_wldq")
            q_done = self.s_act.tok()
            for ntl in range(4):
                sbanks = [self.pb[i] for i in range(4)]
                for mb in range(NB):
                    bk = sbanks[mb % 4]
                    for kt in range(FT):
                        wt = ((q_done,) + self.bank_guard(bk)) \
                            if kt == 0 else ()
                        b.emit("tensor", lambda e, kt=kt, mb=mb, bk=bk,
                               ntl=ntl, st=(kt == 0), sp=(kt == FT - 1):
                               e.matmul(
                            bk[:, 0:512],
                            self.PP[:, kt * 512 + ntl * 128:
                                    kt * 512 + ntl * 128 + 128],
                            self.A3[:, kt * T + mb * 512:
                                    kt * T + mb * 512 + 512],
                            start=st, stop=sp),
                            wait=wt, inc=((self.s_pe, 1),))
                spt = self.s_pe.tok()
                for mb in range(NB):
                    b.emit("vector", lambda e, mb=mb: e.reduce_max(
                        self.SS[:, 8 + mb:9 + mb], sbanks[mb % 4][:, 0:512],
                        AX.X), wait=(spt,), inc=((self.s_dve, 1),))
                b.emit("vector", lambda e, NB=NB: e.tensor_reduce(
                    self.SS[:, 0:1], self.SS[:, 8:8 + NB], AX.X, ALU.max),
                    inc=((self.s_dve, 1),))
                b.emit("vector", lambda e: e.tensor_scalar_mul(
                    self.SS[:, 0:1], self.SS[:, 0:1], -1.0 / 32.0),
                    inc=((self.s_dve, 1),))
                mt_ = self.s_dve.tok()
                for mb in range(NB):
                    pw = self.bank_guard(None, "pSXb") \
                        if (ntl == 0 and mb == 0) else ()
                    b.emit("scalar", lambda e, mb=mb, ntl=ntl: e.activation(
                        SXb[:, P_P + ntl * T + mb * 512:
                            P_P + ntl * T + mb * 512 + 512],
                        sbanks[mb % 4][:, 0:512], AF.Exp,
                        bias=self.SS[:, 0:1], scale=1.0 / 32.0,
                        accum_out=self.SS[:, 16 + mb:17 + mb]),
                        wait=(mt_,) + pw, inc=((self.s_act, 1),))
                    self.bank_done(sbanks[mb % 4], self.s_act.tok())
                et = self.s_act.tok()
                b.emit("vector", lambda e, NB=NB: e.tensor_reduce(
                    self.SS[:, 1:2], self.SS[:, 16:16 + NB], AX.X, ALU.add),
                    wait=(et,), inc=((self.s_dve, 1),))
                b.emit("vector", lambda e: e.reciprocal(
                    self.SS[:, 2:3], self.SS[:, 1:2]), inc=((self.s_dve, 1),))
                rt = self.s_dve.tok()
                for mb in range(NB):
                    b.emit("vector", lambda e, mb=mb, ntl=ntl:
                           e.tensor_scalar_mul(
                        SXb[:, P_P + ntl * T + mb * 512:
                            P_P + ntl * T + mb * 512 + 512],
                        SXb[:, P_P + ntl * T + mb * 512:
                            P_P + ntl * T + mb * 512 + 512],
                        self.SS[:, 2:3]), wait=(rt,), inc=((self.s_dve, 1),))
            self.bank_done(None, self.s_pe.tok(), "PPq")
            psc = self.s_dve.tok()
            for mc in range(NTT):
                pqv = self.pq[mc % 2][:, 0:512]
                for ntl in range(4):
                    w0 = ((psc,) if (mc == 0 and ntl == 0) else ())
                    b.emit("tensor", lambda e, mc=mc, ntl=ntl, pqv=pqv:
                           e.transpose(
                        pqv[:, ntl * 128:ntl * 128 + 128],
                        SXb[:, P_P + ntl * T + mc * 128:
                            P_P + ntl * T + mc * 128 + 128],
                        self.ID[:]),
                        wait=w0 + (self.bank_guard(pqv, ("pq", mc % 2))
                                   if ntl == 0 else ()),
                        inc=((self.s_pe, 1),))
                tt_ = self.s_pe.tok()
                b.emit("vector", lambda e, mc=mc, pqv=pqv: e.tensor_copy(
                    SXb[:, P_PT + mc * 512:P_PT + mc * 512 + 512], pqv),
                    wait=(tt_,) + (self.bank_guard(None, "pTSXb")
                                   if mc == 0 else ()),
                    inc=((self.s_dve, 1),))
                self.bank_done(pqv, self.s_dve.tok(), ("pq", mc % 2))
            ptev = self.s_dve.tok()
            self.bank_done(None, self.s_pe.tok(), "pSXb")
            for dt_ in range(FT):
                bk = self.pb[4 + (dt_ % 2)]
                for mc in range(NTT):
                    wt = ((ptev,) + self.bank_guard(bk)) \
                        if mc == 0 else ()
                    b.emit("tensor", lambda e, dt_=dt_, mc=mc, bk=bk,
                           st=(mc == 0), sp=(mc == NTT - 1): e.matmul(
                        bk[:, 0:512],
                        self.A2[:, mc * D + dt_ * 128:mc * D + dt_ * 128 + 128],
                        SXb[:, P_PT + mc * 512:P_PT + mc * 512 + 512],
                        start=st, stop=sp), wait=wt, inc=((self.s_pe, 1),))
                pt = self.s_pe.tok()
                b.emit("scalar", lambda e, dt_=dt_, bk=bk: e.activation(
                    self.PP[:, 8 * 512 + dt_ * 512:8 * 512 + dt_ * 512 + 512],
                    bk[:, 0:512], AF.Copy),
                    wait=(pt,) + (self.bank_guard(None, "PPr")
                                  if dt_ == 0 else ()),
                    inc=((self.s_act, 1),))
                self.bank_done(bk, self.s_act.tok())
            retev = self.s_act.tok()
            self.bank_done(None, self.s_pe.tok(), "pTSXb")

            def evac_o(mt, nb, bank, tok, ns=ns):
                b.emit("scalar", lambda e, mt=mt, bank=bank: e.activation(
                    SXb[:, P_AO + mt * T + ns * 512:
                        P_AO + mt * T + ns * 512 + 512],
                    bank[:, 0:512], AF.Identity, bias=self.cv("b4", mt)),
                    wait=(tok,), inc=((self.s_act, 1),))
                self.bank_done(bank, self.s_act.tok())

            self.proj("w4", FT, FT, self.PP, evac_o, wdram=w4,
                      banks=self.pb[4:6], nb_range=(0,), src_off=8 * 512,
                      src_T=512, slot_range=(2, 3), first_wait=(retev,))
            self.bank_done(None, self.s_pe.tok(), "PPr")
        A2f = self.A2[:, 0:FT * T].bitcast(F32)
        A3f = self.A3[:, 0:FT * T].bitcast(F32)
        b.wait("sync", (self.s_pe.tok(),))
        b.dma("sync", A2f, self.xpark.ap()[:, 0:4 * T])
        b.dma("sync", A3f, self.xpark.ap()[:, 4 * T:8 * T])
        rl = b.sem("dma_sync").tok()
        ao_done = self.s_act.tok()
        for ft in range(FT):
            eng = ("vector", "gpsimd")[ft % 2]
            wt = (rl, ao_done, self.s_dve.tok(), self.s_pool.tok())
            b.emit(eng, lambda e, ft=ft: e.tensor_copy(
                self.SX[:, ft * T:ft * T + T],
                SXb[:, P_AO + ft * T:P_AO + ft * T + T]),
                wait=wt, inc=((self.esem(eng), 1),))
        cpt = (self.s_dve.tok(), self.s_pool.tok())
        for ft in range(FT):
            eng = ("vector", "gpsimd")[ft % 2]
            b.emit(eng, lambda e, ft=ft: e.tensor_add(
                self.SX[:, ft * T:ft * T + T],
                self.SX[:, ft * T:ft * T + T],
                (A2f if ft < 4 else A3f)[:, (ft % 4) * T:(ft % 4) * T + T]),
                wait=cpt, inc=((self.esem(eng), 1),))
        b.block()
        self.emit_ln("plain")

    def emit_l0_attn(self, upto=99):
        b = self.b
        b.phase_tag = "l0attn"
        T = self.T
        NB = self.NB
        NTT = self.NTT
        if not hasattr(self, "xpark"):
            self.xpark = self.b.nc.dram_tensor("xpark", [128, FT * T], F32,
                                               kind="Internal")
        pk = b.dma("sync", self.xpark.ap(), self.SX[:, 0:FT * T],
                   wait=getattr(self, "ln_done", ()))
        b.wait("sync", (pk,))
        b.block()
        P0 = self.PP[:, 0 * T:1 * T]
        P1 = self.PP[:, 1 * T:2 * T]
        P2 = self.PP[:, 2 * T:3 * T]
        P3 = self.PP[:, 3 * T:4 * T]
        P4 = self.PP[:, 4 * T:5 * T]
        gq = b.inp("gq", (NPAIR, 128, T))
        gk = b.inp("gk", (NPAIR, 128, T))
        for li in range(2):
            self.emit_ln("affine", "ln1_g%d" % li, "ln1_b%d" % li, dst=self.A1)
            if upto <= 1:
                return
            wq = b.inp("wq%d" % li, (FT, 128, FT, 128))
            wk = b.inp("wk%d" % li, (FT, 128, FT, 128))
            wv = b.inp("wv%d" % li, (FT, 128, FT, 128))
            wxo = b.inp("wxo%d" % li, (FT, 128, FT, 128))
            s_msk = b.sem("dma_mask")
            b.dma("sync",
                  self.PF[:, 0:2048].rearrange("p (o n) -> p o n", o=4),
                  b.inp("masks", (4, 128, 512)).ap().rearrange(
                      "o p n -> p o n"),
                  wait=(s_msk.tok(),) + self.ln_done
                  + self.bank_guard(None, "PFmask"),
                  sem=s_msk)
            mskf = s_msk.tok()
            self.bank_done(None, mskf, "PFmask")
            gq_tok = [None] * NPAIR
            gk_tok = [None] * NPAIR

            def issue_gq(p):
                s = b.sem("dma_gq")
                b.dma("gpsimd", P0, gq.ap()[p],
                      wait=self.bank_guard(P0, "P0"), sem=s)
                gq_tok[p] = s.tok()

            def issue_gk(p):
                s = b.sem("dma_gk")
                b.dma("gpsimd", P1, gk.ap()[p],
                      wait=self.bank_guard(P1, "P1"), sem=s)
                gk_tok[p] = s.tok()

            issue_gq(0)
            issue_gk(0)
            for p in range(NPAIR):

                def evac_q(mt, nb, bank, tok, li=li, p=p):
                    tm = self.PT[:, (nb % 2) * 512:(nb % 2) * 512 + 512]
                    b.emit("scalar", lambda e, bank=bank, tm=tm: e.activation(
                        tm, bank[:, 0:512], AF.Identity,
                        bias=self.cv("bq%d" % li, p)),
                        wait=(tok,) + self.bank_guard(None, ("sT", nb % 2)))
                    att = self.s_act.tok()
                    self.bank_done(bank, att)
                    b.emit("vector", lambda e, nb=nb, tm=tm: e.tensor_mul(
                        P2[:, nb * 512:nb * 512 + 512], tm,
                        P0[:, nb * 512:nb * 512 + 512]),
                        wait=(att, gq_tok[p]) + self.bank_guard(P2, "P2"))
                    self.bank_done(None, self.s_dve.tok(), ("sT", nb % 2))

                self.proj("", FT, FT, self.A1, evac_q, wdram=wq,
                          mt_range=(p,), slot_range=(0,), wsem="s_wld0",
                          first_wait=(self.ln_done if p == 0 else ()))
                self.bank_done(None, self.s_dve.tok(), "P2")

                def evac_k(mt, nb, bank, tok, li=li, p=p):
                    tm = self.PT[:, (nb % 2) * 512:(nb % 2) * 512 + 512]
                    b.emit("scalar", lambda e, bank=bank, tm=tm: e.activation(
                        tm, bank[:, 0:512], AF.Identity,
                        bias=self.cv("bk%d" % li, p)),
                        wait=(tok,) + self.bank_guard(None, ("sT", nb % 2)))
                    att = self.s_act.tok()
                    self.bank_done(bank, att)
                    b.emit("vector", lambda e, nb=nb, tm=tm: e.tensor_mul(
                        P3[:, nb * 512:nb * 512 + 512], tm,
                        P1[:, nb * 512:nb * 512 + 512]),
                        wait=(att, gk_tok[p]) + self.bank_guard(P3, "P3"))
                    self.bank_done(None, self.s_dve.tok(), ("sT", nb % 2))

                self.proj("", FT, FT, self.A1, evac_k, wdram=wk,
                          mt_range=(p,), slot_range=(1,), wsem="s_wld1")
                self.bank_done(None, self.s_dve.tok(), "P3")
                qk_done = self.s_dve.tok()

                def evac_v(mt, nb, bank, tok, li=li, p=p, qk=qk_done):
                    b.emit("scalar", lambda e, nb=nb, bank=bank: e.activation(
                        P0[:, nb * 512:nb * 512 + 512], bank[:, 0:512],
                        AF.Identity, bias=self.cv("bv%d" % li, p)),
                        wait=(tok, qk))
                    self.bank_done(bank, self.s_act.tok())

                self.proj("", FT, FT, self.A1, evac_v, wdram=wv,
                          mt_range=(p,), slot_range=(2,), wsem="s_wld2")
                vfm_done = self.s_act.tok()
                self.bank_done(None, vfm_done, "P0")

                def evac_x(mt, nb, bank, tok, li=li, p=p):
                    b.emit("scalar", lambda e, nb=nb, bank=bank: e.activation(
                        P4[:, nb * 512:nb * 512 + 512], bank[:, 0:512],
                        AF.Silu, bias=self.cv("bxo%d" % li, p)),
                        wait=(tok,) + self.bank_guard(P4, "P4"))
                    self.bank_done(bank, self.s_act.tok())

                self.proj("", FT, FT, self.A1, evac_x, wdram=wxo,
                          mt_range=(p,), slot_range=(3,), wsem="s_wld3")
                self.bank_done(None, self.s_act.tok(), "P4")
                if upto <= 2:
                    continue

                for tt in range(NTT):
                    pqv = self.pq[tt % 2][:, 0:128]
                    b.emit("tensor", lambda e, tt=tt, pqv=pqv: e.transpose(
                        pqv, P0[:, tt * 128:tt * 128 + 128], self.ID[:]),
                        wait=(vfm_done, qk_done)
                        + self.bank_guard(pqv, ("pq", tt % 2)),
                        inc=((self.s_pe, 1),))
                    ttk = self.s_pe.tok()
                    b.emit("vector", lambda e, tt=tt, pqv=pqv: e.tensor_copy(
                        P1[:, tt * 128:tt * 128 + 128], pqv), wait=(ttk,))
                    self.bank_done(pqv, self.s_dve.tok(), ("pq", tt % 2))
                vtm_done = self.s_dve.tok()
                self.bank_done(None, self.s_pe.tok(), "P0")
                if p + 1 < NPAIR:
                    issue_gq(p + 1)
                if upto <= 3:
                    continue
                qk_attn = (qk_done, self.s_act.tok())

                sT = [self.MB[:, i * 512:(i + 1) * 512] for i in range(4)]
                LK = 4
                for nb in range(NB):
                    rbank = self.pb[4 + nb % 2]
                    items = [(h, mc) for h in range(2)
                             for mc in range(4 * nb + 4)]
                    n_it = len(items)
                    evtok = [None] * n_it

                    def emit_mm2(i, rbank=rbank, items=items, nb=nb):
                        h, mc = items[i]
                        fs = (mc == 0)
                        b.emit("tensor", lambda e, h=h, mc=mc, fs=fs, i=i:
                               e.matmul(
                            rbank[64 * h:64 * h + 64, 0:512],
                            P1[:, mc * 128 + 64 * h:mc * 128 + 64 * h + 64],
                            sT[i % LK], start=fs, stop=(mc == 4 * nb + 3)),
                            wait=(evtok[i],)
                            + (self.bank_guard(rbank)
                               if (fs and h == 0) else ()),
                            inc=((self.s_pe, 1),))
                        self.bank_done(None, self.s_pe.tok(), ("sTm", i % LK))

                    for i, (h, mc) in enumerate(items):
                        sb = self.pb[i % LK]
                        w0 = (qk_attn + (vtm_done,)) \
                            if (nb == 0 and i == 0) else ()
                        b.emit("tensor", lambda e, h=h, mc=mc, nb=nb, sb=sb:
                               e.matmul(
                            sb[:, 0:512],
                            P3[64 * h:64 * h + 64, mc * 128:mc * 128 + 128],
                            P2[64 * h:64 * h + 64, nb * 512:nb * 512 + 512],
                            start=True, stop=True),
                            wait=w0 + self.bank_guard(sb),
                            inc=((self.s_pe, 1),))
                        mmt = self.s_pe.tok()
                        if i >= LK:
                            emit_mm2(i - LK)
                        st = sT[i % LK]
                        stg = self.bank_guard(st, ("sTm", i % LK))
                        off = mc - 4 * nb
                        if off < 0:
                            b.emit("scalar", lambda e, sb=sb, st=st:
                                   e.activation(st, sb[:, 0:512], AF.Copy),
                                   wait=(mmt,) + stg)
                            self.bank_done(sb, self.s_act.tok())
                            evtok[i] = self.s_act.tok()
                        else:
                            b.emit("vector", lambda e, sb=sb, st=st, off=off:
                                   e.tensor_mul(
                                st, sb[:, 0:512],
                                self.PF[:, off * 512:off * 512 + 512]),
                                wait=(mmt, mskf) + stg)
                            self.bank_done(sb, self.s_dve.tok())
                            evtok[i] = self.s_dve.tok()
                    for i in range(max(0, n_it - LK), n_it):
                        emit_mm2(i)
                    rt = self.s_pe.tok()
                    tm = self.PT[:, (nb % 2) * 512:(nb % 2) * 512 + 512]
                    b.emit("scalar", lambda e, tm=tm, rbank=rbank: e.activation(
                        tm, rbank[:, 0:512], AF.Copy),
                        wait=(rt,) + self.bank_guard(None, ("sT", nb % 2)))
                    att2 = self.s_act.tok()
                    self.bank_done(rbank, att2)
                    b.emit("vector", lambda e, nb=nb, p=p, tm=tm: e.tensor_mul(
                        self.A2[:, p * T + nb * 512:p * T + nb * 512 + 512],
                        tm, P4[:, nb * 512:nb * 512 + 512]),
                        wait=(att2,))
                    self.bank_done(None, self.s_dve.tok(), ("sT", nb % 2))
                self.bank_done(None, self.s_pe.tok(), "P2")
                self.bank_done(None, self.s_pe.tok(), "P3")
                self.bank_done(None, self.s_pe.tok(), "P1")
                self.bank_done(None, self.s_dve.tok(), "P4")
                if p + 1 < NPAIR:
                    issue_gk(p + 1)
            if upto <= 4:
                return

            retg_done = self.s_dve.tok()

            def evac_r(mt, nb, bank, tok, li=li):
                b.emit("vector", lambda e, mt=mt, nb=nb, bank=bank:
                       e.scalar_tensor_tensor(
                    self.SX[:, mt * T + nb * 512:mt * T + nb * 512 + 512],
                    bank[:, 0:512], self.cv("reto_b%d" % li, mt),
                    self.SX[:, mt * T + nb * 512:mt * T + nb * 512 + 512],
                    ALU.add, ALU.add), wait=(tok,))
                self.bank_done(bank, self.s_dve.tok())

            self.proj("reto%d" % li, FT, FT, self.A2, evac_r,
                      first_wait=(retg_done,))
            if upto <= 5:
                return

            self.emit_ln("affine", "ln2_g%d" % li, "ln2_b%d" % li, dst=self.A1)
            f1 = b.inp("ffn1_%d" % li, (32, 128, FT, 128))
            f2 = b.inp("ffn2_%d" % li, (FT, 128, 32, 128))
            for cg in range(4):
                def evac_g(mt, nb, bank, tok, cg=cg, li=li):
                    c = mt - cg * 8
                    a2w = self.bank_guard(None, "A2ffn") \
                        if (c == 0 and nb == 0) else ()
                    b.emit("scalar", lambda e, c=c, nb=nb, bank=bank, mt=mt:
                           e.activation(
                        self.A2[:, c * T + nb * 512:c * T + nb * 512 + 512],
                        bank[:, 0:512], AF.Gelu,
                        bias=self.cv("ffn1_b%d" % li, mt)),
                        wait=(tok,) + a2w)
                    self.bank_done(bank, self.s_act.tok())

                self.proj("", 32, FT, self.A1, evac_g, wdram=f1,
                          mt_range=range(cg * 8, cg * 8 + 8),
                          first_wait=(self.ln_done if cg == 0 else ()))
                act_done = self.s_act.tok()
                wsz = 8 * 128
                slots = [self.WR[:, i * wsz:(i + 1) * wsz] for i in range(4)]
                s_w = b.sem("s_wld")

                def issue_f2(mt, cg=cg, slots=slots, s_w=s_w, f2=f2):
                    b.dma("gpsimd",
                          slots[mt % 4].rearrange("p (kt m) -> p kt m", kt=8),
                          f2.ap()[mt, :, cg * 8:cg * 8 + 8, :],
                          wait=self.bank_guard(None, ("WR", mt % 4)), sem=s_w)

                for mt in range(4):
                    issue_f2(mt)
                for mt in range(FT):
                    sl = slots[mt % 4]
                    wtk = s_w.tok()
                    for nb in range(NB):
                        bank = self.pb[4 + nb % 2]
                        for c in range(8):
                            wt = ()
                            if c == 0:
                                wt = (wtk, act_done) + self.bank_guard(bank)
                            b.emit("tensor", lambda e, c=c, nb=nb, sl=sl,
                                   bank=bank, st=(c == 0), sp=(c == 7):
                                   e.matmul(
                                bank[:, 0:512], sl[:, c * 128:c * 128 + 128],
                                self.A2[:, c * T + nb * 512:
                                        c * T + nb * 512 + 512],
                                start=st, stop=sp),
                                wait=wt, inc=((self.s_pe, 1),))
                        ptk = self.s_pe.tok()
                        if cg == 0:
                            b.emit("vector", lambda e, mt=mt, nb=nb, bank=bank,
                                   li=li: e.scalar_tensor_tensor(
                                self.SX[:, mt * T + nb * 512:
                                        mt * T + nb * 512 + 512],
                                bank[:, 0:512],
                                self.cv("ffn2_b%d" % li, mt),
                                self.SX[:, mt * T + nb * 512:
                                        mt * T + nb * 512 + 512],
                                ALU.add, ALU.add), wait=(ptk,))
                        else:
                            b.emit("vector", lambda e, mt=mt, nb=nb, bank=bank:
                                   e.tensor_add(
                                self.SX[:, mt * T + nb * 512:
                                        mt * T + nb * 512 + 512],
                                self.SX[:, mt * T + nb * 512:
                                        mt * T + nb * 512 + 512],
                                bank[:, 0:512]), wait=(ptk,))
                        self.bank_done(bank, self.s_dve.tok())
                    self.bank_done(None, self.s_pe.tok(), ("WR", mt % 4))
                    if mt + 4 < FT:
                        issue_f2(mt + 4)
                self.bank_done(None, self.s_pe.tok(), "A2ffn")
        self.emit_ln("affine", "lnf_g", "lnf_b", dst=self.A1)
        A2f = self.A2[:, 0:FT * T].bitcast(F32)
        A3f = self.A3[:, 0:FT * T].bitcast(F32)
        b.dma("sync", A2f, self.xpark.ap()[:, 0:4 * T], wait=self.ln_done)
        b.dma("sync", A3f, self.xpark.ap()[:, 4 * T:8 * T])
        rl = b.sem("dma_sync").tok()
        for ft in range(FT):
            eng = ("vector", "gpsimd")[ft % 2]
            b.emit(eng, lambda e, ft=ft: e.tensor_add(
                self.SX[:, ft * T:ft * T + T],
                (A2f if ft < 4 else A3f)[:, (ft % 4) * T:(ft % 4) * T + T],
                self.A1[:, ft * T:ft * T + T]),
                wait=(rl,) + self.ln_done)
        b.block()
        self.emit_ln("plain")


    def emit_l0_ffn(self):
        b = self.b
        b.phase_tag = "l0ffn"
        T = self.T
        NB = self.NB
        NTT = self.NTT
        wrt = self.sbufs_wrt if hasattr(self, "sbufs_wrt") else None
        if wrt is None:
            self.sbufs_wrt = b.sbuf("WRT", (128, 64), BF16)
            wrt = self.sbufs_wrt
        b.dma("gpsimd", wrt[:].rearrange("p (kt e) -> p kt e", kt=FT),
              b.inp("wrouter", (FT, 128, 8)).ap().rearrange("kt p e -> p kt e"))
        wrow = self.b.nc.dram_tensor("wrow", [8, T], BF16, kind="Internal")
        rt = b.sem("dma_gpsimd").tok()
        b.wait("gpsimd", (rt,))
        b.block()
        SS = self.SS
        for tt in range(NTT):
            bk = self.pb[4 + tt % 2]
            for kt in range(FT):
                w0 = self.ln_done if (tt == 0 and kt == 0) else ()
                b.emit("tensor", lambda e, tt=tt, kt=kt, bk=bk,
                       st=(kt == 0), sp=(kt == FT - 1): e.matmul(
                    bk[:, 0:8],
                    self.A1[:, kt * T + tt * 128:kt * T + tt * 128 + 128],
                    wrt[:, kt * 8:kt * 8 + 8], start=st, stop=sp),
                    wait=w0 + (self.bank_guard(bk) if kt == 0 else ()),
                    inc=((self.s_pe, 1),))
            pt = self.s_pe.tok()
            b.emit("vector", lambda e, bk=bk: e.tensor_copy(
                SS[:, 0:8], bk[:, 0:8]), wait=(pt,))
            self.bank_done(bk, self.s_dve.tok())
            b.emit("vector", lambda e: e.reduce_max(
                SS[:, 8:9], SS[:, 0:8], AX.X))
            b.emit("vector", lambda e: e.tensor_scalar_mul(
                SS[:, 9:10], SS[:, 8:9], -1.0))
            b.emit("vector", lambda e: e.tensor_scalar(
                SS[:, 16:24], SS[:, 0:8], SS[:, 8:9], None, ALU.is_equal))
            b.emit("vector", lambda e: e.scalar_tensor_tensor(
                SS[:, 24:32], SS[:, 16:24], -1e30, SS[:, 0:8],
                ALU.mult, ALU.add))
            b.emit("vector", lambda e: e.reduce_max(
                SS[:, 10:11], SS[:, 24:32], AX.X))
            dv = self.s_dve.tok()
            b.emit("scalar", lambda e: e.activation(
                SS[:, 11:12], SS[:, 10:11], AF.Exp, bias=SS[:, 9:10]),
                wait=(dv,))
            at = self.s_act.tok()
            b.emit("vector", lambda e: e.tensor_scalar_add(
                SS[:, 12:13], SS[:, 11:12], 1.0), wait=(at,))
            b.emit("vector", lambda e: e.reciprocal(
                SS[:, 13:14], SS[:, 12:13]))
            b.emit("vector", lambda e: e.tensor_scalar(
                SS[:, 32:40], SS[:, 0:8], SS[:, 10:11], None, ALU.is_ge))
            dv2 = self.s_dve.tok()
            b.emit("scalar", lambda e: e.activation(
                SS[:, 40:48], SS[:, 0:8], AF.Exp, bias=SS[:, 9:10]),
                wait=(dv2,))
            at2 = self.s_act.tok()
            b.emit("vector", lambda e: e.scalar_tensor_tensor(
                SS[:, 48:56], SS[:, 40:48], SS[:, 13:14], SS[:, 32:40],
                ALU.mult, ALU.mult), wait=(at2,))
            dv3 = self.s_dve.tok()
            b.emit("scalar", lambda e: e.activation(
                self.PT[:, 896:904], SS[:, 48:56], AF.Copy), wait=(dv3,))
            at3 = self.s_act.tok()
            b.emit("tensor", lambda e, tt=tt: e.transpose(
                self.pq[0][0:8, 0:128], self.PT[:, 896:904], self.ID[:]),
                wait=(at3,) + self.bank_guard(None, "pqw"),
                inc=((self.s_pe, 1),))
            pt2 = self.s_pe.tok()
            b.emit("vector", lambda e: e.tensor_copy(
                self.PT[0:8, 512:640], self.pq[0][0:8, 0:128]),
                wait=(pt2,) + self.bank_guard(None, "wstg"))
            self.bank_done(None, self.s_dve.tok(), "pqw")
            dv4 = self.s_dve.tok()
            b.dma("sync", wrow.ap()[:, tt * 128:tt * 128 + 128],
                  self.PT[0:8, 512:640], wait=(dv4,))
            self.bank_done(None, b.sem("dma_sync").tok(), "wstg")
        b.wait("sync", (b.sem("dma_sync").tok(),))
        b.block()

        gw = b.inp("gatew", (16, 128, FT, 128))
        uw = b.inp("upw", (16, 128, FT, 128))
        dw = b.inp("downw", (FT, 128, 16, 128))
        s_w = b.sem("s_wld")
        gsl = [self.WR[:, i * 1024:(i + 1) * 1024] for i in range(2)]
        usl = [self.WR[:, 2048 + i * 1024:2048 + (i + 1) * 1024]
               for i in range(2)]
        for cg in range(2):
            def issue_gu(ci, cg=cg):
                c = cg * 8 + ci
                b.dma("gpsimd",
                      gsl[ci % 2].rearrange("p (kt m) -> p kt m", kt=FT),
                      gw.ap()[c],
                      wait=self.bank_guard(None, ("WR", ci % 2)), sem=s_w)
                b.dma("gpsimd",
                      usl[ci % 2].rearrange("p (kt m) -> p kt m", kt=FT),
                      uw.ap()[c],
                      wait=self.bank_guard(None, ("WR", 2 + ci % 2)), sem=s_w)

            issue_gu(0)
            issue_gu(1)
            for ci in range(8):
                wtk = s_w.tok()
                for nb in range(NB):
                    bg = self.pb[0 + nb % 2]
                    bu = self.pb[2 + nb % 2]
                    for kt in range(FT):
                        wt1 = ((wtk,) + self.bank_guard(bg)) if kt == 0 else ()
                        wt2 = ((wtk,) + self.bank_guard(bu)) if kt == 0 else ()
                        b.emit("tensor", lambda e, kt=kt, nb=nb, bg=bg,
                               sl=gsl[ci % 2], st=(kt == 0),
                               sp=(kt == FT - 1): e.matmul(
                            bg[:, 0:512], sl[:, kt * 128:kt * 128 + 128],
                            self.A1[:, kt * T + nb * 512:kt * T + nb * 512 + 512],
                            start=st, stop=sp), wait=wt1,
                            inc=((self.s_pe, 1),))
                        b.emit("tensor", lambda e, kt=kt, nb=nb, bu=bu,
                               sl=usl[ci % 2], st=(kt == 0),
                               sp=(kt == FT - 1): e.matmul(
                            bu[:, 0:512], sl[:, kt * 128:kt * 128 + 128],
                            self.A1[:, kt * T + nb * 512:kt * T + nb * 512 + 512],
                            start=st, stop=sp), wait=wt2,
                            inc=((self.s_pe, 1),))
                    ptk = self.s_pe.tok()
                    st_sl = self.PT[:, (nb % 2) * 512:(nb % 2) * 512 + 512]
                    sun = self.A2[:, ci * T + nb * 512:ci * T + nb * 512 + 512]
                    b.emit("scalar", lambda e, bg=bg, sun=sun: e.activation(
                        sun, bg[:, 0:512], AF.Silu), wait=(ptk,))
                    self.bank_done(bg, self.s_act.tok())
                    b.emit("scalar", lambda e, bu=bu, st_sl=st_sl: e.activation(
                        st_sl, bu[:, 0:512], AF.Copy),
                        wait=self.bank_guard(None, ("sst", nb % 2)))
                    self.bank_done(bu, self.s_act.tok())
                    at = self.s_act.tok()
                    b.emit("vector", lambda e, st_sl=st_sl, sun=sun:
                           e.tensor_mul(sun, sun, st_sl), wait=(at,))
                    self.bank_done(None, self.s_dve.tok(), ("sst", nb % 2))
                self.bank_done(None, self.s_pe.tok(), ("WR", ci % 2))
                self.bank_done(None, self.s_pe.tok(), ("WR", 2 + ci % 2))
                if ci + 2 < 8:
                    issue_gu(ci + 2)
            su_done = self.s_dve.tok()
            dsl = [self.WR[:, i * 1024:(i + 1) * 1024] for i in range(4)]

            def issue_d(mt, cg=cg, dsl=dsl, s_w=s_w, dw=dw):
                b.dma("gpsimd",
                      dsl[mt % 4].rearrange("p (kt m) -> p kt m", kt=8),
                      dw.ap()[mt, :, cg * 8:cg * 8 + 8, :],
                      wait=self.bank_guard(None, ("WR", mt % 4)), sem=s_w)

            b.wait("gpsimd", (self.s_pe.tok(),))
            for mt in range(4):
                issue_d(mt)
            for mt in range(FT):
                sl = dsl[mt % 4]
                wtk = s_w.tok()
                for nb in range(NB):
                    bank = self.pb[4 + nb % 2]
                    for c in range(8):
                        wt = ()
                        if c == 0:
                            wt = (wtk, su_done) + self.bank_guard(bank)
                        b.emit("tensor", lambda e, c=c, nb=nb, sl=sl, bank=bank,
                               st=(c == 0), sp=(c == 7): e.matmul(
                            bank[:, 0:512], sl[:, c * 128:c * 128 + 128],
                            self.A2[:, c * T + nb * 512:c * T + nb * 512 + 512],
                            start=st, stop=sp), wait=wt,
                            inc=((self.s_pe, 1),))
                    ptk = self.s_pe.tok()
                    b.emit("vector", lambda e, mt=mt, nb=nb, bank=bank:
                           e.tensor_add(
                        self.SX[:, mt * T + nb * 512:mt * T + nb * 512 + 512],
                        self.SX[:, mt * T + nb * 512:mt * T + nb * 512 + 512],
                        bank[:, 0:512]), wait=(ptk,))
                    self.bank_done(bank, self.s_dve.tok())
                self.bank_done(None, self.s_pe.tok(), ("WR", mt % 4))
                if mt + 4 < FT:
                    issue_d(mt + 4)
            b.block()

        fc1 = b.inp("fc1", (8, 128, 16384))
        fc2 = b.inp("fc2", (8, FT, 128, FT, 128))
        s_f = b.sem("dma_gpsimd")
        gtmp = self.PP[:, 2048:2048 + NB * 512]

        def wb_tiles(ex):
            ep = ex % 2
            return [self.PP[:, ep * 4096 + i * 512:ep * 4096 + (i + 1) * 512]
                    for i in range(NB)]

        def emit_wb(ex):
            ep = ex % 2
            wbt_t = wb_tiles(ex)
            s_wr = b.sem("dma_wrow%d" % ep)
            for rnd in range(0, NB, 2):
                js = [j for j in (0, 1) if rnd + j < NB]
                for j in js:
                    b.dma("sync", self.MB[0:1, (ep * 2 + j) * 512:
                                          (ep * 2 + j) * 512 + 512],
                          wrow.ap()[ex:ex + 1, (rnd + j) * 512:
                                    (rnd + j) * 512 + 512],
                          wait=self.bank_guard(None, ("wbr", ep, j)),
                          sem=s_wr)
                wbt = s_wr.tok()
                for j in js:
                    nb = rnd + j
                    bk = self.pb[4 + j]
                    b.emit("tensor", lambda e, j=j, bk=bk, ep=ep: e.matmul(
                        bk[:, 0:512], self.O1[:],
                        self.MB[0:1, (ep * 2 + j) * 512:
                                (ep * 2 + j) * 512 + 512],
                        start=True, stop=True),
                        wait=(wbt,) + self.bank_guard(bk),
                        inc=((self.s_pe, 1),))
                    ptk = self.s_pe.tok()
                    b.emit("scalar", lambda e, nb=nb, bk=bk, w=wbt_t[nb]:
                           e.activation(w, bk[:, 0:512], AF.Copy),
                        wait=(ptk,) + self.bank_guard(None, ("wbt", ep, nb)))
                    self.bank_done(bk, self.s_act.tok())
                    self.bank_done(None, self.s_act.tok(), ("wbr", ep, j))
            return self.s_act.tok()

        wb_done = emit_wb(0)
        for ex in range(8):
            wb_t = wb_tiles(ex)
            b.dma("gpsimd", self.A3[:, 0:16384], fc1.ap()[ex],
                  wait=(s_f.tok(),) + self.bank_guard(None, "A3fc1"),
                  sem=s_f)
            ftk = s_f.tok()
            a2g = self.bank_guard(None, "A2act")
            for i in range(8):
                ptoks = [None] * NB
                for nb in range(NB):
                    bank = self.pb[nb % 4]
                    for kt in range(FT):
                        wt = ()
                        if kt == 0:
                            wt = (ftk,) + self.bank_guard(bank)
                        b.emit("tensor", lambda e, i=i, kt=kt, nb=nb,
                               bank=bank, st=(kt == 0), sp=(kt == FT - 1):
                               e.matmul(
                            bank[:, 0:512],
                            self.A3[:, (i * FT + kt) * 128:
                                    (i * FT + kt) * 128 + 128],
                            self.A1[:, kt * T + nb * 512:
                                    kt * T + nb * 512 + 512],
                            start=st, stop=sp), wait=wt,
                            inc=((self.s_pe, 1),))
                    ptk = self.s_pe.tok()
                    b.emit("scalar", lambda e, i=i, nb=nb, bank=bank:
                           e.activation(
                        self.A2[:, i * 2048 + nb * 512:
                                i * 2048 + nb * 512 + 512],
                        bank[:, 0:512], AF.Silu),
                        wait=(ptk,) + (a2g if i == 0 and nb == 0 else ()))
                    self.bank_done(bank, self.s_act.tok())
                for nb in range(NB):
                    bank = self.pb[nb % 4]
                    for kt in range(FT):
                        wt = self.bank_guard(bank) if kt == 0 else ()
                        b.emit("tensor", lambda e, i=i, kt=kt, nb=nb,
                               bank=bank, st=(kt == 0), sp=(kt == FT - 1):
                               e.matmul(
                            bank[:, 0:512],
                            self.A3[:, ((i + 8) * FT + kt) * 128:
                                    ((i + 8) * FT + kt) * 128 + 128],
                            self.A1[:, kt * T + nb * 512:
                                    kt * T + nb * 512 + 512],
                            start=st, stop=sp), wait=wt,
                            inc=((self.s_pe, 1),))
                    ptk = self.s_pe.tok()
                    gsl_ = gtmp[:, nb * 512:nb * 512 + 512]
                    b.emit("scalar", lambda e, g=gsl_, bank=bank: e.activation(
                        g, bank[:, 0:512], AF.Copy),
                        wait=(ptk,) + self.bank_guard(None, ("gt", nb)))
                    self.bank_done(bank, self.s_act.tok())
                    at4 = self.s_act.tok()
                    aslot = self.A2[:, i * 2048 + nb * 512:
                                    i * 2048 + nb * 512 + 512]
                    b.emit("vector", lambda e, g=gsl_, aslot=aslot:
                           e.tensor_mul(aslot, aslot, g), wait=(at4,))
                    self.bank_done(None, self.s_dve.tok(), ("gt", nb))
                    b.emit("gpsimd", lambda e, w=wb_t[nb], aslot=aslot:
                           e.tensor_mul(aslot, aslot, w),
                           wait=(self.s_dve.tok(), wb_done))
            act_done = self.s_pool.tok()
            last_fc1 = self.s_pe.tok()
            self.bank_done(None, last_fc1, "A3fc1")
            for nb in range(NB):
                self.bank_done(None, act_done, ("wbt", ex % 2, nb))
            if ex + 1 < 8:
                wb_done = emit_wb(ex + 1)
            fsl = [self.WR[:, i * 1024:(i + 1) * 1024] for i in range(4)]

            def issue_fc2(dt, ex=ex, fsl=fsl, s_f=s_f, fc2=fc2):
                if dt % 2 == 1:
                    return
                w = (self.bank_guard(None, ("WR", dt % 4))
                     + self.bank_guard(None, ("WR", dt % 4 + 1)))
                b.dma("gpsimd",
                      self.WR[:, (dt % 4) * 1024:
                              (dt % 4 + 2) * 1024].rearrange(
                          "p (t kt m) -> p t kt m", t=2, kt=FT),
                      fc2.ap()[ex, dt:dt + 2].rearrange(
                          "t p kt m -> p t kt m"),
                      wait=w, sem=s_f)

            for dt in range(4):
                issue_fc2(dt)
            fnxt = 4
            for dt in range(FT):
                sl = fsl[dt % 4]
                wtk = s_f.tok()
                for nb in range(NB):
                    bank = self.pb[4 + nb % 2]
                    for it in range(FT):
                        wt = ()
                        if it == 0:
                            wt = (wtk, act_done) + self.bank_guard(bank)
                        b.emit("tensor", lambda e, it=it, nb=nb, sl=sl,
                               bank=bank, st=(it == 0), sp=(it == FT - 1):
                               e.matmul(
                            bank[:, 0:512], sl[:, it * 128:it * 128 + 128],
                            self.A2[:, it * 2048 + nb * 512:
                                    it * 2048 + nb * 512 + 512],
                            start=st, stop=sp), wait=wt,
                            inc=((self.s_pe, 1),))
                    ptk = self.s_pe.tok()
                    b.emit("vector", lambda e, dt=dt, nb=nb, bank=bank:
                           e.tensor_add(
                        self.SX[:, dt * T + nb * 512:dt * T + nb * 512 + 512],
                        self.SX[:, dt * T + nb * 512:dt * T + nb * 512 + 512],
                        bank[:, 0:512]), wait=(ptk,))
                    self.bank_done(bank, self.s_dve.tok())
                self.bank_done(None, self.s_pe.tok(), ("WR", dt % 4))
                while fnxt < FT and fnxt <= dt + 3:
                    issue_fc2(fnxt)
                    fnxt += 1
            self.bank_done(None, (self.s_pe.tok(), self.s_dve.tok(),
                                  self.s_pool.tok()), "A2act")
        self.emit_ln("plain")

    def emit_l1_ffn(self):
        b = self.b
        b.phase_tag = "l1ffn"
        T = self.T
        NB = self.NB
        mcd = self.b.nc.dram_tensor("mcdram", [16, 16384], BF16,
                                    kind="Internal")
        b.dma("gpsimd", self.PP[0:16, 0:4096].rearrange(
            "p (hr i) -> p hr i", hr=256),
            b.inp("makerT", (16, 4, 64, 16)).ap().rearrange(
                "j h r i -> j (h r) i"))
        b.dma("gpsimd", self.A2[0:16, 0:16384].rearrange(
            "p (hr k) -> p hr k", hr=256),
            b.inp("upT", (16, 4, 64, 64)).ap().rearrange(
                "j h r k -> j (h r) k"))
        mtk = b.sem("dma_gpsimd").tok()
        b.wait("gpsimd", (mtk,))
        b.block()
        for hr in range(256):
            bk = self.pb[hr % 2]
            b.emit("tensor", lambda e, hr=hr, bk=bk: e.matmul(
                bk[0:16, 0:64], self.PP[0:16, hr * 16:hr * 16 + 16],
                self.A2[0:16, hr * 64:hr * 64 + 64], start=True, stop=True),
                wait=self.bank_guard(bk), inc=((self.s_pe, 1),))
            ptk = self.s_pe.tok()
            b.emit("scalar", lambda e, hr=hr, bk=bk: e.activation(
                self.A3[0:16, hr * 64:hr * 64 + 64], bk[0:16, 0:64], AF.Copy),
                wait=(ptk,))
            self.bank_done(bk, self.s_act.tok())
        stg = self.s_act.tok()
        b.dma("sync", mcd.ap(), self.A3[0:16, 0:16384], wait=(stg,))
        b.wait("sync", (b.sem("dma_sync").tok(),))
        b.block()
        b.emit("gpsimd", lambda e: e.memset(self.A3[:, 0:16384], 0))
        b.wait("gpsimd", (self.s_pool.tok(),))
        b.block()
        mcv = mcd.ap().rearrange("i (h t a par k) -> i h t a par k",
                                 h=4, t=8, a=4, par=2)
        for a in range(4):
            for par in range(2):
                dst = self.A3[32 * a + 16 * par:32 * a + 16 * par + 16,
                              0:16384].rearrange(
                    "i (h t aa k) -> i h t aa k", h=4, t=8, aa=4)[
                        :, :, :, a, 64 * par:64 * par + 64]
                b.dma("sync", dst, mcv[:, :, :, a, par, :])
        b.wait("sync", (b.sem("dma_sync").tok(),))
        b.block()
        wc = b.inp("wcT", (32, FT, 128, 4, 128))
        s_w = b.sem("s_wld")
        NCS = 16
        csl = [self.WR[:, i * 512:(i + 1) * 512] for i in range(8)] + \
            [self.PP[:, i * 512:(i + 1) * 512] for i in range(8)]

        def issue_c(gi):
            if gi % 2 == 1:
                return
            sidx = gi % NCS
            w = (self.bank_guard(None, ("cs", sidx))
                 + self.bank_guard(None, ("cs", sidx + 1)))
            if sidx < 8:
                dst = self.WR[:, sidx * 512:(sidx + 2) * 512]
            else:
                dst = self.PP[:, (sidx - 8) * 512:(sidx - 6) * 512]
            b.dma("gpsimd",
                  dst.rearrange("p (t kt m) -> p t kt m", t=2, kt=4),
                  wc.ap()[gi // FT, gi % FT:gi % FT + 2].rearrange(
                      "t p kt m -> p t kt m"),
                  wait=w, sem=s_w)

        for gi in range(NCS):
            issue_c(gi)
        self._cnxt = NCS
        h_done = [None] * 33
        rail_q = []

        def rail_item(cg, ci, nb, first):
            c = cg * 4 + ci
            pt_x = (c % 32) // 4
            half = cg % 2
            bk = self.pb[4 + (ci * NB + nb) % 2]
            w0 = self.ln_done if (cg == 0 and ci == 0 and nb == 0) else ()
            b.emit("tensor", lambda e, c=c, nb=nb, bk=bk, pt_x=pt_x:
                   e.matmul(
                bk[:, 0:512], self.A3[:, c * 128:c * 128 + 128],
                self.A1[:, pt_x * T + nb * 512:pt_x * T + nb * 512 + 512],
                start=True, stop=True),
                wait=w0 + self.bank_guard(bk), inc=((self.s_pe, 1),))
            ptk = self.s_pe.tok()
            a2g = self.bank_guard(None, ("A2h", half)) if first else ()
            b.emit("scalar", lambda e, ci=ci, nb=nb, bk=bk, half=half:
                   e.activation(
                self.A2[:, half * 8192 + ci * 2048 + nb * 512:
                        half * 8192 + ci * 2048 + nb * 512 + 512],
                bk[:, 0:512], AF.Gelu), wait=(ptk,) + a2g)
            self.bank_done(bk, self.s_act.tok())

        def queue_rail(cg):
            first = True
            for ci in range(4):
                for nb in range(NB):
                    rail_q.append((cg, ci, nb, first))
                    first = False

        def flush_rail(n):
            for _ in range(min(n, len(rail_q))):
                rail_item(*rail_q.pop(0))

        queue_rail(0)
        flush_rail(99)
        h_done[0] = self.s_act.tok()
        for cg in range(32):
            if cg + 1 < 32:
                queue_rail(cg + 1)
            half = cg % 2
            for dt in range(FT):
                gi = cg * FT + dt
                sl = csl[gi % NCS]
                wtk = s_w.tok()
                for nb in range(NB):
                    bank = self.pb[(dt * NB + nb) % 2]
                    for ci in range(4):
                        wt = ()
                        if ci == 0:
                            wt = (wtk,) + self.bank_guard(bank)
                            if dt == 0 and nb == 0:
                                wt += (h_done[cg],)
                        b.emit("tensor", lambda e, ci=ci, nb=nb, sl=sl,
                               bank=bank, half=half, st=(ci == 0),
                               sp=(ci == 3): e.matmul(
                            bank[:, 0:512], sl[:, ci * 128:ci * 128 + 128],
                            self.A2[:, half * 8192 + ci * 2048 + nb * 512:
                                    half * 8192 + ci * 2048 + nb * 512 + 512],
                            start=st, stop=sp), wait=wt,
                            inc=((self.s_pe, 1),))
                    ptk = self.s_pe.tok()
                    b.emit("vector", lambda e, dt=dt, nb=nb, bank=bank:
                           e.tensor_add(
                        self.SX[:, dt * T + nb * 512:dt * T + nb * 512 + 512],
                        self.SX[:, dt * T + nb * 512:dt * T + nb * 512 + 512],
                        bank[:, 0:512]), wait=(ptk,))
                    self.bank_done(bank, self.s_dve.tok())
                    flush_rail(1)
                self.bank_done(None, self.s_pe.tok(), ("cs", gi % NCS))
                while self._cnxt < 32 * FT and self._cnxt <= gi + NCS - 1:
                    issue_c(self._cnxt)
                    self._cnxt += 1
            flush_rail(99)
            if cg + 1 < 32:
                h_done[cg + 1] = self.s_act.tok()
            self.bank_done(None, self.s_pe.tok(), ("A2h", half))
        self.emit_ln("plain")


_CACHE = {}


def _build(phases, T, colidx, ncol):
    key = (phases, T)
    if key in _CACHE:
        return _CACHE[key]
    nc = bass.Bass("TRN2", target_bir_lowering=False)
    mk = MK(nc, T, colidx)
    mk.emit_init(ncol)
    for ph in phases:
        if ph == "nop":
            pass
        elif ph == "cast":
            mk.emit_cast(mk.A1)
        elif ph == "ln":
            mk.emit_ln("plain")
        elif ph.startswith("l0u"):
            mk.emit_l0_attn(upto=int(ph[3:]))
        elif ph == "l1attn":
            mk.emit_l1_attn()
        elif ph == "l0attn":
            mk.emit_l0_attn()
        elif ph == "l0ffn":
            mk.emit_l0_ffn()
        elif ph == "l1ffn":
            mk.emit_l1_ffn()
        else:
            raise ValueError(ph)
    mk.emit_store()
    mk.b.finalize()
    names = sorted(mk.b.inputs.keys())
    _CACHE[key] = (nc, names)
    return nc, names


def _run_phases(phase_groups, host, x_fm, T):
    ncol = host["cvec"].shape[1]
    for phases in phase_groups:
        nc, names = _build(tuple(phases), T, host["_colidx"], ncol)
        in_maps = []
        for c in range(NCORES):
            m = {}
            for nname in names:
                m[nname] = x_fm[c] if nname == "xin" else host[nname]
            in_maps.append(m)
        res = run_bass_kernel_spmd(nc, in_maps, core_ids=list(range(NCORES)))
        x_fm = [np.asarray(res.results[c]["xout"]) for c in range(NCORES)]
    return x_fm


def kernel(**inputs):
    T = 2048
    host = prepare_host(inputs, T)
    x = np.asarray(inputs["L0_attn_x"], np.float32)
    x_fm = [np.ascontiguousarray(x[c].T) for c in range(NCORES)]
    groups = [["l0attn", "l0ffn", "l1attn", "l1ffn"]]
    x_fm = _run_phases(groups, host, x_fm, T)
    out = np.stack([x_fm[c].T for c in range(NCORES)], axis=0)
    return np.ascontiguousarray(out)


# version 25
# speedup vs baseline: 1.0410x; 1.0410x over previous
import contextlib

import numpy as np

import concourse.bass as bass
import concourse.mybir as mybir
from concourse.bass_utils import run_bass_kernel_spmd

F32 = mybir.dt.float32
BF16 = mybir.dt.bfloat16
AF = mybir.ActivationFunctionType
ALU = mybir.AluOpType
AX = mybir.AxisListType

NCORES = 8
D = 1024
FT = D // 128
NH = 16
HD = 64
NPAIR = 8

ENGS = ("tensor", "vector", "scalar", "gpsimd", "sync")


class Sem:
    __slots__ = ("h", "n", "name")

    def __init__(self, h, name):
        self.h = h
        self.n = 0
        self.name = name

    def tok(self):
        return (self, self.n)


class _Blk:
    def __init__(self):
        self.ops = {e: [] for e in ENGS}
        self.tag = None


class B:

    def __init__(self, nc, T):
        self.nc = nc
        self.T = T
        self.blocks = []
        self.cur = None
        self.stack = contextlib.ExitStack()
        self._sems = {}
        self.inputs = {}
        self.block()

    def sem(self, name):
        s = self._sems.get(name)
        if s is None:
            h = self.stack.enter_context(self.nc.semaphore(name))
            s = Sem(h, name)
            self._sems[name] = s
        return s

    def block(self, tag=None):
        self.cur = _Blk()
        self.cur.tag = tag or getattr(self, "phase_tag", None)
        self.blocks.append(self.cur)

    def esem(self, eng):
        return self.sem("s_" + eng)

    def emit(self, eng, fn, wait=(), inc=(), is_dma=False):
        wait = tuple(wait)
        if eng in ("vector", "scalar", "gpsimd"):
            es = self.esem(eng)
            wait = ((es, es.n),) + wait
            if not is_dma:
                inc = ((es, 1),) + tuple((s, k) for s, k in inc if s is not es)
        assert len(inc) <= 1, "one sem update per instruction"
        for s, k in inc:
            s.n += k
        self.cur.ops[eng].append((fn, tuple(wait), tuple(inc)))

    def wait(self, eng, toks):
        self.cur.ops[eng].append((None, tuple(toks), ()))

    def drain_dmas(self):
        for name, s in self._sems.items():
            if name.startswith("dma_") or name.startswith("s_wld"):
                q = name[4:] if name.startswith("dma_") else "gpsimd"
                if q not in ENGS:
                    q = "sync"
                self.wait(q, (s.tok(),))

    def dma(self, queue, dst, src, wait=(), sem=None):
        s = sem or self.sem("dma_" + queue)
        self.emit(queue, lambda e, d=dst, sr=src: e.dma_start(d, sr),
                  wait=wait, inc=((s, 16),), is_dma=True)
        return s.tok()

    def sbuf(self, name, shape, dt):
        return self.stack.enter_context(self.nc.sbuf_tensor(name, list(shape), dt))

    def psum(self, name, shape, dt):
        return self.stack.enter_context(self.nc.psum_tensor(name, list(shape), dt))

    def inp(self, name, shape, dt=F32):
        if name not in self.inputs:
            self.inputs[name] = self.nc.dram_tensor(name, list(shape), dt,
                                                    kind="ExternalInput")
        return self.inputs[name]

    def finalize(self):
        import contextlib as _ctx
        for bi, blk in enumerate(self.blocks):
            if not any(blk.ops[e] for e in ENGS):
                continue
            scope = (self.nc.named_scope("%s_%d" % (blk.tag, bi))
                     if blk.tag else _ctx.nullcontext())
            with scope, self.nc.Block() as bb:
                for ename in ENGS:
                    entries = blk.ops[ename]
                    if not entries:
                        continue

                    def replay(eng, entries=entries):
                        for fn, waits, incs in entries:
                            for s, v in waits:
                                eng.wait_ge(s.h, v)
                            if fn is None:
                                continue
                            inst = fn(eng)
                            for s, k in incs:
                                inst.then_inc(s.h, k)

                    getattr(bb, ename)(replay)
        self.stack.close()


def _tiles_mk(w):
    K, M = w.shape
    return np.ascontiguousarray(
        w.reshape(K // 128, 128, M // 128, 128).transpose(2, 1, 0, 3))


def prepare_host(inputs, T):
    h = {}
    f32 = np.float32

    n = np.arange(T, dtype=np.float64)
    g = 1.0 - 2.0 ** (-5.0 - np.arange(NH, dtype=np.float64))
    gq = np.zeros((NPAIR, 128, T), f32)
    gk = np.zeros((NPAIR, 128, T), f32)
    for p in range(NPAIR):
        for j, hh in enumerate((2 * p, 2 * p + 1)):
            gq[p, 64 * j:64 * j + 64, :] = (g[hh] ** n)[None, :].astype(f32)
            gk[p, 64 * j:64 * j + 64, :] = (g[hh] ** (-n))[None, :].astype(f32)
    h["gq"] = gq
    h["gk"] = gk
    masks = np.zeros((4, 128, 512), f32)
    ii = np.arange(128)[:, None]
    jj = np.arange(512)[None, :]
    for oi, off in enumerate((0, 128, 256, 384)):
        masks[oi] = (jj >= ii + off).astype(f32)
    h["masks"] = masks
    h["ident"] = np.eye(128, dtype=f32)
    h["ones1"] = np.ones((1, 128), f32)
    h["onesm"] = np.full((128, 1), 1.0 / D, f32)

    cols = []
    colidx = {}

    def addvec(name, v):
        v = np.asarray(v, f32).reshape(-1, 128)
        colidx[name] = len(cols)
        for t in range(v.shape[0]):
            cols.append(v[t])

    for i in range(2):
        pb = inputs["L0_attn_proj_b"][i]
        addvec(f"bq{i}", pb[0:1024])
        addvec(f"bk{i}", pb[1024:2048])
        addvec(f"bv{i}", pb[2048:3072])
        addvec(f"bxo{i}", pb[3072:4096])
        addvec(f"reto_b{i}", inputs["L0_attn_reto_b"][i])
        addvec(f"ffn1_b{i}", inputs["L0_attn_ffn1_b"][i])
        addvec(f"ffn2_b{i}", inputs["L0_attn_ffn2_b"][i])
        addvec(f"ln1_g{i}", inputs["L0_attn_ln1_g"][i])
        addvec(f"ln1_b{i}", inputs["L0_attn_ln1_b"][i])
        addvec(f"ln2_g{i}", inputs["L0_attn_ln2_g"][i])
        addvec(f"ln2_b{i}", inputs["L0_attn_ln2_b"][i])
    addvec("lnf_g", inputs["L0_attn_lnf_g"])
    addvec("lnf_b", inputs["L0_attn_lnf_b"])
    addvec("b1", inputs["L1_attn_b1"])
    addvec("b2", inputs["L1_attn_b2"])
    addvec("b4", inputs["L1_attn_b4"])
    h["cvec"] = np.ascontiguousarray(np.stack(cols, axis=1))
    h["_colidx"] = colidx
    h["b3row"] = np.asarray(inputs["L1_attn_b3"], f32).reshape(1, D)

    for i in range(2):
        pw = inputs["L0_attn_proj_w"][i]
        h[f"wq{i}"] = _tiles_mk(pw[:, 0:1024])
        h[f"wk{i}"] = _tiles_mk(pw[:, 1024:2048])
        h[f"wv{i}"] = _tiles_mk(pw[:, 2048:3072])
        h[f"wxo{i}"] = _tiles_mk(pw[:, 3072:4096])
        h[f"reto{i}"] = _tiles_mk(inputs["L0_attn_reto_w"][i])
        h[f"ffn1_{i}"] = _tiles_mk(inputs["L0_attn_ffn1_w"][i])
        h[f"ffn2_{i}"] = _tiles_mk(inputs["L0_attn_ffn2_w"][i])

    h["wrouter"] = np.ascontiguousarray(
        inputs["L0_ffn_w_router"].reshape(FT, 128, 8))
    h["fc1"] = np.stack([_tiles_mk(inputs["L0_ffn_fc1_w"][e]).transpose(
        1, 0, 2, 3).reshape(128, 16384) for e in range(8)])
    h["fc2"] = np.stack([_tiles_mk(inputs["L0_ffn_fc2_w"][e]) for e in range(8)])
    h["gatew"] = _tiles_mk(inputs["L0_ffn_gate_w"])
    h["upw"] = _tiles_mk(inputs["L0_ffn_up_w"])
    h["downw"] = _tiles_mk(inputs["L0_ffn_down_w"])

    h["w1"] = _tiles_mk(inputs["L1_attn_W1"])
    h["w2"] = _tiles_mk(inputs["L1_attn_W2"])
    h["w3"] = np.ascontiguousarray(inputs["L1_attn_W3"].reshape(FT, 128, D))
    h["w4"] = _tiles_mk(inputs["L1_attn_W4"])

    h["makerT"] = np.ascontiguousarray(
        inputs["L1_ffn_rail_maker"].transpose(3, 0, 1, 2))
    h["upT"] = np.ascontiguousarray(
        inputs["L1_ffn_rail_up_proj"].transpose(3, 0, 1, 2))
    wct = np.ascontiguousarray(inputs["L1_ffn_w_cproj"].T)
    wct = wct.reshape(16, 8, 128, 8, 128).transpose(0, 3, 2, 1, 4)
    h["wcT"] = np.ascontiguousarray(wct)
    return h


class MK:
    def __init__(self, nc, T, colidx):
        self.b = B(nc, T)
        b = self.b
        self.T = T
        self.NB = T // 512
        self.NTT = T // 128
        self.colidx = colidx
        ncol = 320
        self.SX = b.sbuf("SX", (128, FT * T), F32)
        self.A1 = b.sbuf("A1", (128, max(FT * T, 16384)), BF16)
        self.A2 = b.sbuf("A2", (128, max(FT * T, 16384)), BF16)
        self.A3 = b.sbuf("A3", (128, max(FT * T, 16384)), BF16)
        self.PP = b.sbuf("PP", (128, max(5 * T, 8192)), BF16)
        self.PT = b.sbuf("PT", (128, 1024), BF16)
        self.WR = b.sbuf("WR", (128, 4096), BF16)
        self.CB = b.sbuf("CB", (128, ncol), F32)
        self.MB = b.sbuf("MB", (128, 4 * 512), BF16)
        self.ID = b.sbuf("ID", (128, 128), BF16)
        self.O1 = b.sbuf("O1", (1, 128), BF16)
        self.OM = b.sbuf("OM", (128, 1), BF16)
        self.SS = b.sbuf("SS", (128, 64), F32)
        self.PF = b.sbuf("PF", (128, 2048), F32)
        self.pb = [b.psum(f"pb{i}", (128, 512), F32) for i in range(6)]
        self.pq = [b.psum("pq0", (128, 1024), BF16),
                   b.psum("pq1", (128, 1024), BF16)]
        self.s_pe = b.sem("s_pe")
        self.s_act = b.esem("scalar")
        self.s_dve = b.esem("vector")
        self.s_pool = b.esem("gpsimd")
        self._bank_free = {}
        self._uid = 0
        self._consts = {}

    def uid(self):
        self._uid += 1
        return self._uid

    def cv(self, name, tile):
        c = self.colidx[name] + tile
        return self.CB[:, c:c + 1]

    def bank_guard(self, bank, key=None):
        t = self._bank_free.get(key if key is not None else id(bank))
        if not t:
            return ()
        return t if isinstance(t, tuple) and t and isinstance(t[0], tuple) \
            else (t,)

    def bank_done(self, bank, tok, key=None):
        self._bank_free[key if key is not None else id(bank)] = tok

    def esem(self, eng):
        return {"scalar": self.s_act, "vector": self.s_dve,
                "gpsimd": self.s_pool, "tensor": self.s_pe}[eng]

    def const_f32(self, val):
        val = float(val)
        if val not in self._consts:
            ap = self.b.sbuf("c%d" % self.uid(), (128, 1), F32)
            self.b.emit("gpsimd", lambda e, a=ap, v=val: e.memset(a[:], v),
                        inc=((self.s_pool, 1),))
            self.b.nc.const_aps.register(F32, val, ap[:])
            self._consts[val] = ap
        return val

    def emit_init(self, cvec_ncol):
        b = self.b
        T = self.T
        xin = b.inp("xin", (D, T))
        b.dma("sync",
              self.SX[:, 0:FT * T].rearrange("p (ft t) -> p ft t", ft=FT),
              xin.ap().rearrange("(ft p) t -> p ft t", p=128))
        b.dma("sync", self.CB[:, 0:cvec_ncol],
              b.inp("cvec", (128, cvec_ncol)).ap())
        b.dma("gpsimd",
              self.MB[:, 0:4 * 512].rearrange("p (o n) -> p o n", o=4),
              b.inp("masks", (4, 128, 512)).ap().rearrange("o p n -> p o n"))
        b.dma("gpsimd", self.ID[:], b.inp("ident", (128, 128)).ap())
        b.dma("gpsimd", self.O1[:], b.inp("ones1", (1, 128)).ap())
        b.dma("gpsimd", self.OM[:], b.inp("onesm", (128, 1)).ap())
        self.const_f32(1e-5)
        self.const_f32(1.0 / 32.0)
        self.const_f32(-0.5)
        b.drain_dmas()
        b.block()

    def emit_store(self):
        b = self.b
        T = self.T
        out = self.b.nc.dram_tensor("xout", [D, T], F32, kind="ExternalOutput")
        s = b.sem("dma_out")
        b.dma("sync", out.ap().rearrange("(ft p) t -> p ft t", p=128),
              self.SX[:, 0:FT * T].rearrange("p (ft t) -> p ft t", ft=FT),
              wait=getattr(self, "ln_done", ()), sem=s)
        b.emit("sync", lambda e: e.nop(), wait=(s.tok(),))
        b.block()

    def emit_cast(self, dst=None):
        b = self.b
        T = self.T
        dst = dst if dst is not None else self.A1
        engs = ("scalar", "vector", "gpsimd")
        for ft in range(FT):
            eng = engs[ft % 3]
            sl = slice(ft * T, (ft + 1) * T)
            if eng == "scalar":
                b.emit("scalar", lambda e, s=sl: e.activation(
                    dst[:, s], self.SX[:, s], AF.Copy), inc=((self.s_act, 1),))
            else:
                b.emit(eng, lambda e, s=sl: e.tensor_copy(
                    dst[:, s], self.SX[:, s]), inc=((self.esem(eng), 1),))
        b.block()

    def warm(self, bank, wait=()):
        self.b.emit("tensor", lambda e, bank=bank: e.matmul(
            bank[0:1, 0:64], self.O1[0:1, 0:1], self.O1[0:1, 0:64],
            start=True, stop=True), wait=wait, inc=((self.s_pe, 1),))

    def emit_ln(self, kind, gname=None, bname=None, dst=None, castbuf=None,
                upto=99):
        b = self.b
        T = self.T
        cb = castbuf if castbuf is not None else self.A1
        wb = self.pb[5]
        b.block()
        apply_q = []

        def emit_apply(nb, at, mtile, rtile):
            for ft in range(FT):
                s = slice(ft * T + nb * 512, ft * T + nb * 512 + 512)
                eng = ("vector", "gpsimd")[ft % 2]
                sem = self.esem(eng)
                tgt = self.SX if kind == "plain" else dst
                b.emit(eng, lambda e, s=s, tgt=tgt, mtile=mtile: e.tensor_add(
                    tgt[:, s], self.SX[:, s], mtile), wait=(at,),
                    inc=((sem, 1),))
                b.emit(eng, lambda e, s=s, tgt=tgt, rtile=rtile: e.tensor_mul(
                    tgt[:, s], tgt[:, s], rtile), inc=((sem, 1),))
                if kind == "plain":
                    eng2 = ("gpsimd", "vector")[ft % 2]
                    b.emit(eng2, lambda e, s=s: e.tensor_copy(
                        self.A1[:, s], self.SX[:, s]), wait=(sem.tok(),))
                if kind == "affine":
                    et = sem.tok()
                    b.emit("scalar", lambda e, s=s, ft=ft: e.activation(
                        dst[:, s], dst[:, s], AF.Identity,
                        bias=self.cv(bname, ft), scale=self.cv(gname, ft)),
                        wait=(et,), inc=((self.s_act, 1),))
            par = nb % 2
            ldone = (self.s_dve.tok(), self.s_pool.tok(), self.s_act.tok())
            self.bank_done(None, ldone, ("PTm", par))
            self.bank_done(None, ldone, ("PTr", par))

        for nb in range(self.NB):
            par = nb % 2
            pmean = self.pb[0 + 2 * par]
            psq = self.pb[1 + 2 * par]
            srm = self.MB[0:1, par * 1024:par * 1024 + 512]
            srr = self.MB[0:1, par * 1024 + 512:par * 1024 + 1024]
            scr = self.PF[0:1, par * 1024 + 512:par * 1024 + 1024]
            mtile = self.PF[:, par * 1024:par * 1024 + 512]
            rtile = self.PF[:, par * 1024 + 512:par * 1024 + 1024]
            sq = [self.PT[:, 0:512], self.PT[:, 512:1024]]
            sqtok = [None] * FT
            for ft in range(FT):
                s = slice(ft * T + nb * 512, ft * T + nb * 512 + 512)
                ceng = ("scalar", "vector", "gpsimd")[ft % 3]
                csem = self.esem(ceng)
                if ceng == "scalar":
                    b.emit("scalar", lambda e, s=s: e.activation(
                        cb[:, s], self.SX[:, s], AF.Copy), inc=((csem, 1),))
                else:
                    b.emit(ceng, lambda e, s=s: e.tensor_copy(
                        cb[:, s], self.SX[:, s]), inc=((csem, 1),))
                ct = csem.tok()
                if ft >= 2:
                    w2 = (sqtok[ft - 2],) + (self.bank_guard(psq)
                                             if ft == 2 else ())
                    b.emit("tensor", lambda e, q=sq[ft % 2], ft=ft,
                           psq=psq: e.matmul(
                        psq[0:1, 0:512], self.OM[:], q,
                        start=(ft == 2), stop=False),
                        wait=w2, inc=((self.s_pe, 1),))
                    self.bank_done(None, self.s_pe.tok(), ("sq", ft % 2))
                qs = sq[ft % 2]
                b.emit("scalar", lambda e, s=s, q=qs: e.activation(
                    q, cb[:, s], AF.Square),
                    wait=(ct,) + self.bank_guard(qs, ("sq", ft % 2)))
                sqtok[ft] = self.s_act.tok()
                w1 = (ct,) + (self.bank_guard(pmean) if ft == 0 else ())
                b.emit("tensor", lambda e, s=s, ft=ft, pmean=pmean: e.matmul(
                    pmean[0:1, 0:512], self.OM[:], cb[:, s],
                    start=(ft == 0), stop=(ft == FT - 1)),
                    wait=w1, inc=((self.s_pe, 1),))
            for ft in (FT - 2, FT - 1):
                b.emit("tensor", lambda e, q=sq[ft % 2], ft=ft, psq=psq:
                       e.matmul(
                    psq[0:1, 0:512], self.OM[:], q,
                    start=False, stop=(ft == FT - 1)),
                    wait=(sqtok[ft],), inc=((self.s_pe, 1),))
                self.bank_done(None, self.s_pe.tok(), ("sq", ft % 2))
            pet = self.s_pe.tok()
            b.emit("scalar", lambda e, pmean=pmean, scr=scr: e.activation(
                scr, pmean[0:1, 0:512], AF.Square),
                wait=(pet,) + self.bank_guard(None, ("SC", par))
                + self.bank_guard(None, ("PTr", par)))
            sq_t = self.s_act.tok()
            self.warm(wb, (sq_t,))
            b.emit("vector", lambda e, psq=psq, scr=scr: e.tensor_sub(
                psq[0:1, 0:512], psq[0:1, 0:512], scr), wait=(sq_t,))
            self.bank_done(None, self.s_dve.tok(), ("SC", par))
            b.emit("vector", lambda e, pmean=pmean, srm=srm:
                   e.tensor_scalar_mul(srm, pmean[0:1, 0:512], -1.0),
                   wait=self.bank_guard(None, ("SRm", par)))
            dv_t = self.s_dve.tok()
            b.emit("scalar", lambda e, psq=psq: e.activation(
                psq[0:1, 0:512], psq[0:1, 0:512], AF.Ln, bias=1e-5),
                wait=(dv_t,))
            b.emit("scalar", lambda e, psq=psq, srr=srr: e.activation(
                srr, psq[0:1, 0:512], AF.Exp, scale=-0.5),
                wait=self.bank_guard(None, ("SRr", par)))
            rt = (dv_t, self.s_act.tok())
            self.bank_done(pmean, rt[1])
            self.bank_done(psq, rt[1])
            self.warm(wb, (rt[1],))
            b.emit("tensor", lambda e, pmean=pmean, srm=srm: e.matmul(
                pmean[:, 0:512], self.O1[:], srm, start=True, stop=True),
                wait=rt, inc=((self.s_pe, 1),))
            self.bank_done(None, self.s_pe.tok(), ("SRm", par))
            b.emit("tensor", lambda e, psq=psq, srr=srr: e.matmul(
                psq[:, 0:512], self.O1[:], srr, start=True, stop=True),
                wait=rt, inc=((self.s_pe, 1),))
            self.bank_done(None, self.s_pe.tok(), ("SRr", par))
            bt = self.s_pe.tok()
            b.emit("scalar", lambda e, pmean=pmean, mtile=mtile: e.activation(
                mtile, pmean[:, 0:512], AF.Copy),
                wait=(bt,) + self.bank_guard(None, ("PTm", par)))
            b.emit("scalar", lambda e, psq=psq, rtile=rtile: e.activation(
                rtile, psq[:, 0:512], AF.Copy),
                wait=self.bank_guard(None, ("PTr", par)))
            at = self.s_act.tok()
            self.bank_done(pmean, at)
            self.bank_done(psq, at)
            self.warm(wb, (at,))
            apply_q.append((nb, at, mtile, rtile))
            if len(apply_q) > 1:
                emit_apply(*apply_q.pop(0))
        while apply_q:
            emit_apply(*apply_q.pop(0))
        self.ln_done = (self.s_dve.tok(), self.s_pool.tok(),
                        self.s_act.tok())

    def proj(self, wname, Mt, Kt, src_bf, evac, wdram=None, banks=None,
             nb_range=None, src_off=0, src_T=None, mt_range=None,
             slot_range=None, wsem="s_wld", first_wait=()):
        b = self.b
        T = self.T
        sT = src_T if src_T is not None else T
        nbs = list(range(self.NB)) if nb_range is None else list(nb_range)
        mts = list(range(Mt)) if mt_range is None else list(mt_range)
        wdram = wdram if wdram is not None else b.inp(wname, (Mt, 128, Kt, 128))
        wsz = Kt * 128
        srange = list(slot_range) if slot_range is not None else \
            list(range(max(1, min(4, 4096 // wsz))))
        nslot = len(srange)
        assert nslot >= 2 or len(mts) == 1, "weight strip too large for ring"
        slots = [self.WR[:, i * wsz:(i + 1) * wsz] for i in srange]
        banks = banks or self.pb[0:4]
        s_w = b.sem(wsem)
        dtok = [None] * (len(mts) if mt_range is not None else Mt)
        pair_ok = (mt_range is None and nslot % 2 == 0)

        def issue_dma(i):
            if pair_ok:
                if i % 2 == 1:
                    return
                w = (self.bank_guard(None, ("WR", srange[i % nslot]))
                     + self.bank_guard(None, ("WR", srange[(i + 1) % nslot])))
                if i + 1 < len(mts) and mts[i + 1] == mts[i] + 1 \
                        and (i % nslot) + 1 == (i + 1) % nslot:
                    dst = self.WR[:, srange[i % nslot] * wsz:
                                  (srange[i % nslot] + 2) * wsz].rearrange(
                        "p (t kt m) -> p t kt m", t=2, kt=Kt)
                    dtok[i] = b.dma(
                        "gpsimd", dst,
                        wdram.ap()[mts[i]:mts[i] + 2].rearrange(
                            "t p kt m -> p t kt m"),
                        wait=w, sem=s_w)
                    dtok[i + 1] = dtok[i]
                    return
            w = self.bank_guard(None, ("WR", srange[i % nslot]))
            dtok[i] = b.dma(
                "gpsimd",
                slots[i % nslot].rearrange("p (kt m) -> p kt m", kt=Kt),
                wdram.ap()[mts[i]], wait=w, sem=s_w)

        nxt = min(nslot, len(mts))
        for i in range(nxt):
            issue_dma(i)
        for i, mt in enumerate(mts):
            sl = slots[i % nslot]
            for kt in range(Kt):
                for j, nb in enumerate(nbs):
                    bank = banks[j % len(banks)]
                    wt = ()
                    if kt == 0 and j == 0:
                        wt += (s_w.tok(),)
                        if i == 0:
                            wt += tuple(first_wait)
                    if kt == 0:
                        wt += self.bank_guard(bank)
                    rhs = src_bf[:, src_off + kt * sT + nb * 512:
                                 src_off + kt * sT + nb * 512 + 512]
                    b.emit("tensor", lambda e, sl=sl, kt=kt, rhs=rhs, bank=bank,
                           st=(kt == 0), sp=(kt == Kt - 1): e.matmul(
                        bank[:, 0:512], sl[:, kt * 128:kt * 128 + 128], rhs,
                        start=st, stop=sp),
                        wait=wt, inc=((self.s_pe, 1),))
            self.bank_done(None, self.s_pe.tok(), ("WR", srange[i % nslot]))
            lim = i + nslot - (1 if pair_ok else 0)
            while nxt < len(mts) and nxt <= lim:
                issue_dma(nxt)
                nxt += 1
            
            for j, nb in enumerate(nbs):
                evac(mt, nb, banks[j % len(banks)], self.s_pe.tok())

    def emit_l1_attn(self):
        b = self.b
        b.phase_tag = "l1attn"
        T = self.T
        NTT = self.NTT
        NB = self.NB
        SXb = self.SX[:, 0:FT * T].bitcast(BF16)
        P_P = 0
        P_PT = 4 * T
        P_AO = 8 * T
        if not hasattr(self, "xpark"):
            self.xpark = self.b.nc.dram_tensor("xpark", [128, FT * T], F32,
                                               kind="Internal")
        xpark = self.xpark
        pk = b.dma("sync", xpark.ap(), self.SX[:, 0:FT * T],
                   wait=getattr(self, "ln_done", ()))
        b.wait("sync", (pk,))
        dt0 = b.dma("gpsimd", self.MB[0:1, 0:D], b.inp("b3row", (1, D)).ap())
        for half in range(2):
            bk = self.pb[4]
            b.emit("tensor", lambda e, h=half, bk=bk: e.matmul(
                bk[:, 0:512], self.O1[:],
                self.MB[0:1, h * 512:h * 512 + 512],
                start=True, stop=True),
                wait=(dt0,) + self.bank_guard(bk), inc=((self.s_pe, 1),))
            b.emit("scalar", lambda e, h=half, bk=bk: e.activation(
                self.PF[:, h * 512:h * 512 + 512], bk[:, 0:512], AF.Copy),
                wait=(self.s_pe.tok(),), inc=((self.s_act, 1),))
            self.bank_done(bk, self.s_act.tok())
        b.block()
        w3 = b.inp("w3", (FT, 128, D))
        s_w = b.sem("s_wld")
        for vh in range(2):
            wv = None
            for kt in range(FT):
                sl = self.WR[:, kt * 512:(kt + 1) * 512]
                wv = b.dma("gpsimd", sl,
                           w3.ap()[kt, :, vh * 512:vh * 512 + 512], sem=s_w)
            for tt in range(NTT):
                bk = self.pb[tt % 2]
                for kt in range(FT):
                    wt = ()
                    if kt == 0:
                        wt = (wv,) + self.bank_guard(bk)
                        if vh == 0 and tt == 0:
                            wt += self.ln_done
                    b.emit("tensor", lambda e, kt=kt, tt=tt, bk=bk,
                           st=(kt == 0), sp=(kt == FT - 1): e.matmul(
                        bk[:, 0:512],
                        self.A1[:, kt * T + tt * 128:kt * T + tt * 128 + 128],
                        self.WR[:, kt * 512:kt * 512 + 512],
                        start=st, stop=sp), wait=wt, inc=((self.s_pe, 1),))
                pt = self.s_pe.tok()
                b.emit("vector", lambda e, tt=tt, vh=vh, bk=bk: e.tensor_add(
                    self.A2[:, tt * D + vh * 512:tt * D + vh * 512 + 512],
                    bk[:, 0:512], self.PF[:, vh * 512:vh * 512 + 512]),
                    wait=(pt,), inc=((self.s_dve, 1),))
                self.bank_done(bk, self.s_dve.tok())
            b.block()

        def evac_k(mt, nb, bank, tok):
            b.emit("scalar", lambda e, mt=mt, nb=nb, bank=bank: e.activation(
                self.A3[:, mt * T + nb * 512:mt * T + nb * 512 + 512],
                bank[:, 0:512], AF.Identity, bias=self.cv("b2", mt)),
                wait=(tok,), inc=((self.s_act, 1),))
            self.bank_done(bank, self.s_act.tok())

        self.proj("w2", FT, FT, self.A1, evac_k)
        b.block()

        w1 = b.inp("w1", (FT, 128, FT, 128))
        w4 = b.inp("w4", (FT, 128, FT, 128))
        for ns in range(NB):
            def evac_q(mt, nb, bank, tok):
                qg = self.bank_guard(None, "PPq") if mt == 0 else ()
                b.emit("scalar", lambda e, mt=mt, bank=bank: e.activation(
                    self.PP[:, mt * 512:mt * 512 + 512],
                    bank[:, 0:512], AF.Identity, bias=self.cv("b1", mt)),
                    wait=(tok,) + qg, inc=((self.s_act, 1),))
                self.bank_done(bank, self.s_act.tok())

            self.proj("w1", FT, FT, self.A1, evac_q, wdram=w1,
                      banks=self.pb[4:6], nb_range=(ns,), slot_range=(0, 1),
                      wsem="s_wldq")
            q_done = self.s_act.tok()
            for ntl in range(4):
                sbanks = [self.pb[i] for i in range(4)]
                for mb in range(NB):
                    bk = sbanks[mb % 4]
                    for kt in range(FT):
                        wt = ((q_done,) + self.bank_guard(bk)) \
                            if kt == 0 else ()
                        b.emit("tensor", lambda e, kt=kt, mb=mb, bk=bk,
                               ntl=ntl, st=(kt == 0), sp=(kt == FT - 1):
                               e.matmul(
                            bk[:, 0:512],
                            self.PP[:, kt * 512 + ntl * 128:
                                    kt * 512 + ntl * 128 + 128],
                            self.A3[:, kt * T + mb * 512:
                                    kt * T + mb * 512 + 512],
                            start=st, stop=sp),
                            wait=wt, inc=((self.s_pe, 1),))
                spt = self.s_pe.tok()
                for mb in range(NB):
                    b.emit("vector", lambda e, mb=mb: e.reduce_max(
                        self.SS[:, 8 + mb:9 + mb], sbanks[mb % 4][:, 0:512],
                        AX.X), wait=(spt,), inc=((self.s_dve, 1),))
                b.emit("vector", lambda e, NB=NB: e.tensor_reduce(
                    self.SS[:, 0:1], self.SS[:, 8:8 + NB], AX.X, ALU.max),
                    inc=((self.s_dve, 1),))
                b.emit("vector", lambda e: e.tensor_scalar_mul(
                    self.SS[:, 0:1], self.SS[:, 0:1], -1.0 / 32.0),
                    inc=((self.s_dve, 1),))
                mt_ = self.s_dve.tok()
                for mb in range(NB):
                    pw = self.bank_guard(None, "pSXb") \
                        if (ntl == 0 and mb == 0) else ()
                    b.emit("scalar", lambda e, mb=mb, ntl=ntl: e.activation(
                        SXb[:, P_P + ntl * T + mb * 512:
                            P_P + ntl * T + mb * 512 + 512],
                        sbanks[mb % 4][:, 0:512], AF.Exp,
                        bias=self.SS[:, 0:1], scale=1.0 / 32.0,
                        accum_out=self.SS[:, 16 + mb:17 + mb]),
                        wait=(mt_,) + pw, inc=((self.s_act, 1),))
                    self.bank_done(sbanks[mb % 4], self.s_act.tok())
                et = self.s_act.tok()
                b.emit("vector", lambda e, NB=NB: e.tensor_reduce(
                    self.SS[:, 1:2], self.SS[:, 16:16 + NB], AX.X, ALU.add),
                    wait=(et,), inc=((self.s_dve, 1),))
                b.emit("vector", lambda e: e.reciprocal(
                    self.SS[:, 2:3], self.SS[:, 1:2]), inc=((self.s_dve, 1),))
                rt = self.s_dve.tok()
                for mb in range(NB):
                    b.emit("vector", lambda e, mb=mb, ntl=ntl:
                           e.tensor_scalar_mul(
                        SXb[:, P_P + ntl * T + mb * 512:
                            P_P + ntl * T + mb * 512 + 512],
                        SXb[:, P_P + ntl * T + mb * 512:
                            P_P + ntl * T + mb * 512 + 512],
                        self.SS[:, 2:3]), wait=(rt,), inc=((self.s_dve, 1),))
            self.bank_done(None, self.s_pe.tok(), "PPq")
            psc = self.s_dve.tok()
            for mc in range(NTT):
                pqv = self.pq[mc % 2][:, 0:512]
                for ntl in range(4):
                    w0 = ((psc,) if (mc == 0 and ntl == 0) else ())
                    b.emit("tensor", lambda e, mc=mc, ntl=ntl, pqv=pqv:
                           e.transpose(
                        pqv[:, ntl * 128:ntl * 128 + 128],
                        SXb[:, P_P + ntl * T + mc * 128:
                            P_P + ntl * T + mc * 128 + 128],
                        self.ID[:]),
                        wait=w0 + (self.bank_guard(pqv, ("pq", mc % 2))
                                   if ntl == 0 else ()),
                        inc=((self.s_pe, 1),))
                tt_ = self.s_pe.tok()
                b.emit("vector", lambda e, mc=mc, pqv=pqv: e.tensor_copy(
                    SXb[:, P_PT + mc * 512:P_PT + mc * 512 + 512], pqv),
                    wait=(tt_,) + (self.bank_guard(None, "pTSXb")
                                   if mc == 0 else ()),
                    inc=((self.s_dve, 1),))
                self.bank_done(pqv, self.s_dve.tok(), ("pq", mc % 2))
            ptev = self.s_dve.tok()
            self.bank_done(None, self.s_pe.tok(), "pSXb")
            for dt_ in range(FT):
                bk = self.pb[4 + (dt_ % 2)]
                for mc in range(NTT):
                    wt = ((ptev,) + self.bank_guard(bk)) \
                        if mc == 0 else ()
                    b.emit("tensor", lambda e, dt_=dt_, mc=mc, bk=bk,
                           st=(mc == 0), sp=(mc == NTT - 1): e.matmul(
                        bk[:, 0:512],
                        self.A2[:, mc * D + dt_ * 128:mc * D + dt_ * 128 + 128],
                        SXb[:, P_PT + mc * 512:P_PT + mc * 512 + 512],
                        start=st, stop=sp), wait=wt, inc=((self.s_pe, 1),))
                pt = self.s_pe.tok()
                b.emit("scalar", lambda e, dt_=dt_, bk=bk: e.activation(
                    self.PP[:, 8 * 512 + dt_ * 512:8 * 512 + dt_ * 512 + 512],
                    bk[:, 0:512], AF.Copy),
                    wait=(pt,) + (self.bank_guard(None, "PPr")
                                  if dt_ == 0 else ()),
                    inc=((self.s_act, 1),))
                self.bank_done(bk, self.s_act.tok())
            retev = self.s_act.tok()
            self.bank_done(None, self.s_pe.tok(), "pTSXb")

            def evac_o(mt, nb, bank, tok, ns=ns):
                b.emit("scalar", lambda e, mt=mt, bank=bank: e.activation(
                    SXb[:, P_AO + mt * T + ns * 512:
                        P_AO + mt * T + ns * 512 + 512],
                    bank[:, 0:512], AF.Identity, bias=self.cv("b4", mt)),
                    wait=(tok,), inc=((self.s_act, 1),))
                self.bank_done(bank, self.s_act.tok())

            self.proj("w4", FT, FT, self.PP, evac_o, wdram=w4,
                      banks=self.pb[4:6], nb_range=(0,), src_off=8 * 512,
                      src_T=512, slot_range=(2, 3), first_wait=(retev,))
            self.bank_done(None, self.s_pe.tok(), "PPr")
        A2f = self.A2[:, 0:FT * T].bitcast(F32)
        A3f = self.A3[:, 0:FT * T].bitcast(F32)
        b.wait("sync", (self.s_pe.tok(),))
        b.dma("sync", A2f, self.xpark.ap()[:, 0:4 * T])
        b.dma("sync", A3f, self.xpark.ap()[:, 4 * T:8 * T])
        rl = b.sem("dma_sync").tok()
        ao_done = self.s_act.tok()
        for ft in range(FT):
            eng = ("vector", "gpsimd")[ft % 2]
            wt = (rl, ao_done, self.s_dve.tok(), self.s_pool.tok())
            b.emit(eng, lambda e, ft=ft: e.tensor_copy(
                self.SX[:, ft * T:ft * T + T],
                SXb[:, P_AO + ft * T:P_AO + ft * T + T]),
                wait=wt, inc=((self.esem(eng), 1),))
        cpt = (self.s_dve.tok(), self.s_pool.tok())
        for ft in range(FT):
            eng = ("vector", "gpsimd")[ft % 2]
            b.emit(eng, lambda e, ft=ft: e.tensor_add(
                self.SX[:, ft * T:ft * T + T],
                self.SX[:, ft * T:ft * T + T],
                (A2f if ft < 4 else A3f)[:, (ft % 4) * T:(ft % 4) * T + T]),
                wait=cpt, inc=((self.esem(eng), 1),))
        b.block()
        self.emit_ln("plain")

    def emit_l0_attn(self, upto=99):
        b = self.b
        b.phase_tag = "l0attn"
        T = self.T
        NB = self.NB
        NTT = self.NTT
        if not hasattr(self, "xpark"):
            self.xpark = self.b.nc.dram_tensor("xpark", [128, FT * T], F32,
                                               kind="Internal")
        pk = b.dma("sync", self.xpark.ap(), self.SX[:, 0:FT * T],
                   wait=getattr(self, "ln_done", ()))
        b.wait("sync", (pk,))
        b.block()
        P0 = self.PP[:, 0 * T:1 * T]
        P1 = self.PP[:, 1 * T:2 * T]
        P2 = self.PP[:, 2 * T:3 * T]
        P3 = self.PP[:, 3 * T:4 * T]
        P4 = self.PP[:, 4 * T:5 * T]
        gq = b.inp("gq", (NPAIR, 128, T))
        gk = b.inp("gk", (NPAIR, 128, T))
        for li in range(2):
            self.emit_ln("affine", "ln1_g%d" % li, "ln1_b%d" % li, dst=self.A1)
            if upto <= 1:
                return
            wq = b.inp("wq%d" % li, (FT, 128, FT, 128))
            wk = b.inp("wk%d" % li, (FT, 128, FT, 128))
            wv = b.inp("wv%d" % li, (FT, 128, FT, 128))
            wxo = b.inp("wxo%d" % li, (FT, 128, FT, 128))
            s_msk = b.sem("dma_mask")
            b.dma("sync",
                  self.PF[:, 0:2048].rearrange("p (o n) -> p o n", o=4),
                  b.inp("masks", (4, 128, 512)).ap().rearrange(
                      "o p n -> p o n"),
                  wait=(s_msk.tok(),) + self.ln_done
                  + self.bank_guard(None, "PFmask"),
                  sem=s_msk)
            mskf = s_msk.tok()
            self.bank_done(None, mskf, "PFmask")
            gq_tok = [None] * NPAIR
            gk_tok = [None] * NPAIR

            def issue_gq(p):
                s = b.sem("dma_gq")
                b.dma("gpsimd", P0, gq.ap()[p],
                      wait=self.bank_guard(P0, "P0"), sem=s)
                gq_tok[p] = s.tok()

            def issue_gk(p):
                s = b.sem("dma_gk")
                b.dma("gpsimd", P1, gk.ap()[p],
                      wait=self.bank_guard(P1, "P1"), sem=s)
                gk_tok[p] = s.tok()

            issue_gq(0)
            issue_gk(0)
            for p in range(NPAIR):

                def evac_q(mt, nb, bank, tok, li=li, p=p):
                    tm = self.PT[:, (nb % 2) * 512:(nb % 2) * 512 + 512]
                    b.emit("scalar", lambda e, bank=bank, tm=tm: e.activation(
                        tm, bank[:, 0:512], AF.Identity,
                        bias=self.cv("bq%d" % li, p)),
                        wait=(tok,) + self.bank_guard(None, ("sT", nb % 2)))
                    att = self.s_act.tok()
                    self.bank_done(bank, att)
                    b.emit("vector", lambda e, nb=nb, tm=tm: e.tensor_mul(
                        P2[:, nb * 512:nb * 512 + 512], tm,
                        P0[:, nb * 512:nb * 512 + 512]),
                        wait=(att, gq_tok[p]) + self.bank_guard(P2, "P2"))
                    self.bank_done(None, self.s_dve.tok(), ("sT", nb % 2))

                self.proj("", FT, FT, self.A1, evac_q, wdram=wq,
                          mt_range=(p,), slot_range=(0,), wsem="s_wld0",
                          first_wait=(self.ln_done if p == 0 else ()))
                self.bank_done(None, self.s_dve.tok(), "P2")

                def evac_k(mt, nb, bank, tok, li=li, p=p):
                    tm = self.PT[:, (nb % 2) * 512:(nb % 2) * 512 + 512]
                    b.emit("scalar", lambda e, bank=bank, tm=tm: e.activation(
                        tm, bank[:, 0:512], AF.Identity,
                        bias=self.cv("bk%d" % li, p)),
                        wait=(tok,) + self.bank_guard(None, ("sT", nb % 2)))
                    att = self.s_act.tok()
                    self.bank_done(bank, att)
                    b.emit("vector", lambda e, nb=nb, tm=tm: e.tensor_mul(
                        P3[:, nb * 512:nb * 512 + 512], tm,
                        P1[:, nb * 512:nb * 512 + 512]),
                        wait=(att, gk_tok[p]) + self.bank_guard(P3, "P3"))
                    self.bank_done(None, self.s_dve.tok(), ("sT", nb % 2))

                self.proj("", FT, FT, self.A1, evac_k, wdram=wk,
                          mt_range=(p,), slot_range=(1,), wsem="s_wld1")
                self.bank_done(None, self.s_dve.tok(), "P3")
                qk_done = self.s_dve.tok()

                def evac_v(mt, nb, bank, tok, li=li, p=p, qk=qk_done):
                    b.emit("scalar", lambda e, nb=nb, bank=bank: e.activation(
                        P0[:, nb * 512:nb * 512 + 512], bank[:, 0:512],
                        AF.Identity, bias=self.cv("bv%d" % li, p)),
                        wait=(tok, qk))
                    self.bank_done(bank, self.s_act.tok())

                self.proj("", FT, FT, self.A1, evac_v, wdram=wv,
                          mt_range=(p,), slot_range=(2,), wsem="s_wld2")
                vfm_done = self.s_act.tok()
                self.bank_done(None, vfm_done, "P0")

                def evac_x(mt, nb, bank, tok, li=li, p=p):
                    b.emit("scalar", lambda e, nb=nb, bank=bank: e.activation(
                        P4[:, nb * 512:nb * 512 + 512], bank[:, 0:512],
                        AF.Silu, bias=self.cv("bxo%d" % li, p)),
                        wait=(tok,) + self.bank_guard(P4, "P4"))
                    self.bank_done(bank, self.s_act.tok())

                self.proj("", FT, FT, self.A1, evac_x, wdram=wxo,
                          mt_range=(p,), slot_range=(3,), wsem="s_wld3")
                self.bank_done(None, self.s_act.tok(), "P4")
                if upto <= 2:
                    continue

                for tt in range(NTT):
                    pqv = self.pq[tt % 2][:, 0:128]
                    b.emit("tensor", lambda e, tt=tt, pqv=pqv: e.transpose(
                        pqv, P0[:, tt * 128:tt * 128 + 128], self.ID[:]),
                        wait=(vfm_done, qk_done)
                        + self.bank_guard(pqv, ("pq", tt % 2)),
                        inc=((self.s_pe, 1),))
                    ttk = self.s_pe.tok()
                    b.emit("vector", lambda e, tt=tt, pqv=pqv: e.tensor_copy(
                        P1[:, tt * 128:tt * 128 + 128], pqv), wait=(ttk,))
                    self.bank_done(pqv, self.s_dve.tok(), ("pq", tt % 2))
                vtm_done = self.s_dve.tok()
                self.bank_done(None, self.s_pe.tok(), "P0")
                if p + 1 < NPAIR:
                    issue_gq(p + 1)
                if upto <= 3:
                    continue
                qk_attn = (qk_done, self.s_act.tok())

                sT = [self.MB[:, i * 512:(i + 1) * 512] for i in range(4)]
                LK = 4
                for nb in range(NB):
                    rbank = self.pb[4 + nb % 2]
                    items = [(h, mc) for h in range(2)
                             for mc in range(4 * nb + 4)]
                    n_it = len(items)
                    evtok = [None] * n_it

                    def emit_mm2(i, rbank=rbank, items=items, nb=nb):
                        h, mc = items[i]
                        fs = (mc == 0)
                        b.emit("tensor", lambda e, h=h, mc=mc, fs=fs, i=i:
                               e.matmul(
                            rbank[64 * h:64 * h + 64, 0:512],
                            P1[:, mc * 128 + 64 * h:mc * 128 + 64 * h + 64],
                            sT[i % LK], start=fs, stop=(mc == 4 * nb + 3)),
                            wait=(evtok[i],)
                            + (self.bank_guard(rbank)
                               if (fs and h == 0) else ()),
                            inc=((self.s_pe, 1),))
                        self.bank_done(None, self.s_pe.tok(), ("sTm", i % LK))

                    for i, (h, mc) in enumerate(items):
                        sb = self.pb[i % LK]
                        w0 = (qk_attn + (vtm_done,)) \
                            if (nb == 0 and i == 0) else ()
                        b.emit("tensor", lambda e, h=h, mc=mc, nb=nb, sb=sb:
                               e.matmul(
                            sb[:, 0:512],
                            P3[64 * h:64 * h + 64, mc * 128:mc * 128 + 128],
                            P2[64 * h:64 * h + 64, nb * 512:nb * 512 + 512],
                            start=True, stop=True),
                            wait=w0 + self.bank_guard(sb),
                            inc=((self.s_pe, 1),))
                        mmt = self.s_pe.tok()
                        if i >= LK:
                            emit_mm2(i - LK)
                        st = sT[i % LK]
                        stg = self.bank_guard(st, ("sTm", i % LK))
                        off = mc - 4 * nb
                        if off < 0 and i % 2 == 0:
                            b.emit("scalar", lambda e, sb=sb, st=st:
                                   e.activation(st, sb[:, 0:512], AF.Copy),
                                   wait=(mmt,) + stg)
                            self.bank_done(sb, self.s_act.tok())
                            evtok[i] = self.s_act.tok()
                        elif off < 0:
                            b.emit("vector", lambda e, sb=sb, st=st:
                                   e.tensor_copy(st, sb[:, 0:512]),
                                   wait=(mmt,) + stg)
                            self.bank_done(sb, self.s_dve.tok())
                            evtok[i] = self.s_dve.tok()
                        else:
                            b.emit("vector", lambda e, sb=sb, st=st, off=off:
                                   e.tensor_mul(
                                st, sb[:, 0:512],
                                self.PF[:, off * 512:off * 512 + 512]),
                                wait=(mmt, mskf) + stg)
                            self.bank_done(sb, self.s_dve.tok())
                            evtok[i] = self.s_dve.tok()
                    for i in range(max(0, n_it - LK), n_it):
                        emit_mm2(i)
                    rt = self.s_pe.tok()
                    tm = self.PT[:, (nb % 2) * 512:(nb % 2) * 512 + 512]
                    b.emit("scalar", lambda e, tm=tm, rbank=rbank: e.activation(
                        tm, rbank[:, 0:512], AF.Copy),
                        wait=(rt,) + self.bank_guard(None, ("sT", nb % 2)))
                    att2 = self.s_act.tok()
                    self.bank_done(rbank, att2)
                    b.emit("vector", lambda e, nb=nb, p=p, tm=tm: e.tensor_mul(
                        self.A2[:, p * T + nb * 512:p * T + nb * 512 + 512],
                        tm, P4[:, nb * 512:nb * 512 + 512]),
                        wait=(att2,))
                    self.bank_done(None, self.s_dve.tok(), ("sT", nb % 2))
                self.bank_done(None, self.s_pe.tok(), "P2")
                self.bank_done(None, self.s_pe.tok(), "P3")
                self.bank_done(None, self.s_pe.tok(), "P1")
                self.bank_done(None, self.s_dve.tok(), "P4")
                if p + 1 < NPAIR:
                    issue_gk(p + 1)
            if upto <= 4:
                return

            retg_done = self.s_dve.tok()

            def evac_r(mt, nb, bank, tok, li=li):
                b.emit("vector", lambda e, mt=mt, nb=nb, bank=bank:
                       e.scalar_tensor_tensor(
                    self.SX[:, mt * T + nb * 512:mt * T + nb * 512 + 512],
                    bank[:, 0:512], self.cv("reto_b%d" % li, mt),
                    self.SX[:, mt * T + nb * 512:mt * T + nb * 512 + 512],
                    ALU.add, ALU.add), wait=(tok,))
                self.bank_done(bank, self.s_dve.tok())

            self.proj("reto%d" % li, FT, FT, self.A2, evac_r,
                      first_wait=(retg_done,))
            if upto <= 5:
                return

            self.emit_ln("affine", "ln2_g%d" % li, "ln2_b%d" % li, dst=self.A1)
            f1 = b.inp("ffn1_%d" % li, (32, 128, FT, 128))
            f2 = b.inp("ffn2_%d" % li, (FT, 128, 32, 128))
            for cg in range(4):
                def evac_g(mt, nb, bank, tok, cg=cg, li=li):
                    c = mt - cg * 8
                    a2w = self.bank_guard(None, "A2ffn") \
                        if (c == 0 and nb == 0) else ()
                    b.emit("scalar", lambda e, c=c, nb=nb, bank=bank, mt=mt:
                           e.activation(
                        self.A2[:, c * T + nb * 512:c * T + nb * 512 + 512],
                        bank[:, 0:512], AF.Gelu,
                        bias=self.cv("ffn1_b%d" % li, mt)),
                        wait=(tok,) + a2w)
                    self.bank_done(bank, self.s_act.tok())

                self.proj("", 32, FT, self.A1, evac_g, wdram=f1,
                          mt_range=range(cg * 8, cg * 8 + 8),
                          first_wait=(self.ln_done if cg == 0 else ()))
                act_done = self.s_act.tok()
                wsz = 8 * 128
                slots = [self.WR[:, i * wsz:(i + 1) * wsz] for i in range(4)]
                s_w = b.sem("s_wld")

                def issue_f2(mt, cg=cg, slots=slots, s_w=s_w, f2=f2):
                    b.dma("gpsimd",
                          slots[mt % 4].rearrange("p (kt m) -> p kt m", kt=8),
                          f2.ap()[mt, :, cg * 8:cg * 8 + 8, :],
                          wait=self.bank_guard(None, ("WR", mt % 4)), sem=s_w)

                for mt in range(4):
                    issue_f2(mt)
                for mt in range(FT):
                    sl = slots[mt % 4]
                    wtk = s_w.tok()
                    for nb in range(NB):
                        bank = self.pb[4 + nb % 2]
                        for c in range(8):
                            wt = ()
                            if c == 0:
                                wt = (wtk, act_done) + self.bank_guard(bank)
                            b.emit("tensor", lambda e, c=c, nb=nb, sl=sl,
                                   bank=bank, st=(c == 0), sp=(c == 7):
                                   e.matmul(
                                bank[:, 0:512], sl[:, c * 128:c * 128 + 128],
                                self.A2[:, c * T + nb * 512:
                                        c * T + nb * 512 + 512],
                                start=st, stop=sp),
                                wait=wt, inc=((self.s_pe, 1),))
                        ptk = self.s_pe.tok()
                        if cg == 0:
                            b.emit("vector", lambda e, mt=mt, nb=nb, bank=bank,
                                   li=li: e.scalar_tensor_tensor(
                                self.SX[:, mt * T + nb * 512:
                                        mt * T + nb * 512 + 512],
                                bank[:, 0:512],
                                self.cv("ffn2_b%d" % li, mt),
                                self.SX[:, mt * T + nb * 512:
                                        mt * T + nb * 512 + 512],
                                ALU.add, ALU.add), wait=(ptk,))
                        else:
                            b.emit("vector", lambda e, mt=mt, nb=nb, bank=bank:
                                   e.tensor_add(
                                self.SX[:, mt * T + nb * 512:
                                        mt * T + nb * 512 + 512],
                                self.SX[:, mt * T + nb * 512:
                                        mt * T + nb * 512 + 512],
                                bank[:, 0:512]), wait=(ptk,))
                        self.bank_done(bank, self.s_dve.tok())
                    self.bank_done(None, self.s_pe.tok(), ("WR", mt % 4))
                    if mt + 4 < FT:
                        issue_f2(mt + 4)
                self.bank_done(None, self.s_pe.tok(), "A2ffn")
        self.emit_ln("affine", "lnf_g", "lnf_b", dst=self.A1)
        A2f = self.A2[:, 0:FT * T].bitcast(F32)
        A3f = self.A3[:, 0:FT * T].bitcast(F32)
        b.dma("sync", A2f, self.xpark.ap()[:, 0:4 * T], wait=self.ln_done)
        b.dma("sync", A3f, self.xpark.ap()[:, 4 * T:8 * T])
        rl = b.sem("dma_sync").tok()
        for ft in range(FT):
            eng = ("vector", "gpsimd")[ft % 2]
            b.emit(eng, lambda e, ft=ft: e.tensor_add(
                self.SX[:, ft * T:ft * T + T],
                (A2f if ft < 4 else A3f)[:, (ft % 4) * T:(ft % 4) * T + T],
                self.A1[:, ft * T:ft * T + T]),
                wait=(rl,) + self.ln_done)
        b.block()
        self.emit_ln("plain")


    def emit_l0_ffn(self):
        b = self.b
        b.phase_tag = "l0ffn"
        T = self.T
        NB = self.NB
        NTT = self.NTT
        wrt = self.sbufs_wrt if hasattr(self, "sbufs_wrt") else None
        if wrt is None:
            self.sbufs_wrt = b.sbuf("WRT", (128, 64), BF16)
            wrt = self.sbufs_wrt
        b.dma("gpsimd", wrt[:].rearrange("p (kt e) -> p kt e", kt=FT),
              b.inp("wrouter", (FT, 128, 8)).ap().rearrange("kt p e -> p kt e"))
        wrow = self.b.nc.dram_tensor("wrow", [8, T], BF16, kind="Internal")
        rt = b.sem("dma_gpsimd").tok()
        b.wait("gpsimd", (rt,))
        b.block()
        SS = self.SS
        for tt in range(NTT):
            bk = self.pb[4 + tt % 2]
            for kt in range(FT):
                w0 = self.ln_done if (tt == 0 and kt == 0) else ()
                b.emit("tensor", lambda e, tt=tt, kt=kt, bk=bk,
                       st=(kt == 0), sp=(kt == FT - 1): e.matmul(
                    bk[:, 0:8],
                    self.A1[:, kt * T + tt * 128:kt * T + tt * 128 + 128],
                    wrt[:, kt * 8:kt * 8 + 8], start=st, stop=sp),
                    wait=w0 + (self.bank_guard(bk) if kt == 0 else ()),
                    inc=((self.s_pe, 1),))
            pt = self.s_pe.tok()
            b.emit("vector", lambda e, bk=bk: e.tensor_copy(
                SS[:, 0:8], bk[:, 0:8]), wait=(pt,))
            self.bank_done(bk, self.s_dve.tok())
            b.emit("vector", lambda e: e.reduce_max(
                SS[:, 8:9], SS[:, 0:8], AX.X))
            b.emit("vector", lambda e: e.tensor_scalar_mul(
                SS[:, 9:10], SS[:, 8:9], -1.0))
            b.emit("vector", lambda e: e.tensor_scalar(
                SS[:, 16:24], SS[:, 0:8], SS[:, 8:9], None, ALU.is_equal))
            b.emit("vector", lambda e: e.scalar_tensor_tensor(
                SS[:, 24:32], SS[:, 16:24], -1e30, SS[:, 0:8],
                ALU.mult, ALU.add))
            b.emit("vector", lambda e: e.reduce_max(
                SS[:, 10:11], SS[:, 24:32], AX.X))
            dv = self.s_dve.tok()
            b.emit("scalar", lambda e: e.activation(
                SS[:, 11:12], SS[:, 10:11], AF.Exp, bias=SS[:, 9:10]),
                wait=(dv,))
            at = self.s_act.tok()
            b.emit("vector", lambda e: e.tensor_scalar_add(
                SS[:, 12:13], SS[:, 11:12], 1.0), wait=(at,))
            b.emit("vector", lambda e: e.reciprocal(
                SS[:, 13:14], SS[:, 12:13]))
            b.emit("vector", lambda e: e.tensor_scalar(
                SS[:, 32:40], SS[:, 0:8], SS[:, 10:11], None, ALU.is_ge))
            dv2 = self.s_dve.tok()
            b.emit("scalar", lambda e: e.activation(
                SS[:, 40:48], SS[:, 0:8], AF.Exp, bias=SS[:, 9:10]),
                wait=(dv2,))
            at2 = self.s_act.tok()
            b.emit("vector", lambda e: e.scalar_tensor_tensor(
                SS[:, 48:56], SS[:, 40:48], SS[:, 13:14], SS[:, 32:40],
                ALU.mult, ALU.mult), wait=(at2,))
            dv3 = self.s_dve.tok()
            b.emit("scalar", lambda e: e.activation(
                self.PT[:, 896:904], SS[:, 48:56], AF.Copy), wait=(dv3,))
            at3 = self.s_act.tok()
            b.emit("tensor", lambda e, tt=tt: e.transpose(
                self.pq[0][0:8, 0:128], self.PT[:, 896:904], self.ID[:]),
                wait=(at3,) + self.bank_guard(None, "pqw"),
                inc=((self.s_pe, 1),))
            pt2 = self.s_pe.tok()
            b.emit("vector", lambda e: e.tensor_copy(
                self.PT[0:8, 512:640], self.pq[0][0:8, 0:128]),
                wait=(pt2,) + self.bank_guard(None, "wstg"))
            self.bank_done(None, self.s_dve.tok(), "pqw")
            dv4 = self.s_dve.tok()
            b.dma("sync", wrow.ap()[:, tt * 128:tt * 128 + 128],
                  self.PT[0:8, 512:640], wait=(dv4,))
            self.bank_done(None, b.sem("dma_sync").tok(), "wstg")
        b.wait("sync", (b.sem("dma_sync").tok(),))
        b.block()

        gw = b.inp("gatew", (16, 128, FT, 128))
        uw = b.inp("upw", (16, 128, FT, 128))
        dw = b.inp("downw", (FT, 128, 16, 128))
        s_w = b.sem("s_wld")
        gsl = [self.WR[:, i * 1024:(i + 1) * 1024] for i in range(2)]
        usl = [self.WR[:, 2048 + i * 1024:2048 + (i + 1) * 1024]
               for i in range(2)]
        for cg in range(2):
            def issue_gu(ci, cg=cg):
                c = cg * 8 + ci
                b.dma("gpsimd",
                      gsl[ci % 2].rearrange("p (kt m) -> p kt m", kt=FT),
                      gw.ap()[c],
                      wait=self.bank_guard(None, ("WR", ci % 2)), sem=s_w)
                b.dma("gpsimd",
                      usl[ci % 2].rearrange("p (kt m) -> p kt m", kt=FT),
                      uw.ap()[c],
                      wait=self.bank_guard(None, ("WR", 2 + ci % 2)), sem=s_w)

            issue_gu(0)
            issue_gu(1)
            for ci in range(8):
                wtk = s_w.tok()
                for nb in range(NB):
                    bg = self.pb[0 + nb % 2]
                    bu = self.pb[2 + nb % 2]
                    for kt in range(FT):
                        wt1 = ((wtk,) + self.bank_guard(bg)) if kt == 0 else ()
                        wt2 = ((wtk,) + self.bank_guard(bu)) if kt == 0 else ()
                        b.emit("tensor", lambda e, kt=kt, nb=nb, bg=bg,
                               sl=gsl[ci % 2], st=(kt == 0),
                               sp=(kt == FT - 1): e.matmul(
                            bg[:, 0:512], sl[:, kt * 128:kt * 128 + 128],
                            self.A1[:, kt * T + nb * 512:kt * T + nb * 512 + 512],
                            start=st, stop=sp), wait=wt1,
                            inc=((self.s_pe, 1),))
                        b.emit("tensor", lambda e, kt=kt, nb=nb, bu=bu,
                               sl=usl[ci % 2], st=(kt == 0),
                               sp=(kt == FT - 1): e.matmul(
                            bu[:, 0:512], sl[:, kt * 128:kt * 128 + 128],
                            self.A1[:, kt * T + nb * 512:kt * T + nb * 512 + 512],
                            start=st, stop=sp), wait=wt2,
                            inc=((self.s_pe, 1),))
                    ptk = self.s_pe.tok()
                    st_sl = self.PT[:, (nb % 2) * 512:(nb % 2) * 512 + 512]
                    sun = self.A2[:, ci * T + nb * 512:ci * T + nb * 512 + 512]
                    b.emit("scalar", lambda e, bg=bg, sun=sun: e.activation(
                        sun, bg[:, 0:512], AF.Silu), wait=(ptk,))
                    self.bank_done(bg, self.s_act.tok())
                    b.emit("scalar", lambda e, bu=bu, st_sl=st_sl: e.activation(
                        st_sl, bu[:, 0:512], AF.Copy),
                        wait=self.bank_guard(None, ("sst", nb % 2)))
                    self.bank_done(bu, self.s_act.tok())
                    at = self.s_act.tok()
                    b.emit("vector", lambda e, st_sl=st_sl, sun=sun:
                           e.tensor_mul(sun, sun, st_sl), wait=(at,))
                    self.bank_done(None, self.s_dve.tok(), ("sst", nb % 2))
                self.bank_done(None, self.s_pe.tok(), ("WR", ci % 2))
                self.bank_done(None, self.s_pe.tok(), ("WR", 2 + ci % 2))
                if ci + 2 < 8:
                    issue_gu(ci + 2)
            su_done = self.s_dve.tok()
            dsl = [self.WR[:, i * 1024:(i + 1) * 1024] for i in range(4)]

            def issue_d(mt, cg=cg, dsl=dsl, s_w=s_w, dw=dw):
                b.dma("gpsimd",
                      dsl[mt % 4].rearrange("p (kt m) -> p kt m", kt=8),
                      dw.ap()[mt, :, cg * 8:cg * 8 + 8, :],
                      wait=self.bank_guard(None, ("WR", mt % 4)), sem=s_w)

            b.wait("gpsimd", (self.s_pe.tok(),))
            for mt in range(4):
                issue_d(mt)
            for mt in range(FT):
                sl = dsl[mt % 4]
                wtk = s_w.tok()
                for nb in range(NB):
                    bank = self.pb[4 + nb % 2]
                    for c in range(8):
                        wt = ()
                        if c == 0:
                            wt = (wtk, su_done) + self.bank_guard(bank)
                        b.emit("tensor", lambda e, c=c, nb=nb, sl=sl, bank=bank,
                               st=(c == 0), sp=(c == 7): e.matmul(
                            bank[:, 0:512], sl[:, c * 128:c * 128 + 128],
                            self.A2[:, c * T + nb * 512:c * T + nb * 512 + 512],
                            start=st, stop=sp), wait=wt,
                            inc=((self.s_pe, 1),))
                    ptk = self.s_pe.tok()
                    b.emit("vector", lambda e, mt=mt, nb=nb, bank=bank:
                           e.tensor_add(
                        self.SX[:, mt * T + nb * 512:mt * T + nb * 512 + 512],
                        self.SX[:, mt * T + nb * 512:mt * T + nb * 512 + 512],
                        bank[:, 0:512]), wait=(ptk,))
                    self.bank_done(bank, self.s_dve.tok())
                self.bank_done(None, self.s_pe.tok(), ("WR", mt % 4))
                if mt + 4 < FT:
                    issue_d(mt + 4)
            b.block()

        fc1 = b.inp("fc1", (8, 128, 16384))
        fc2 = b.inp("fc2", (8, FT, 128, FT, 128))
        s_f = b.sem("dma_gpsimd")
        gtmp = self.PP[:, 2048:2048 + NB * 512]

        def wb_tiles(ex):
            ep = ex % 2
            return [self.PP[:, ep * 4096 + i * 512:ep * 4096 + (i + 1) * 512]
                    for i in range(NB)]

        def emit_wb(ex):
            ep = ex % 2
            wbt_t = wb_tiles(ex)
            s_wr = b.sem("dma_wrow%d" % ep)
            for rnd in range(0, NB, 2):
                js = [j for j in (0, 1) if rnd + j < NB]
                for j in js:
                    b.dma("sync", self.MB[0:1, (ep * 2 + j) * 512:
                                          (ep * 2 + j) * 512 + 512],
                          wrow.ap()[ex:ex + 1, (rnd + j) * 512:
                                    (rnd + j) * 512 + 512],
                          wait=self.bank_guard(None, ("wbr", ep, j)),
                          sem=s_wr)
                wbt = s_wr.tok()
                for j in js:
                    nb = rnd + j
                    bk = self.pb[4 + j]
                    b.emit("tensor", lambda e, j=j, bk=bk, ep=ep: e.matmul(
                        bk[:, 0:512], self.O1[:],
                        self.MB[0:1, (ep * 2 + j) * 512:
                                (ep * 2 + j) * 512 + 512],
                        start=True, stop=True),
                        wait=(wbt,) + self.bank_guard(bk),
                        inc=((self.s_pe, 1),))
                    ptk = self.s_pe.tok()
                    b.emit("scalar", lambda e, nb=nb, bk=bk, w=wbt_t[nb]:
                           e.activation(w, bk[:, 0:512], AF.Copy),
                        wait=(ptk,) + self.bank_guard(None, ("wbt", ep, nb)))
                    self.bank_done(bk, self.s_act.tok())
                    self.bank_done(None, self.s_act.tok(), ("wbr", ep, j))
            return self.s_act.tok()

        wb_done = emit_wb(0)
        for ex in range(8):
            wb_t = wb_tiles(ex)
            b.dma("gpsimd", self.A3[:, 0:16384], fc1.ap()[ex],
                  wait=(s_f.tok(),) + self.bank_guard(None, "A3fc1"),
                  sem=s_f)
            ftk = s_f.tok()
            a2g = self.bank_guard(None, "A2act")
            for i in range(8):
                ptoks = [None] * NB
                for nb in range(NB):
                    bank = self.pb[nb % 4]
                    for kt in range(FT):
                        wt = ()
                        if kt == 0:
                            wt = (ftk,) + self.bank_guard(bank)
                        b.emit("tensor", lambda e, i=i, kt=kt, nb=nb,
                               bank=bank, st=(kt == 0), sp=(kt == FT - 1):
                               e.matmul(
                            bank[:, 0:512],
                            self.A3[:, (i * FT + kt) * 128:
                                    (i * FT + kt) * 128 + 128],
                            self.A1[:, kt * T + nb * 512:
                                    kt * T + nb * 512 + 512],
                            start=st, stop=sp), wait=wt,
                            inc=((self.s_pe, 1),))
                    ptk = self.s_pe.tok()
                    b.emit("scalar", lambda e, i=i, nb=nb, bank=bank:
                           e.activation(
                        self.A2[:, i * 2048 + nb * 512:
                                i * 2048 + nb * 512 + 512],
                        bank[:, 0:512], AF.Silu),
                        wait=(ptk,) + (a2g if i == 0 and nb == 0 else ()))
                    self.bank_done(bank, self.s_act.tok())
                for nb in range(NB):
                    bank = self.pb[nb % 4]
                    for kt in range(FT):
                        wt = self.bank_guard(bank) if kt == 0 else ()
                        b.emit("tensor", lambda e, i=i, kt=kt, nb=nb,
                               bank=bank, st=(kt == 0), sp=(kt == FT - 1):
                               e.matmul(
                            bank[:, 0:512],
                            self.A3[:, ((i + 8) * FT + kt) * 128:
                                    ((i + 8) * FT + kt) * 128 + 128],
                            self.A1[:, kt * T + nb * 512:
                                    kt * T + nb * 512 + 512],
                            start=st, stop=sp), wait=wt,
                            inc=((self.s_pe, 1),))
                    ptk = self.s_pe.tok()
                    gsl_ = gtmp[:, nb * 512:nb * 512 + 512]
                    b.emit("scalar", lambda e, g=gsl_, bank=bank: e.activation(
                        g, bank[:, 0:512], AF.Copy),
                        wait=(ptk,) + self.bank_guard(None, ("gt", nb)))
                    self.bank_done(bank, self.s_act.tok())
                    at4 = self.s_act.tok()
                    aslot = self.A2[:, i * 2048 + nb * 512:
                                    i * 2048 + nb * 512 + 512]
                    b.emit("vector", lambda e, g=gsl_, aslot=aslot:
                           e.tensor_mul(aslot, aslot, g), wait=(at4,))
                    self.bank_done(None, self.s_dve.tok(), ("gt", nb))
                    b.emit("gpsimd", lambda e, w=wb_t[nb], aslot=aslot:
                           e.tensor_mul(aslot, aslot, w),
                           wait=(self.s_dve.tok(), wb_done))
            act_done = self.s_pool.tok()
            last_fc1 = self.s_pe.tok()
            self.bank_done(None, last_fc1, "A3fc1")
            for nb in range(NB):
                self.bank_done(None, act_done, ("wbt", ex % 2, nb))
            if ex + 1 < 8:
                wb_done = emit_wb(ex + 1)
            fsl = [self.WR[:, i * 1024:(i + 1) * 1024] for i in range(4)]

            def issue_fc2(dt, ex=ex, fsl=fsl, s_f=s_f, fc2=fc2):
                if dt % 2 == 1:
                    return
                w = (self.bank_guard(None, ("WR", dt % 4))
                     + self.bank_guard(None, ("WR", dt % 4 + 1)))
                b.dma("gpsimd",
                      self.WR[:, (dt % 4) * 1024:
                              (dt % 4 + 2) * 1024].rearrange(
                          "p (t kt m) -> p t kt m", t=2, kt=FT),
                      fc2.ap()[ex, dt:dt + 2].rearrange(
                          "t p kt m -> p t kt m"),
                      wait=w, sem=s_f)

            for dt in range(4):
                issue_fc2(dt)
            fnxt = 4
            for dt in range(FT):
                sl = fsl[dt % 4]
                wtk = s_f.tok()
                for nb in range(NB):
                    bank = self.pb[4 + nb % 2]
                    for it in range(FT):
                        wt = ()
                        if it == 0:
                            wt = (wtk, act_done) + self.bank_guard(bank)
                        b.emit("tensor", lambda e, it=it, nb=nb, sl=sl,
                               bank=bank, st=(it == 0), sp=(it == FT - 1):
                               e.matmul(
                            bank[:, 0:512], sl[:, it * 128:it * 128 + 128],
                            self.A2[:, it * 2048 + nb * 512:
                                    it * 2048 + nb * 512 + 512],
                            start=st, stop=sp), wait=wt,
                            inc=((self.s_pe, 1),))
                    ptk = self.s_pe.tok()
                    b.emit("vector", lambda e, dt=dt, nb=nb, bank=bank:
                           e.tensor_add(
                        self.SX[:, dt * T + nb * 512:dt * T + nb * 512 + 512],
                        self.SX[:, dt * T + nb * 512:dt * T + nb * 512 + 512],
                        bank[:, 0:512]), wait=(ptk,))
                    self.bank_done(bank, self.s_dve.tok())
                self.bank_done(None, self.s_pe.tok(), ("WR", dt % 4))
                while fnxt < FT and fnxt <= dt + 3:
                    issue_fc2(fnxt)
                    fnxt += 1
            self.bank_done(None, (self.s_pe.tok(), self.s_dve.tok(),
                                  self.s_pool.tok()), "A2act")
        self.emit_ln("plain")

    def emit_l1_ffn(self):
        b = self.b
        b.phase_tag = "l1ffn"
        T = self.T
        NB = self.NB
        mcd = self.b.nc.dram_tensor("mcdram", [16, 16384], BF16,
                                    kind="Internal")
        b.dma("gpsimd", self.PP[0:16, 0:4096].rearrange(
            "p (hr i) -> p hr i", hr=256),
            b.inp("makerT", (16, 4, 64, 16)).ap().rearrange(
                "j h r i -> j (h r) i"))
        b.dma("gpsimd", self.A2[0:16, 0:16384].rearrange(
            "p (hr k) -> p hr k", hr=256),
            b.inp("upT", (16, 4, 64, 64)).ap().rearrange(
                "j h r k -> j (h r) k"))
        mtk = b.sem("dma_gpsimd").tok()
        b.wait("gpsimd", (mtk,))
        b.block()
        for hr in range(256):
            bk = self.pb[hr % 2]
            b.emit("tensor", lambda e, hr=hr, bk=bk: e.matmul(
                bk[0:16, 0:64], self.PP[0:16, hr * 16:hr * 16 + 16],
                self.A2[0:16, hr * 64:hr * 64 + 64], start=True, stop=True),
                wait=self.bank_guard(bk), inc=((self.s_pe, 1),))
            ptk = self.s_pe.tok()
            b.emit("scalar", lambda e, hr=hr, bk=bk: e.activation(
                self.A3[0:16, hr * 64:hr * 64 + 64], bk[0:16, 0:64], AF.Copy),
                wait=(ptk,))
            self.bank_done(bk, self.s_act.tok())
        stg = self.s_act.tok()
        b.dma("sync", mcd.ap(), self.A3[0:16, 0:16384], wait=(stg,))
        b.wait("sync", (b.sem("dma_sync").tok(),))
        b.block()
        b.emit("gpsimd", lambda e: e.memset(self.A3[:, 0:16384], 0))
        b.wait("gpsimd", (self.s_pool.tok(),))
        b.block()
        mcv = mcd.ap().rearrange("i (h t a par k) -> i h t a par k",
                                 h=4, t=8, a=4, par=2)
        for a in range(4):
            for par in range(2):
                dst = self.A3[32 * a + 16 * par:32 * a + 16 * par + 16,
                              0:16384].rearrange(
                    "i (h t aa k) -> i h t aa k", h=4, t=8, aa=4)[
                        :, :, :, a, 64 * par:64 * par + 64]
                b.dma("sync", dst, mcv[:, :, :, a, par, :])
        b.wait("sync", (b.sem("dma_sync").tok(),))
        b.block()
        wc = b.inp("wcT", (16, FT, 128, 8, 128))
        s_w = b.sem("s_wld")
        csl = [self.WR[:, i * 1024:(i + 1) * 1024] for i in range(4)] + \
            [self.PP[:, i * 1024:(i + 1) * 1024] for i in range(8)]
        NCS = len(csl)
        for cg in range(16):
            mc_guard = self.bank_guard(None, "A2h")
            for ci in range(8):
                c = cg * 8 + ci
                pt_x = (c % 32) // 4
                for nb in range(NB):
                    bk = self.pb[2 + (ci * NB + nb) % 4]
                    w0 = self.ln_done if (cg == 0 and ci == 0 and nb == 0) \
                        else ()
                    b.emit("tensor", lambda e, c=c, nb=nb, bk=bk, pt_x=pt_x:
                           e.matmul(
                        bk[:, 0:512], self.A3[:, c * 128:c * 128 + 128],
                        self.A1[:, pt_x * T + nb * 512:pt_x * T + nb * 512 + 512],
                        start=True, stop=True),
                        wait=w0 + self.bank_guard(bk),
                        inc=((self.s_pe, 1),))
                    ptk = self.s_pe.tok()
                    b.emit("scalar", lambda e, ci=ci, nb=nb, bk=bk:
                           e.activation(
                        self.A2[:, ci * 2048 + nb * 512:
                                ci * 2048 + nb * 512 + 512],
                        bk[:, 0:512], AF.Gelu),
                        wait=(ptk,) + (mc_guard if ci == 0 and nb == 0
                                       else ()))
                    self.bank_done(bk, self.s_act.tok())
            h_done = self.s_act.tok()
            def issue_c(gi, cg=cg, s_w=s_w, wc=wc, csl=csl):
                if gi % 2 == 1:
                    return
                sidx = gi % NCS
                w = (self.bank_guard(None, ("cs", sidx))
                     + self.bank_guard(None, ("cs", sidx + 1)))
                if sidx < 4:
                    dst = self.WR[:, sidx * 1024:(sidx + 2) * 1024]
                else:
                    dst = self.PP[:, (sidx - 4) * 1024:(sidx - 2) * 1024]
                b.dma("gpsimd",
                      dst.rearrange("p (t kt m) -> p t kt m", t=2, kt=8),
                      wc.ap()[gi // FT, gi % FT:gi % FT + 2].rearrange(
                          "t p kt m -> p t kt m"),
                      wait=w, sem=s_w)

            if cg == 0:
                for gi in range(NCS):
                    issue_c(gi)
                self._cnxt = NCS
            for dt in range(FT):
                gi = cg * FT + dt
                sl = csl[gi % NCS]
                wtk = s_w.tok()
                for nb in range(NB):
                    bank = self.pb[(dt * NB + nb) % 2]
                    for ci in range(8):
                        wt = ()
                        if ci == 0:
                            wt = (wtk, h_done) + self.bank_guard(bank)
                        b.emit("tensor", lambda e, ci=ci, nb=nb, sl=sl,
                               bank=bank, st=(ci == 0), sp=(ci == 7):
                               e.matmul(
                            bank[:, 0:512], sl[:, ci * 128:ci * 128 + 128],
                            self.A2[:, ci * 2048 + nb * 512:
                                    ci * 2048 + nb * 512 + 512],
                            start=st, stop=sp), wait=wt,
                            inc=((self.s_pe, 1),))
                    ptk = self.s_pe.tok()
                    b.emit("vector", lambda e, dt=dt, nb=nb, bank=bank:
                           e.tensor_add(
                        self.SX[:, dt * T + nb * 512:dt * T + nb * 512 + 512],
                        self.SX[:, dt * T + nb * 512:dt * T + nb * 512 + 512],
                        bank[:, 0:512]), wait=(ptk,))
                    self.bank_done(bank, self.s_dve.tok())
                self.bank_done(None, self.s_pe.tok(), ("cs", gi % NCS))
                while self._cnxt < 16 * FT and self._cnxt <= gi + NCS - 1:
                    issue_c(self._cnxt)
                    self._cnxt += 1
            self.bank_done(None, self.s_pe.tok(), "A2h")
        self.emit_ln("plain")


_CACHE = {}


def _build(phases, T, colidx, ncol):
    key = (phases, T)
    if key in _CACHE:
        return _CACHE[key]
    nc = bass.Bass("TRN2", target_bir_lowering=False)
    mk = MK(nc, T, colidx)
    mk.emit_init(ncol)
    for ph in phases:
        if ph == "nop":
            pass
        elif ph == "cast":
            mk.emit_cast(mk.A1)
        elif ph == "ln":
            mk.emit_ln("plain")
        elif ph.startswith("l0u"):
            mk.emit_l0_attn(upto=int(ph[3:]))
        elif ph == "l1attn":
            mk.emit_l1_attn()
        elif ph == "l0attn":
            mk.emit_l0_attn()
        elif ph == "l0ffn":
            mk.emit_l0_ffn()
        elif ph == "l1ffn":
            mk.emit_l1_ffn()
        else:
            raise ValueError(ph)
    mk.emit_store()
    mk.b.finalize()
    names = sorted(mk.b.inputs.keys())
    _CACHE[key] = (nc, names)
    return nc, names


def _run_phases(phase_groups, host, x_fm, T):
    ncol = host["cvec"].shape[1]
    for phases in phase_groups:
        nc, names = _build(tuple(phases), T, host["_colidx"], ncol)
        in_maps = []
        for c in range(NCORES):
            m = {}
            for nname in names:
                m[nname] = x_fm[c] if nname == "xin" else host[nname]
            in_maps.append(m)
        res = run_bass_kernel_spmd(nc, in_maps, core_ids=list(range(NCORES)))
        x_fm = [np.asarray(res.results[c]["xout"]) for c in range(NCORES)]
    return x_fm


def kernel(**inputs):
    T = 2048
    host = prepare_host(inputs, T)
    x = np.asarray(inputs["L0_attn_x"], np.float32)
    x_fm = [np.ascontiguousarray(x[c].T) for c in range(NCORES)]
    groups = [["l0attn", "l0ffn", "l1attn", "l1ffn"]]
    x_fm = _run_phases(groups, host, x_fm, T)
    out = np.stack([x_fm[c].T for c in range(NCORES)], axis=0)
    return np.ascontiguousarray(out)


# version 26
# speedup vs baseline: 1.0454x; 1.0042x over previous
import contextlib

import numpy as np

import concourse.bass as bass
import concourse.mybir as mybir
from concourse.bass_utils import run_bass_kernel_spmd

F32 = mybir.dt.float32
BF16 = mybir.dt.bfloat16
AF = mybir.ActivationFunctionType
ALU = mybir.AluOpType
AX = mybir.AxisListType

NCORES = 8
D = 1024
FT = D // 128
NH = 16
HD = 64
NPAIR = 8

ENGS = ("tensor", "vector", "scalar", "gpsimd", "sync")


class Sem:
    __slots__ = ("h", "n", "name")

    def __init__(self, h, name):
        self.h = h
        self.n = 0
        self.name = name

    def tok(self):
        return (self, self.n)


class _Blk:
    def __init__(self):
        self.ops = {e: [] for e in ENGS}
        self.tag = None


class B:

    def __init__(self, nc, T):
        self.nc = nc
        self.T = T
        self.blocks = []
        self.cur = None
        self.stack = contextlib.ExitStack()
        self._sems = {}
        self.inputs = {}
        self.block()

    def sem(self, name):
        s = self._sems.get(name)
        if s is None:
            h = self.stack.enter_context(self.nc.semaphore(name))
            s = Sem(h, name)
            self._sems[name] = s
        return s

    def block(self, tag=None):
        self.cur = _Blk()
        self.cur.tag = tag or getattr(self, "phase_tag", None)
        self.blocks.append(self.cur)

    def esem(self, eng):
        return self.sem("s_" + eng)

    def emit(self, eng, fn, wait=(), inc=(), is_dma=False):
        wait = tuple(wait)
        if eng in ("vector", "scalar", "gpsimd"):
            es = self.esem(eng)
            wait = ((es, es.n),) + wait
            if not is_dma:
                inc = ((es, 1),) + tuple((s, k) for s, k in inc if s is not es)
        assert len(inc) <= 1, "one sem update per instruction"
        for s, k in inc:
            s.n += k
        self.cur.ops[eng].append((fn, tuple(wait), tuple(inc)))

    def wait(self, eng, toks):
        self.cur.ops[eng].append((None, tuple(toks), ()))

    def drain_dmas(self):
        for name, s in self._sems.items():
            if name.startswith("dma_") or name.startswith("s_wld"):
                q = name[4:] if name.startswith("dma_") else "gpsimd"
                if q not in ENGS:
                    q = "sync"
                self.wait(q, (s.tok(),))

    def dma(self, queue, dst, src, wait=(), sem=None):
        s = sem or self.sem("dma_" + queue)
        self.emit(queue, lambda e, d=dst, sr=src: e.dma_start(d, sr),
                  wait=wait, inc=((s, 16),), is_dma=True)
        return s.tok()

    def sbuf(self, name, shape, dt):
        return self.stack.enter_context(self.nc.sbuf_tensor(name, list(shape), dt))

    def psum(self, name, shape, dt):
        return self.stack.enter_context(self.nc.psum_tensor(name, list(shape), dt))

    def inp(self, name, shape, dt=F32):
        if name not in self.inputs:
            self.inputs[name] = self.nc.dram_tensor(name, list(shape), dt,
                                                    kind="ExternalInput")
        return self.inputs[name]

    def finalize(self):
        import contextlib as _ctx
        for bi, blk in enumerate(self.blocks):
            if not any(blk.ops[e] for e in ENGS):
                continue
            scope = (self.nc.named_scope("%s_%d" % (blk.tag, bi))
                     if blk.tag else _ctx.nullcontext())
            with scope, self.nc.Block() as bb:
                for ename in ENGS:
                    entries = blk.ops[ename]
                    if not entries:
                        continue

                    def replay(eng, entries=entries):
                        for fn, waits, incs in entries:
                            for s, v in waits:
                                eng.wait_ge(s.h, v)
                            if fn is None:
                                continue
                            inst = fn(eng)
                            for s, k in incs:
                                inst.then_inc(s.h, k)

                    getattr(bb, ename)(replay)
        self.stack.close()


def _tiles_mk(w):
    K, M = w.shape
    return np.ascontiguousarray(
        w.reshape(K // 128, 128, M // 128, 128).transpose(2, 1, 0, 3))


def prepare_host(inputs, T):
    h = {}
    f32 = np.float32

    n = np.arange(T, dtype=np.float64)
    g = 1.0 - 2.0 ** (-5.0 - np.arange(NH, dtype=np.float64))
    gq = np.zeros((NPAIR, 128, T), f32)
    gk = np.zeros((NPAIR, 128, T), f32)
    for p in range(NPAIR):
        for j, hh in enumerate((2 * p, 2 * p + 1)):
            gq[p, 64 * j:64 * j + 64, :] = (g[hh] ** n)[None, :].astype(f32)
            gk[p, 64 * j:64 * j + 64, :] = (g[hh] ** (-n))[None, :].astype(f32)
    h["gq"] = gq
    h["gk"] = gk
    masks = np.zeros((4, 128, 512), f32)
    ii = np.arange(128)[:, None]
    jj = np.arange(512)[None, :]
    for oi, off in enumerate((0, 128, 256, 384)):
        masks[oi] = (jj >= ii + off).astype(f32)
    h["masks"] = masks
    h["ident"] = np.eye(128, dtype=f32)
    h["ones1"] = np.ones((1, 128), f32)
    h["onesm"] = np.full((128, 1), 1.0 / D, f32)

    cols = []
    colidx = {}

    def addvec(name, v):
        v = np.asarray(v, f32).reshape(-1, 128)
        colidx[name] = len(cols)
        for t in range(v.shape[0]):
            cols.append(v[t])

    for i in range(2):
        pb = inputs["L0_attn_proj_b"][i]
        addvec(f"bq{i}", pb[0:1024])
        addvec(f"bk{i}", pb[1024:2048])
        addvec(f"bv{i}", pb[2048:3072])
        addvec(f"bxo{i}", pb[3072:4096])
        addvec(f"reto_b{i}", inputs["L0_attn_reto_b"][i])
        addvec(f"ffn1_b{i}", inputs["L0_attn_ffn1_b"][i])
        addvec(f"ffn2_b{i}", inputs["L0_attn_ffn2_b"][i])
        addvec(f"ln1_g{i}", inputs["L0_attn_ln1_g"][i])
        addvec(f"ln1_b{i}", inputs["L0_attn_ln1_b"][i])
        addvec(f"ln2_g{i}", inputs["L0_attn_ln2_g"][i])
        addvec(f"ln2_b{i}", inputs["L0_attn_ln2_b"][i])
    addvec("lnf_g", inputs["L0_attn_lnf_g"])
    addvec("lnf_b", inputs["L0_attn_lnf_b"])
    addvec("b1", inputs["L1_attn_b1"])
    addvec("b2", inputs["L1_attn_b2"])
    addvec("b4", inputs["L1_attn_b4"])
    h["cvec"] = np.ascontiguousarray(np.stack(cols, axis=1))
    h["_colidx"] = colidx
    h["b3row"] = np.asarray(inputs["L1_attn_b3"], f32).reshape(1, D)

    for i in range(2):
        pw = inputs["L0_attn_proj_w"][i]
        h[f"wq{i}"] = _tiles_mk(pw[:, 0:1024])
        h[f"wk{i}"] = _tiles_mk(pw[:, 1024:2048])
        h[f"wv{i}"] = _tiles_mk(pw[:, 2048:3072])
        h[f"wxo{i}"] = _tiles_mk(pw[:, 3072:4096])
        h[f"reto{i}"] = _tiles_mk(inputs["L0_attn_reto_w"][i])
        h[f"ffn1_{i}"] = _tiles_mk(inputs["L0_attn_ffn1_w"][i])
        h[f"ffn2_{i}"] = _tiles_mk(inputs["L0_attn_ffn2_w"][i])

    h["wrouter"] = np.ascontiguousarray(
        inputs["L0_ffn_w_router"].reshape(FT, 128, 8))
    h["fc1"] = np.stack([_tiles_mk(inputs["L0_ffn_fc1_w"][e]).transpose(
        1, 0, 2, 3).reshape(128, 16384) for e in range(8)])
    h["fc2"] = np.stack([_tiles_mk(inputs["L0_ffn_fc2_w"][e]) for e in range(8)])
    h["gatew"] = _tiles_mk(inputs["L0_ffn_gate_w"])
    h["upw"] = _tiles_mk(inputs["L0_ffn_up_w"])
    h["downw"] = _tiles_mk(inputs["L0_ffn_down_w"])

    h["w1"] = _tiles_mk(inputs["L1_attn_W1"])
    h["w2"] = _tiles_mk(inputs["L1_attn_W2"])
    h["w3"] = np.ascontiguousarray(inputs["L1_attn_W3"].reshape(FT, 128, D))
    h["w4"] = _tiles_mk(inputs["L1_attn_W4"])

    h["makerT"] = np.ascontiguousarray(
        inputs["L1_ffn_rail_maker"].transpose(3, 0, 1, 2))
    h["upT"] = np.ascontiguousarray(
        inputs["L1_ffn_rail_up_proj"].transpose(3, 0, 1, 2))
    wct = np.ascontiguousarray(inputs["L1_ffn_w_cproj"].T)
    wct = wct.reshape(16, 8, 128, 8, 128).transpose(0, 3, 2, 1, 4)
    h["wcT"] = np.ascontiguousarray(wct)
    return h


class MK:
    def __init__(self, nc, T, colidx):
        self.b = B(nc, T)
        b = self.b
        self.T = T
        self.NB = T // 512
        self.NTT = T // 128
        self.colidx = colidx
        ncol = 320
        self.SX = b.sbuf("SX", (128, FT * T), F32)
        self.A1 = b.sbuf("A1", (128, max(FT * T, 16384)), BF16)
        self.A2 = b.sbuf("A2", (128, max(FT * T, 16384)), BF16)
        self.A3 = b.sbuf("A3", (128, max(FT * T, 16384)), BF16)
        self.PP = b.sbuf("PP", (128, max(5 * T, 8192)), BF16)
        self.PT = b.sbuf("PT", (128, 1024), BF16)
        self.WR = b.sbuf("WR", (128, 4096), BF16)
        self.CB = b.sbuf("CB", (128, ncol), F32)
        self.MB = b.sbuf("MB", (128, 4 * 512), BF16)
        self.ID = b.sbuf("ID", (128, 128), BF16)
        self.O1 = b.sbuf("O1", (1, 128), BF16)
        self.OM = b.sbuf("OM", (128, 1), BF16)
        self.SS = b.sbuf("SS", (128, 64), F32)
        self.PF = b.sbuf("PF", (128, 2048), F32)
        self.pb = [b.psum(f"pb{i}", (128, 512), F32) for i in range(6)]
        self.pq = [b.psum("pq0", (128, 1024), BF16),
                   b.psum("pq1", (128, 1024), BF16)]
        self.s_pe = b.sem("s_pe")
        self.s_act = b.esem("scalar")
        self.s_dve = b.esem("vector")
        self.s_pool = b.esem("gpsimd")
        self._bank_free = {}
        self._uid = 0
        self._consts = {}

    def uid(self):
        self._uid += 1
        return self._uid

    def cv(self, name, tile):
        c = self.colidx[name] + tile
        return self.CB[:, c:c + 1]

    def bank_guard(self, bank, key=None):
        t = self._bank_free.get(key if key is not None else id(bank))
        if not t:
            return ()
        return t if isinstance(t, tuple) and t and isinstance(t[0], tuple) \
            else (t,)

    def bank_done(self, bank, tok, key=None):
        self._bank_free[key if key is not None else id(bank)] = tok

    def esem(self, eng):
        return {"scalar": self.s_act, "vector": self.s_dve,
                "gpsimd": self.s_pool, "tensor": self.s_pe}[eng]

    def const_f32(self, val):
        val = float(val)
        if val not in self._consts:
            ap = self.b.sbuf("c%d" % self.uid(), (128, 1), F32)
            self.b.emit("gpsimd", lambda e, a=ap, v=val: e.memset(a[:], v),
                        inc=((self.s_pool, 1),))
            self.b.nc.const_aps.register(F32, val, ap[:])
            self._consts[val] = ap
        return val

    def emit_init(self, cvec_ncol):
        b = self.b
        T = self.T
        xin = b.inp("xin", (D, T))
        b.dma("sync",
              self.SX[:, 0:FT * T].rearrange("p (ft t) -> p ft t", ft=FT),
              xin.ap().rearrange("(ft p) t -> p ft t", p=128))
        b.dma("sync", self.CB[:, 0:cvec_ncol],
              b.inp("cvec", (128, cvec_ncol)).ap())
        b.dma("gpsimd",
              self.MB[:, 0:4 * 512].rearrange("p (o n) -> p o n", o=4),
              b.inp("masks", (4, 128, 512)).ap().rearrange("o p n -> p o n"))
        b.dma("gpsimd", self.ID[:], b.inp("ident", (128, 128)).ap())
        b.dma("gpsimd", self.O1[:], b.inp("ones1", (1, 128)).ap())
        b.dma("gpsimd", self.OM[:], b.inp("onesm", (128, 1)).ap())
        self.const_f32(1e-5)
        self.const_f32(1.0 / 32.0)
        self.const_f32(-0.5)
        b.drain_dmas()
        b.block()

    def emit_store(self):
        b = self.b
        T = self.T
        out = self.b.nc.dram_tensor("xout", [D, T], F32, kind="ExternalOutput")
        s = b.sem("dma_out")
        b.dma("sync", out.ap().rearrange("(ft p) t -> p ft t", p=128),
              self.SX[:, 0:FT * T].rearrange("p (ft t) -> p ft t", ft=FT),
              wait=getattr(self, "ln_done", ()), sem=s)
        b.emit("sync", lambda e: e.nop(), wait=(s.tok(),))
        b.block()

    def emit_cast(self, dst=None):
        b = self.b
        T = self.T
        dst = dst if dst is not None else self.A1
        engs = ("scalar", "vector", "gpsimd")
        for ft in range(FT):
            eng = engs[ft % 3]
            sl = slice(ft * T, (ft + 1) * T)
            if eng == "scalar":
                b.emit("scalar", lambda e, s=sl: e.activation(
                    dst[:, s], self.SX[:, s], AF.Copy), inc=((self.s_act, 1),))
            else:
                b.emit(eng, lambda e, s=sl: e.tensor_copy(
                    dst[:, s], self.SX[:, s]), inc=((self.esem(eng), 1),))
        b.block()

    def warm(self, bank, wait=()):
        self.b.emit("tensor", lambda e, bank=bank: e.matmul(
            bank[0:1, 0:64], self.O1[0:1, 0:1], self.O1[0:1, 0:64],
            start=True, stop=True), wait=wait, inc=((self.s_pe, 1),))

    def emit_ln(self, kind, gname=None, bname=None, dst=None, castbuf=None,
                upto=99):
        b = self.b
        T = self.T
        cb = castbuf if castbuf is not None else self.A1
        wb = self.pb[5]
        b.block()
        apply_q = []

        def emit_apply(nb, at, mtile, rtile):
            for ft in range(FT):
                s = slice(ft * T + nb * 512, ft * T + nb * 512 + 512)
                eng = ("vector", "gpsimd")[ft % 2]
                sem = self.esem(eng)
                tgt = self.SX if kind == "plain" else dst
                b.emit(eng, lambda e, s=s, tgt=tgt, mtile=mtile: e.tensor_add(
                    tgt[:, s], self.SX[:, s], mtile), wait=(at,),
                    inc=((sem, 1),))
                b.emit(eng, lambda e, s=s, tgt=tgt, rtile=rtile: e.tensor_mul(
                    tgt[:, s], tgt[:, s], rtile), inc=((sem, 1),))
                if kind == "plain":
                    eng2 = ("gpsimd", "vector")[ft % 2]
                    b.emit(eng2, lambda e, s=s: e.tensor_copy(
                        self.A1[:, s], self.SX[:, s]), wait=(sem.tok(),))
                if kind == "affine":
                    et = sem.tok()
                    b.emit("scalar", lambda e, s=s, ft=ft: e.activation(
                        dst[:, s], dst[:, s], AF.Identity,
                        bias=self.cv(bname, ft), scale=self.cv(gname, ft)),
                        wait=(et,), inc=((self.s_act, 1),))
            par = nb % 2
            ldone = (self.s_dve.tok(), self.s_pool.tok(), self.s_act.tok())
            self.bank_done(None, ldone, ("PTm", par))
            self.bank_done(None, ldone, ("PTr", par))

        for nb in range(self.NB):
            par = nb % 2
            pmean = self.pb[0 + 2 * par]
            psq = self.pb[1 + 2 * par]
            srm = self.MB[0:1, par * 1024:par * 1024 + 512]
            srr = self.MB[0:1, par * 1024 + 512:par * 1024 + 1024]
            scr = self.PF[0:1, par * 1024 + 512:par * 1024 + 1024]
            mtile = self.PF[:, par * 1024:par * 1024 + 512]
            rtile = self.PF[:, par * 1024 + 512:par * 1024 + 1024]
            sq = [self.PT[:, 0:512], self.PT[:, 512:1024]]
            sqtok = [None] * FT
            for ft in range(FT):
                s = slice(ft * T + nb * 512, ft * T + nb * 512 + 512)
                ceng = ("scalar", "vector", "gpsimd")[ft % 3]
                csem = self.esem(ceng)
                if ceng == "scalar":
                    b.emit("scalar", lambda e, s=s: e.activation(
                        cb[:, s], self.SX[:, s], AF.Copy), inc=((csem, 1),))
                else:
                    b.emit(ceng, lambda e, s=s: e.tensor_copy(
                        cb[:, s], self.SX[:, s]), inc=((csem, 1),))
                ct = csem.tok()
                if ft >= 2:
                    w2 = (sqtok[ft - 2],) + (self.bank_guard(psq)
                                             if ft == 2 else ())
                    b.emit("tensor", lambda e, q=sq[ft % 2], ft=ft,
                           psq=psq: e.matmul(
                        psq[0:1, 0:512], self.OM[:], q,
                        start=(ft == 2), stop=False),
                        wait=w2, inc=((self.s_pe, 1),))
                    self.bank_done(None, self.s_pe.tok(), ("sq", ft % 2))
                qs = sq[ft % 2]
                b.emit("scalar", lambda e, s=s, q=qs: e.activation(
                    q, cb[:, s], AF.Square),
                    wait=(ct,) + self.bank_guard(qs, ("sq", ft % 2)))
                sqtok[ft] = self.s_act.tok()
                w1 = (ct,) + (self.bank_guard(pmean) if ft == 0 else ())
                b.emit("tensor", lambda e, s=s, ft=ft, pmean=pmean: e.matmul(
                    pmean[0:1, 0:512], self.OM[:], cb[:, s],
                    start=(ft == 0), stop=(ft == FT - 1)),
                    wait=w1, inc=((self.s_pe, 1),))
            for ft in (FT - 2, FT - 1):
                b.emit("tensor", lambda e, q=sq[ft % 2], ft=ft, psq=psq:
                       e.matmul(
                    psq[0:1, 0:512], self.OM[:], q,
                    start=False, stop=(ft == FT - 1)),
                    wait=(sqtok[ft],), inc=((self.s_pe, 1),))
                self.bank_done(None, self.s_pe.tok(), ("sq", ft % 2))
            pet = self.s_pe.tok()
            b.emit("scalar", lambda e, pmean=pmean, scr=scr: e.activation(
                scr, pmean[0:1, 0:512], AF.Square),
                wait=(pet,) + self.bank_guard(None, ("SC", par))
                + self.bank_guard(None, ("PTr", par)))
            sq_t = self.s_act.tok()
            self.warm(wb, (sq_t,))
            b.emit("vector", lambda e, psq=psq, scr=scr: e.tensor_sub(
                psq[0:1, 0:512], psq[0:1, 0:512], scr), wait=(sq_t,))
            self.bank_done(None, self.s_dve.tok(), ("SC", par))
            b.emit("vector", lambda e, pmean=pmean, srm=srm:
                   e.tensor_scalar_mul(srm, pmean[0:1, 0:512], -1.0),
                   wait=self.bank_guard(None, ("SRm", par)))
            dv_t = self.s_dve.tok()
            b.emit("scalar", lambda e, psq=psq: e.activation(
                psq[0:1, 0:512], psq[0:1, 0:512], AF.Ln, bias=1e-5),
                wait=(dv_t,))
            b.emit("scalar", lambda e, psq=psq, srr=srr: e.activation(
                srr, psq[0:1, 0:512], AF.Exp, scale=-0.5),
                wait=self.bank_guard(None, ("SRr", par)))
            rt = (dv_t, self.s_act.tok())
            self.bank_done(pmean, rt[1])
            self.bank_done(psq, rt[1])
            self.warm(wb, (rt[1],))
            b.emit("tensor", lambda e, pmean=pmean, srm=srm: e.matmul(
                pmean[:, 0:512], self.O1[:], srm, start=True, stop=True),
                wait=rt, inc=((self.s_pe, 1),))
            self.bank_done(None, self.s_pe.tok(), ("SRm", par))
            b.emit("tensor", lambda e, psq=psq, srr=srr: e.matmul(
                psq[:, 0:512], self.O1[:], srr, start=True, stop=True),
                wait=rt, inc=((self.s_pe, 1),))
            self.bank_done(None, self.s_pe.tok(), ("SRr", par))
            bt = self.s_pe.tok()
            b.emit("scalar", lambda e, pmean=pmean, mtile=mtile: e.activation(
                mtile, pmean[:, 0:512], AF.Copy),
                wait=(bt,) + self.bank_guard(None, ("PTm", par)))
            b.emit("scalar", lambda e, psq=psq, rtile=rtile: e.activation(
                rtile, psq[:, 0:512], AF.Copy),
                wait=self.bank_guard(None, ("PTr", par)))
            at = self.s_act.tok()
            self.bank_done(pmean, at)
            self.bank_done(psq, at)
            self.warm(wb, (at,))
            apply_q.append((nb, at, mtile, rtile))
            if len(apply_q) > 1:
                emit_apply(*apply_q.pop(0))
        while apply_q:
            emit_apply(*apply_q.pop(0))
        self.ln_done = (self.s_dve.tok(), self.s_pool.tok(),
                        self.s_act.tok())

    def proj(self, wname, Mt, Kt, src_bf, evac, wdram=None, banks=None,
             nb_range=None, src_off=0, src_T=None, mt_range=None,
             slot_range=None, wsem="s_wld", first_wait=()):
        b = self.b
        T = self.T
        sT = src_T if src_T is not None else T
        nbs = list(range(self.NB)) if nb_range is None else list(nb_range)
        mts = list(range(Mt)) if mt_range is None else list(mt_range)
        wdram = wdram if wdram is not None else b.inp(wname, (Mt, 128, Kt, 128))
        wsz = Kt * 128
        srange = list(slot_range) if slot_range is not None else \
            list(range(max(1, min(4, 4096 // wsz))))
        nslot = len(srange)
        assert nslot >= 2 or len(mts) == 1, "weight strip too large for ring"
        slots = [self.WR[:, i * wsz:(i + 1) * wsz] for i in srange]
        banks = banks or self.pb[0:4]
        s_w = b.sem(wsem)
        dtok = [None] * (len(mts) if mt_range is not None else Mt)
        pair_ok = (mt_range is None and nslot % 2 == 0)

        def issue_dma(i):
            if pair_ok:
                if i % 2 == 1:
                    return
                w = (self.bank_guard(None, ("WR", srange[i % nslot]))
                     + self.bank_guard(None, ("WR", srange[(i + 1) % nslot])))
                if i + 1 < len(mts) and mts[i + 1] == mts[i] + 1 \
                        and (i % nslot) + 1 == (i + 1) % nslot:
                    dst = self.WR[:, srange[i % nslot] * wsz:
                                  (srange[i % nslot] + 2) * wsz].rearrange(
                        "p (t kt m) -> p t kt m", t=2, kt=Kt)
                    dtok[i] = b.dma(
                        "gpsimd", dst,
                        wdram.ap()[mts[i]:mts[i] + 2].rearrange(
                            "t p kt m -> p t kt m"),
                        wait=w, sem=s_w)
                    dtok[i + 1] = dtok[i]
                    return
            w = self.bank_guard(None, ("WR", srange[i % nslot]))
            dtok[i] = b.dma(
                "gpsimd",
                slots[i % nslot].rearrange("p (kt m) -> p kt m", kt=Kt),
                wdram.ap()[mts[i]], wait=w, sem=s_w)

        nxt = min(nslot, len(mts))
        for i in range(nxt):
            issue_dma(i)
        for i, mt in enumerate(mts):
            sl = slots[i % nslot]
            for kt in range(Kt):
                for j, nb in enumerate(nbs):
                    bank = banks[j % len(banks)]
                    wt = ()
                    if kt == 0 and j == 0:
                        wt += (s_w.tok(),)
                        if i == 0:
                            wt += tuple(first_wait)
                    if kt == 0:
                        wt += self.bank_guard(bank)
                    rhs = src_bf[:, src_off + kt * sT + nb * 512:
                                 src_off + kt * sT + nb * 512 + 512]
                    b.emit("tensor", lambda e, sl=sl, kt=kt, rhs=rhs, bank=bank,
                           st=(kt == 0), sp=(kt == Kt - 1): e.matmul(
                        bank[:, 0:512], sl[:, kt * 128:kt * 128 + 128], rhs,
                        start=st, stop=sp),
                        wait=wt, inc=((self.s_pe, 1),))
            self.bank_done(None, self.s_pe.tok(), ("WR", srange[i % nslot]))
            lim = i + nslot - (1 if pair_ok else 0)
            while nxt < len(mts) and nxt <= lim:
                issue_dma(nxt)
                nxt += 1
            
            for j, nb in enumerate(nbs):
                evac(mt, nb, banks[j % len(banks)], self.s_pe.tok())

    def emit_l1_attn(self):
        b = self.b
        b.phase_tag = "l1attn"
        T = self.T
        NTT = self.NTT
        NB = self.NB
        SXb = self.SX[:, 0:FT * T].bitcast(BF16)
        P_P = 0
        P_PT = 4 * T
        P_AO = 8 * T
        if not hasattr(self, "xpark"):
            self.xpark = self.b.nc.dram_tensor("xpark", [128, FT * T], F32,
                                               kind="Internal")
        xpark = self.xpark
        pk = b.dma("sync", xpark.ap(), self.SX[:, 0:FT * T],
                   wait=getattr(self, "ln_done", ()))
        b.wait("sync", (pk,))
        dt0 = b.dma("gpsimd", self.MB[0:1, 0:D], b.inp("b3row", (1, D)).ap())
        for half in range(2):
            bk = self.pb[4]
            b.emit("tensor", lambda e, h=half, bk=bk: e.matmul(
                bk[:, 0:512], self.O1[:],
                self.MB[0:1, h * 512:h * 512 + 512],
                start=True, stop=True),
                wait=(dt0,) + self.bank_guard(bk), inc=((self.s_pe, 1),))
            b.emit("scalar", lambda e, h=half, bk=bk: e.activation(
                self.PF[:, h * 512:h * 512 + 512], bk[:, 0:512], AF.Copy),
                wait=(self.s_pe.tok(),), inc=((self.s_act, 1),))
            self.bank_done(bk, self.s_act.tok())
        b.block()
        w3 = b.inp("w3", (FT, 128, D))
        s_w = b.sem("s_wld")
        for vh in range(2):
            wv = None
            for kt in range(FT):
                sl = self.WR[:, kt * 512:(kt + 1) * 512]
                wv = b.dma("gpsimd", sl,
                           w3.ap()[kt, :, vh * 512:vh * 512 + 512], sem=s_w)
            for tt in range(NTT):
                bk = self.pb[tt % 2]
                for kt in range(FT):
                    wt = ()
                    if kt == 0:
                        wt = (wv,) + self.bank_guard(bk)
                        if vh == 0 and tt == 0:
                            wt += self.ln_done
                    b.emit("tensor", lambda e, kt=kt, tt=tt, bk=bk,
                           st=(kt == 0), sp=(kt == FT - 1): e.matmul(
                        bk[:, 0:512],
                        self.A1[:, kt * T + tt * 128:kt * T + tt * 128 + 128],
                        self.WR[:, kt * 512:kt * 512 + 512],
                        start=st, stop=sp), wait=wt, inc=((self.s_pe, 1),))
                pt = self.s_pe.tok()
                b.emit("vector", lambda e, tt=tt, vh=vh, bk=bk: e.tensor_add(
                    self.A2[:, tt * D + vh * 512:tt * D + vh * 512 + 512],
                    bk[:, 0:512], self.PF[:, vh * 512:vh * 512 + 512]),
                    wait=(pt,), inc=((self.s_dve, 1),))
                self.bank_done(bk, self.s_dve.tok())
            b.block()

        def evac_k(mt, nb, bank, tok):
            b.emit("scalar", lambda e, mt=mt, nb=nb, bank=bank: e.activation(
                self.A3[:, mt * T + nb * 512:mt * T + nb * 512 + 512],
                bank[:, 0:512], AF.Identity, bias=self.cv("b2", mt)),
                wait=(tok,), inc=((self.s_act, 1),))
            self.bank_done(bank, self.s_act.tok())

        self.proj("w2", FT, FT, self.A1, evac_k)
        b.block()

        w1 = b.inp("w1", (FT, 128, FT, 128))
        w4 = b.inp("w4", (FT, 128, FT, 128))
        for ns in range(NB):
            def evac_q(mt, nb, bank, tok):
                qg = self.bank_guard(None, "PPq") if mt == 0 else ()
                b.emit("scalar", lambda e, mt=mt, bank=bank: e.activation(
                    self.PP[:, mt * 512:mt * 512 + 512],
                    bank[:, 0:512], AF.Identity, bias=self.cv("b1", mt)),
                    wait=(tok,) + qg, inc=((self.s_act, 1),))
                self.bank_done(bank, self.s_act.tok())

            self.proj("w1", FT, FT, self.A1, evac_q, wdram=w1,
                      banks=self.pb[4:6], nb_range=(ns,), slot_range=(0, 1),
                      wsem="s_wldq")
            q_done = self.s_act.tok()
            for ntl in range(4):
                sbanks = [self.pb[i] for i in range(4)]
                for mb in range(NB):
                    bk = sbanks[mb % 4]
                    for kt in range(FT):
                        wt = ((q_done,) + self.bank_guard(bk)) \
                            if kt == 0 else ()
                        b.emit("tensor", lambda e, kt=kt, mb=mb, bk=bk,
                               ntl=ntl, st=(kt == 0), sp=(kt == FT - 1):
                               e.matmul(
                            bk[:, 0:512],
                            self.PP[:, kt * 512 + ntl * 128:
                                    kt * 512 + ntl * 128 + 128],
                            self.A3[:, kt * T + mb * 512:
                                    kt * T + mb * 512 + 512],
                            start=st, stop=sp),
                            wait=wt, inc=((self.s_pe, 1),))
                spt = self.s_pe.tok()
                for mb in range(NB):
                    b.emit("vector", lambda e, mb=mb: e.reduce_max(
                        self.SS[:, 8 + mb:9 + mb], sbanks[mb % 4][:, 0:512],
                        AX.X), wait=(spt,), inc=((self.s_dve, 1),))
                b.emit("vector", lambda e, NB=NB: e.tensor_reduce(
                    self.SS[:, 0:1], self.SS[:, 8:8 + NB], AX.X, ALU.max),
                    inc=((self.s_dve, 1),))
                b.emit("vector", lambda e: e.tensor_scalar_mul(
                    self.SS[:, 0:1], self.SS[:, 0:1], -1.0 / 32.0),
                    inc=((self.s_dve, 1),))
                mt_ = self.s_dve.tok()
                for mb in range(NB):
                    pw = self.bank_guard(None, "pSXb") \
                        if (ntl == 0 and mb == 0) else ()
                    b.emit("scalar", lambda e, mb=mb, ntl=ntl: e.activation(
                        SXb[:, P_P + ntl * T + mb * 512:
                            P_P + ntl * T + mb * 512 + 512],
                        sbanks[mb % 4][:, 0:512], AF.Exp,
                        bias=self.SS[:, 0:1], scale=1.0 / 32.0,
                        accum_out=self.SS[:, 16 + mb:17 + mb]),
                        wait=(mt_,) + pw, inc=((self.s_act, 1),))
                    self.bank_done(sbanks[mb % 4], self.s_act.tok())
                et = self.s_act.tok()
                b.emit("vector", lambda e, NB=NB: e.tensor_reduce(
                    self.SS[:, 1:2], self.SS[:, 16:16 + NB], AX.X, ALU.add),
                    wait=(et,), inc=((self.s_dve, 1),))
                b.emit("vector", lambda e: e.reciprocal(
                    self.SS[:, 2:3], self.SS[:, 1:2]), inc=((self.s_dve, 1),))
                rt = self.s_dve.tok()
                for mb in range(NB):
                    b.emit("vector", lambda e, mb=mb, ntl=ntl:
                           e.tensor_scalar_mul(
                        SXb[:, P_P + ntl * T + mb * 512:
                            P_P + ntl * T + mb * 512 + 512],
                        SXb[:, P_P + ntl * T + mb * 512:
                            P_P + ntl * T + mb * 512 + 512],
                        self.SS[:, 2:3]), wait=(rt,), inc=((self.s_dve, 1),))
            self.bank_done(None, self.s_pe.tok(), "PPq")
            psc = self.s_dve.tok()
            for mc in range(NTT):
                pqv = self.pq[mc % 2][:, 0:512]
                for ntl in range(4):
                    w0 = ((psc,) if (mc == 0 and ntl == 0) else ())
                    b.emit("tensor", lambda e, mc=mc, ntl=ntl, pqv=pqv:
                           e.transpose(
                        pqv[:, ntl * 128:ntl * 128 + 128],
                        SXb[:, P_P + ntl * T + mc * 128:
                            P_P + ntl * T + mc * 128 + 128],
                        self.ID[:]),
                        wait=w0 + (self.bank_guard(pqv, ("pq", mc % 2))
                                   if ntl == 0 else ()),
                        inc=((self.s_pe, 1),))
                tt_ = self.s_pe.tok()
                b.emit("vector", lambda e, mc=mc, pqv=pqv: e.tensor_copy(
                    SXb[:, P_PT + mc * 512:P_PT + mc * 512 + 512], pqv),
                    wait=(tt_,) + (self.bank_guard(None, "pTSXb")
                                   if mc == 0 else ()),
                    inc=((self.s_dve, 1),))
                self.bank_done(pqv, self.s_dve.tok(), ("pq", mc % 2))
            ptev = self.s_dve.tok()
            self.bank_done(None, self.s_pe.tok(), "pSXb")
            for dt_ in range(FT):
                bk = self.pb[4 + (dt_ % 2)]
                for mc in range(NTT):
                    wt = ((ptev,) + self.bank_guard(bk)) \
                        if mc == 0 else ()
                    b.emit("tensor", lambda e, dt_=dt_, mc=mc, bk=bk,
                           st=(mc == 0), sp=(mc == NTT - 1): e.matmul(
                        bk[:, 0:512],
                        self.A2[:, mc * D + dt_ * 128:mc * D + dt_ * 128 + 128],
                        SXb[:, P_PT + mc * 512:P_PT + mc * 512 + 512],
                        start=st, stop=sp), wait=wt, inc=((self.s_pe, 1),))
                pt = self.s_pe.tok()
                b.emit("scalar", lambda e, dt_=dt_, bk=bk: e.activation(
                    self.PP[:, 8 * 512 + dt_ * 512:8 * 512 + dt_ * 512 + 512],
                    bk[:, 0:512], AF.Copy),
                    wait=(pt,) + (self.bank_guard(None, "PPr")
                                  if dt_ == 0 else ()),
                    inc=((self.s_act, 1),))
                self.bank_done(bk, self.s_act.tok())
            retev = self.s_act.tok()
            self.bank_done(None, self.s_pe.tok(), "pTSXb")

            def evac_o(mt, nb, bank, tok, ns=ns):
                b.emit("scalar", lambda e, mt=mt, bank=bank: e.activation(
                    SXb[:, P_AO + mt * T + ns * 512:
                        P_AO + mt * T + ns * 512 + 512],
                    bank[:, 0:512], AF.Identity, bias=self.cv("b4", mt)),
                    wait=(tok,), inc=((self.s_act, 1),))
                self.bank_done(bank, self.s_act.tok())

            self.proj("w4", FT, FT, self.PP, evac_o, wdram=w4,
                      banks=self.pb[4:6], nb_range=(0,), src_off=8 * 512,
                      src_T=512, slot_range=(2, 3), first_wait=(retev,))
            self.bank_done(None, self.s_pe.tok(), "PPr")
        A2f = self.A2[:, 0:FT * T].bitcast(F32)
        A3f = self.A3[:, 0:FT * T].bitcast(F32)
        b.wait("sync", (self.s_pe.tok(),))
        b.dma("sync", A2f, self.xpark.ap()[:, 0:4 * T])
        b.dma("sync", A3f, self.xpark.ap()[:, 4 * T:8 * T])
        rl = b.sem("dma_sync").tok()
        ao_done = self.s_act.tok()
        for ft in range(FT):
            eng = ("vector", "gpsimd")[ft % 2]
            wt = (rl, ao_done, self.s_dve.tok(), self.s_pool.tok())
            b.emit(eng, lambda e, ft=ft: e.tensor_copy(
                self.SX[:, ft * T:ft * T + T],
                SXb[:, P_AO + ft * T:P_AO + ft * T + T]),
                wait=wt, inc=((self.esem(eng), 1),))
        cpt = (self.s_dve.tok(), self.s_pool.tok())
        for ft in range(FT):
            eng = ("vector", "gpsimd")[ft % 2]
            b.emit(eng, lambda e, ft=ft: e.tensor_add(
                self.SX[:, ft * T:ft * T + T],
                self.SX[:, ft * T:ft * T + T],
                (A2f if ft < 4 else A3f)[:, (ft % 4) * T:(ft % 4) * T + T]),
                wait=cpt, inc=((self.esem(eng), 1),))
        b.block()
        self.emit_ln("plain")

    def emit_l0_attn(self, upto=99):
        b = self.b
        b.phase_tag = "l0attn"
        T = self.T
        NB = self.NB
        NTT = self.NTT
        if not hasattr(self, "xpark"):
            self.xpark = self.b.nc.dram_tensor("xpark", [128, FT * T], F32,
                                               kind="Internal")
        pk = b.dma("sync", self.xpark.ap(), self.SX[:, 0:FT * T],
                   wait=getattr(self, "ln_done", ()))
        b.wait("sync", (pk,))
        b.block()
        P0 = self.PP[:, 0 * T:1 * T]
        P1 = self.PP[:, 1 * T:2 * T]
        P2 = self.PP[:, 2 * T:3 * T]
        P3 = self.PP[:, 3 * T:4 * T]
        P4 = self.PP[:, 4 * T:5 * T]
        gq = b.inp("gq", (NPAIR, 128, T))
        gk = b.inp("gk", (NPAIR, 128, T))
        for li in range(2):
            self.emit_ln("affine", "ln1_g%d" % li, "ln1_b%d" % li, dst=self.A1)
            if upto <= 1:
                return
            wq = b.inp("wq%d" % li, (FT, 128, FT, 128))
            wk = b.inp("wk%d" % li, (FT, 128, FT, 128))
            wv = b.inp("wv%d" % li, (FT, 128, FT, 128))
            wxo = b.inp("wxo%d" % li, (FT, 128, FT, 128))
            s_msk = b.sem("dma_mask")
            b.dma("sync",
                  self.PF[:, 0:2048].rearrange("p (o n) -> p o n", o=4),
                  b.inp("masks", (4, 128, 512)).ap().rearrange(
                      "o p n -> p o n"),
                  wait=(s_msk.tok(),) + self.ln_done
                  + self.bank_guard(None, "PFmask"),
                  sem=s_msk)
            mskf = s_msk.tok()
            self.bank_done(None, mskf, "PFmask")
            gq_tok = [None] * NPAIR
            gk_tok = [None] * NPAIR

            def issue_gq(p):
                s = b.sem("dma_gq")
                b.dma("gpsimd", P0, gq.ap()[p],
                      wait=self.bank_guard(P0, "P0"), sem=s)
                gq_tok[p] = s.tok()

            def issue_gk(p):
                s = b.sem("dma_gk")
                b.dma("gpsimd", P1, gk.ap()[p],
                      wait=self.bank_guard(P1, "P1"), sem=s)
                gk_tok[p] = s.tok()

            issue_gq(0)
            issue_gk(0)
            for p in range(NPAIR):

                def evac_q(mt, nb, bank, tok, li=li, p=p):
                    tm = self.PT[:, (nb % 2) * 512:(nb % 2) * 512 + 512]
                    b.emit("scalar", lambda e, bank=bank, tm=tm: e.activation(
                        tm, bank[:, 0:512], AF.Identity,
                        bias=self.cv("bq%d" % li, p)),
                        wait=(tok,) + self.bank_guard(None, ("sT", nb % 2)))
                    att = self.s_act.tok()
                    self.bank_done(bank, att)
                    b.emit("vector", lambda e, nb=nb, tm=tm: e.tensor_mul(
                        P2[:, nb * 512:nb * 512 + 512], tm,
                        P0[:, nb * 512:nb * 512 + 512]),
                        wait=(att, gq_tok[p]) + self.bank_guard(P2, "P2"))
                    self.bank_done(None, self.s_dve.tok(), ("sT", nb % 2))

                self.proj("", FT, FT, self.A1, evac_q, wdram=wq,
                          mt_range=(p,), slot_range=(0,), wsem="s_wld0",
                          first_wait=(self.ln_done if p == 0 else ()))
                self.bank_done(None, self.s_dve.tok(), "P2")

                def evac_k(mt, nb, bank, tok, li=li, p=p):
                    tm = self.PT[:, (nb % 2) * 512:(nb % 2) * 512 + 512]
                    b.emit("scalar", lambda e, bank=bank, tm=tm: e.activation(
                        tm, bank[:, 0:512], AF.Identity,
                        bias=self.cv("bk%d" % li, p)),
                        wait=(tok,) + self.bank_guard(None, ("sT", nb % 2)))
                    att = self.s_act.tok()
                    self.bank_done(bank, att)
                    b.emit("vector", lambda e, nb=nb, tm=tm: e.tensor_mul(
                        P3[:, nb * 512:nb * 512 + 512], tm,
                        P1[:, nb * 512:nb * 512 + 512]),
                        wait=(att, gk_tok[p]) + self.bank_guard(P3, "P3"))
                    self.bank_done(None, self.s_dve.tok(), ("sT", nb % 2))

                self.proj("", FT, FT, self.A1, evac_k, wdram=wk,
                          mt_range=(p,), slot_range=(1,), wsem="s_wld1")
                self.bank_done(None, self.s_dve.tok(), "P3")
                qk_done = self.s_dve.tok()

                def evac_v(mt, nb, bank, tok, li=li, p=p, qk=qk_done):
                    b.emit("scalar", lambda e, nb=nb, bank=bank: e.activation(
                        P0[:, nb * 512:nb * 512 + 512], bank[:, 0:512],
                        AF.Identity, bias=self.cv("bv%d" % li, p)),
                        wait=(tok, qk))
                    self.bank_done(bank, self.s_act.tok())

                self.proj("", FT, FT, self.A1, evac_v, wdram=wv,
                          mt_range=(p,), slot_range=(2,), wsem="s_wld2")
                vfm_done = self.s_act.tok()
                self.bank_done(None, vfm_done, "P0")

                def evac_x(mt, nb, bank, tok, li=li, p=p):
                    b.emit("scalar", lambda e, nb=nb, bank=bank: e.activation(
                        P4[:, nb * 512:nb * 512 + 512], bank[:, 0:512],
                        AF.Silu, bias=self.cv("bxo%d" % li, p)),
                        wait=(tok,) + self.bank_guard(P4, "P4"))
                    self.bank_done(bank, self.s_act.tok())

                self.proj("", FT, FT, self.A1, evac_x, wdram=wxo,
                          mt_range=(p,), slot_range=(3,), wsem="s_wld3")
                self.bank_done(None, self.s_act.tok(), "P4")
                if upto <= 2:
                    continue

                for tt in range(NTT):
                    pqv = self.pq[tt % 2][:, 0:128]
                    b.emit("tensor", lambda e, tt=tt, pqv=pqv: e.transpose(
                        pqv, P0[:, tt * 128:tt * 128 + 128], self.ID[:]),
                        wait=(vfm_done, qk_done)
                        + self.bank_guard(pqv, ("pq", tt % 2)),
                        inc=((self.s_pe, 1),))
                    ttk = self.s_pe.tok()
                    b.emit("vector", lambda e, tt=tt, pqv=pqv: e.tensor_copy(
                        P1[:, tt * 128:tt * 128 + 128], pqv), wait=(ttk,))
                    self.bank_done(pqv, self.s_dve.tok(), ("pq", tt % 2))
                vtm_done = self.s_dve.tok()
                self.bank_done(None, self.s_pe.tok(), "P0")
                if p + 1 < NPAIR:
                    issue_gq(p + 1)
                if upto <= 3:
                    continue
                qk_attn = (qk_done, self.s_act.tok())

                sT = [self.MB[:, i * 512:(i + 1) * 512] for i in range(4)]
                for nb in range(NB):
                    rb = [self.pb[4], self.pb[5]]
                    M = 4 * nb + 4
                    evtok = [[None, None] for _ in range(M)]

                    def emit_mm2(mc, rb=rb, nb=nb, M=M):
                        for h in range(2):
                            b.emit("tensor", lambda e, h=h, mc=mc, rb=rb:
                                   e.matmul(
                                rb[h][64 * h:64 * h + 64, 0:512],
                                P1[:, mc * 128 + 64 * h:
                                   mc * 128 + 64 * h + 64],
                                sT[2 * (mc % 2) + h],
                                start=(mc == 0), stop=(mc == M - 1)),
                                wait=(evtok[mc][h],)
                                + (self.bank_guard(rb[h])
                                   if mc == 0 else ()),
                                inc=((self.s_pe, 1),))
                        self.bank_done(None, self.s_pe.tok(),
                                       ("sTp", mc % 2))

                    for mc in range(M):
                        for h in range(2):
                            sb = self.pb[2 * (mc % 2) + h]
                            w0 = ()
                            if nb == 0 and mc == 0 and h == 0:
                                w0 = qk_attn + (vtm_done,)
                            b.emit("tensor", lambda e, h=h, mc=mc, nb=nb,
                                   sb=sb: e.matmul(
                                sb[:, 0:512],
                                P3[64 * h:64 * h + 64,
                                   mc * 128:mc * 128 + 128],
                                P2[64 * h:64 * h + 64,
                                   nb * 512:nb * 512 + 512],
                                start=True, stop=True),
                                wait=w0 + self.bank_guard(sb),
                                inc=((self.s_pe, 1),))
                        mmt = self.s_pe.tok()
                        if mc >= 2:
                            emit_mm2(mc - 2)
                        off = mc - 4 * nb
                        stg = self.bank_guard(None, ("sTp", mc % 2))
                        for h in range(2):
                            sb = self.pb[2 * (mc % 2) + h]
                            st = sT[2 * (mc % 2) + h]
                            if off < 0:
                                b.emit("scalar", lambda e, sb=sb, st=st:
                                       e.activation(st, sb[:, 0:512],
                                                    AF.Copy),
                                       wait=(mmt,) + (stg if h == 0 else ()))
                                self.bank_done(sb, self.s_act.tok())
                                evtok[mc][h] = self.s_act.tok()
                            else:
                                b.emit("vector", lambda e, sb=sb, st=st,
                                       off=off: e.tensor_mul(
                                    st, sb[:, 0:512],
                                    self.PF[:, off * 512:off * 512 + 512]),
                                    wait=(mmt, mskf)
                                    + (stg if h == 0 else ()))
                                self.bank_done(sb, self.s_dve.tok())
                                evtok[mc][h] = self.s_dve.tok()
                    for mc in range(max(0, M - 2), M):
                        emit_mm2(mc)
                    rt = self.s_pe.tok()
                    tm = self.PT[:, (nb % 2) * 512:(nb % 2) * 512 + 512]
                    b.emit("scalar", lambda e, tm=tm, rb=rb: e.activation(
                        tm[0:64, :], rb[0][0:64, 0:512], AF.Copy),
                        wait=(rt,) + self.bank_guard(None, ("sT", nb % 2)))
                    b.emit("scalar", lambda e, tm=tm, rb=rb: e.activation(
                        tm[64:128, :], rb[1][64:128, 0:512], AF.Copy))
                    att2 = self.s_act.tok()
                    self.bank_done(rb[0], att2)
                    self.bank_done(rb[1], att2)
                    b.emit("vector", lambda e, nb=nb, p=p, tm=tm: e.tensor_mul(
                        self.A2[:, p * T + nb * 512:p * T + nb * 512 + 512],
                        tm, P4[:, nb * 512:nb * 512 + 512]),
                        wait=(att2,))
                    self.bank_done(None, self.s_dve.tok(), ("sT", nb % 2))
                self.bank_done(None, self.s_pe.tok(), "P2")
                self.bank_done(None, self.s_pe.tok(), "P3")
                self.bank_done(None, self.s_pe.tok(), "P1")
                self.bank_done(None, self.s_dve.tok(), "P4")
                if p + 1 < NPAIR:
                    issue_gk(p + 1)
            if upto <= 4:
                return

            retg_done = self.s_dve.tok()

            def evac_r(mt, nb, bank, tok, li=li):
                b.emit("vector", lambda e, mt=mt, nb=nb, bank=bank:
                       e.scalar_tensor_tensor(
                    self.SX[:, mt * T + nb * 512:mt * T + nb * 512 + 512],
                    bank[:, 0:512], self.cv("reto_b%d" % li, mt),
                    self.SX[:, mt * T + nb * 512:mt * T + nb * 512 + 512],
                    ALU.add, ALU.add), wait=(tok,))
                self.bank_done(bank, self.s_dve.tok())

            self.proj("reto%d" % li, FT, FT, self.A2, evac_r,
                      first_wait=(retg_done,))
            if upto <= 5:
                return

            self.emit_ln("affine", "ln2_g%d" % li, "ln2_b%d" % li, dst=self.A1)
            f1 = b.inp("ffn1_%d" % li, (32, 128, FT, 128))
            f2 = b.inp("ffn2_%d" % li, (FT, 128, 32, 128))
            for cg in range(4):
                def evac_g(mt, nb, bank, tok, cg=cg, li=li):
                    c = mt - cg * 8
                    a2w = self.bank_guard(None, "A2ffn") \
                        if (c == 0 and nb == 0) else ()
                    b.emit("scalar", lambda e, c=c, nb=nb, bank=bank, mt=mt:
                           e.activation(
                        self.A2[:, c * T + nb * 512:c * T + nb * 512 + 512],
                        bank[:, 0:512], AF.Gelu,
                        bias=self.cv("ffn1_b%d" % li, mt)),
                        wait=(tok,) + a2w)
                    self.bank_done(bank, self.s_act.tok())

                self.proj("", 32, FT, self.A1, evac_g, wdram=f1,
                          mt_range=range(cg * 8, cg * 8 + 8),
                          first_wait=(self.ln_done if cg == 0 else ()))
                act_done = self.s_act.tok()
                wsz = 8 * 128
                slots = [self.WR[:, i * wsz:(i + 1) * wsz] for i in range(4)]
                s_w = b.sem("s_wld")

                def issue_f2(mt, cg=cg, slots=slots, s_w=s_w, f2=f2):
                    b.dma("gpsimd",
                          slots[mt % 4].rearrange("p (kt m) -> p kt m", kt=8),
                          f2.ap()[mt, :, cg * 8:cg * 8 + 8, :],
                          wait=self.bank_guard(None, ("WR", mt % 4)), sem=s_w)

                for mt in range(4):
                    issue_f2(mt)
                for mt in range(FT):
                    sl = slots[mt % 4]
                    wtk = s_w.tok()
                    for nb in range(NB):
                        bank = self.pb[4 + nb % 2]
                        for c in range(8):
                            wt = ()
                            if c == 0:
                                wt = (wtk, act_done) + self.bank_guard(bank)
                            b.emit("tensor", lambda e, c=c, nb=nb, sl=sl,
                                   bank=bank, st=(c == 0), sp=(c == 7):
                                   e.matmul(
                                bank[:, 0:512], sl[:, c * 128:c * 128 + 128],
                                self.A2[:, c * T + nb * 512:
                                        c * T + nb * 512 + 512],
                                start=st, stop=sp),
                                wait=wt, inc=((self.s_pe, 1),))
                        ptk = self.s_pe.tok()
                        if cg == 0:
                            b.emit("vector", lambda e, mt=mt, nb=nb, bank=bank,
                                   li=li: e.scalar_tensor_tensor(
                                self.SX[:, mt * T + nb * 512:
                                        mt * T + nb * 512 + 512],
                                bank[:, 0:512],
                                self.cv("ffn2_b%d" % li, mt),
                                self.SX[:, mt * T + nb * 512:
                                        mt * T + nb * 512 + 512],
                                ALU.add, ALU.add), wait=(ptk,))
                        else:
                            b.emit("vector", lambda e, mt=mt, nb=nb, bank=bank:
                                   e.tensor_add(
                                self.SX[:, mt * T + nb * 512:
                                        mt * T + nb * 512 + 512],
                                self.SX[:, mt * T + nb * 512:
                                        mt * T + nb * 512 + 512],
                                bank[:, 0:512]), wait=(ptk,))
                        self.bank_done(bank, self.s_dve.tok())
                    self.bank_done(None, self.s_pe.tok(), ("WR", mt % 4))
                    if mt + 4 < FT:
                        issue_f2(mt + 4)
                self.bank_done(None, self.s_pe.tok(), "A2ffn")
        self.emit_ln("affine", "lnf_g", "lnf_b", dst=self.A1)
        A2f = self.A2[:, 0:FT * T].bitcast(F32)
        A3f = self.A3[:, 0:FT * T].bitcast(F32)
        b.dma("sync", A2f, self.xpark.ap()[:, 0:4 * T], wait=self.ln_done)
        b.dma("sync", A3f, self.xpark.ap()[:, 4 * T:8 * T])
        rl = b.sem("dma_sync").tok()
        for ft in range(FT):
            eng = ("vector", "gpsimd")[ft % 2]
            b.emit(eng, lambda e, ft=ft: e.tensor_add(
                self.SX[:, ft * T:ft * T + T],
                (A2f if ft < 4 else A3f)[:, (ft % 4) * T:(ft % 4) * T + T],
                self.A1[:, ft * T:ft * T + T]),
                wait=(rl,) + self.ln_done)
        b.block()
        self.emit_ln("plain")


    def emit_l0_ffn(self):
        b = self.b
        b.phase_tag = "l0ffn"
        T = self.T
        NB = self.NB
        NTT = self.NTT
        wrt = self.sbufs_wrt if hasattr(self, "sbufs_wrt") else None
        if wrt is None:
            self.sbufs_wrt = b.sbuf("WRT", (128, 64), BF16)
            wrt = self.sbufs_wrt
        b.dma("gpsimd", wrt[:].rearrange("p (kt e) -> p kt e", kt=FT),
              b.inp("wrouter", (FT, 128, 8)).ap().rearrange("kt p e -> p kt e"))
        wrow = self.b.nc.dram_tensor("wrow", [8, T], BF16, kind="Internal")
        rt = b.sem("dma_gpsimd").tok()
        b.wait("gpsimd", (rt,))
        b.block()
        SS = self.SS
        for tt in range(NTT):
            bk = self.pb[4 + tt % 2]
            for kt in range(FT):
                w0 = self.ln_done if (tt == 0 and kt == 0) else ()
                b.emit("tensor", lambda e, tt=tt, kt=kt, bk=bk,
                       st=(kt == 0), sp=(kt == FT - 1): e.matmul(
                    bk[:, 0:8],
                    self.A1[:, kt * T + tt * 128:kt * T + tt * 128 + 128],
                    wrt[:, kt * 8:kt * 8 + 8], start=st, stop=sp),
                    wait=w0 + (self.bank_guard(bk) if kt == 0 else ()),
                    inc=((self.s_pe, 1),))
            pt = self.s_pe.tok()
            b.emit("vector", lambda e, bk=bk: e.tensor_copy(
                SS[:, 0:8], bk[:, 0:8]), wait=(pt,))
            self.bank_done(bk, self.s_dve.tok())
            b.emit("vector", lambda e: e.reduce_max(
                SS[:, 8:9], SS[:, 0:8], AX.X))
            b.emit("vector", lambda e: e.tensor_scalar_mul(
                SS[:, 9:10], SS[:, 8:9], -1.0))
            b.emit("vector", lambda e: e.tensor_scalar(
                SS[:, 16:24], SS[:, 0:8], SS[:, 8:9], None, ALU.is_equal))
            b.emit("vector", lambda e: e.scalar_tensor_tensor(
                SS[:, 24:32], SS[:, 16:24], -1e30, SS[:, 0:8],
                ALU.mult, ALU.add))
            b.emit("vector", lambda e: e.reduce_max(
                SS[:, 10:11], SS[:, 24:32], AX.X))
            dv = self.s_dve.tok()
            b.emit("scalar", lambda e: e.activation(
                SS[:, 11:12], SS[:, 10:11], AF.Exp, bias=SS[:, 9:10]),
                wait=(dv,))
            at = self.s_act.tok()
            b.emit("vector", lambda e: e.tensor_scalar_add(
                SS[:, 12:13], SS[:, 11:12], 1.0), wait=(at,))
            b.emit("vector", lambda e: e.reciprocal(
                SS[:, 13:14], SS[:, 12:13]))
            b.emit("vector", lambda e: e.tensor_scalar(
                SS[:, 32:40], SS[:, 0:8], SS[:, 10:11], None, ALU.is_ge))
            dv2 = self.s_dve.tok()
            b.emit("scalar", lambda e: e.activation(
                SS[:, 40:48], SS[:, 0:8], AF.Exp, bias=SS[:, 9:10]),
                wait=(dv2,))
            at2 = self.s_act.tok()
            b.emit("vector", lambda e: e.scalar_tensor_tensor(
                SS[:, 48:56], SS[:, 40:48], SS[:, 13:14], SS[:, 32:40],
                ALU.mult, ALU.mult), wait=(at2,))
            dv3 = self.s_dve.tok()
            b.emit("scalar", lambda e: e.activation(
                self.PT[:, 896:904], SS[:, 48:56], AF.Copy), wait=(dv3,))
            at3 = self.s_act.tok()
            b.emit("tensor", lambda e, tt=tt: e.transpose(
                self.pq[0][0:8, 0:128], self.PT[:, 896:904], self.ID[:]),
                wait=(at3,) + self.bank_guard(None, "pqw"),
                inc=((self.s_pe, 1),))
            pt2 = self.s_pe.tok()
            b.emit("vector", lambda e: e.tensor_copy(
                self.PT[0:8, 512:640], self.pq[0][0:8, 0:128]),
                wait=(pt2,) + self.bank_guard(None, "wstg"))
            self.bank_done(None, self.s_dve.tok(), "pqw")
            dv4 = self.s_dve.tok()
            b.dma("sync", wrow.ap()[:, tt * 128:tt * 128 + 128],
                  self.PT[0:8, 512:640], wait=(dv4,))
            self.bank_done(None, b.sem("dma_sync").tok(), "wstg")
        b.wait("sync", (b.sem("dma_sync").tok(),))
        b.block()

        gw = b.inp("gatew", (16, 128, FT, 128))
        uw = b.inp("upw", (16, 128, FT, 128))
        dw = b.inp("downw", (FT, 128, 16, 128))
        s_w = b.sem("s_wld")
        gsl = [self.WR[:, i * 1024:(i + 1) * 1024] for i in range(2)]
        usl = [self.WR[:, 2048 + i * 1024:2048 + (i + 1) * 1024]
               for i in range(2)]
        for cg in range(2):
            def issue_gu(ci, cg=cg):
                c = cg * 8 + ci
                b.dma("gpsimd",
                      gsl[ci % 2].rearrange("p (kt m) -> p kt m", kt=FT),
                      gw.ap()[c],
                      wait=self.bank_guard(None, ("WR", ci % 2)), sem=s_w)
                b.dma("gpsimd",
                      usl[ci % 2].rearrange("p (kt m) -> p kt m", kt=FT),
                      uw.ap()[c],
                      wait=self.bank_guard(None, ("WR", 2 + ci % 2)), sem=s_w)

            issue_gu(0)
            issue_gu(1)
            for ci in range(8):
                wtk = s_w.tok()
                for nb in range(NB):
                    bg = self.pb[0 + nb % 2]
                    bu = self.pb[2 + nb % 2]
                    for kt in range(FT):
                        wt1 = ((wtk,) + self.bank_guard(bg)) if kt == 0 else ()
                        wt2 = ((wtk,) + self.bank_guard(bu)) if kt == 0 else ()
                        b.emit("tensor", lambda e, kt=kt, nb=nb, bg=bg,
                               sl=gsl[ci % 2], st=(kt == 0),
                               sp=(kt == FT - 1): e.matmul(
                            bg[:, 0:512], sl[:, kt * 128:kt * 128 + 128],
                            self.A1[:, kt * T + nb * 512:kt * T + nb * 512 + 512],
                            start=st, stop=sp), wait=wt1,
                            inc=((self.s_pe, 1),))
                        b.emit("tensor", lambda e, kt=kt, nb=nb, bu=bu,
                               sl=usl[ci % 2], st=(kt == 0),
                               sp=(kt == FT - 1): e.matmul(
                            bu[:, 0:512], sl[:, kt * 128:kt * 128 + 128],
                            self.A1[:, kt * T + nb * 512:kt * T + nb * 512 + 512],
                            start=st, stop=sp), wait=wt2,
                            inc=((self.s_pe, 1),))
                    ptk = self.s_pe.tok()
                    st_sl = self.PT[:, (nb % 2) * 512:(nb % 2) * 512 + 512]
                    sun = self.A2[:, ci * T + nb * 512:ci * T + nb * 512 + 512]
                    b.emit("scalar", lambda e, bg=bg, sun=sun: e.activation(
                        sun, bg[:, 0:512], AF.Silu), wait=(ptk,))
                    self.bank_done(bg, self.s_act.tok())
                    b.emit("scalar", lambda e, bu=bu, st_sl=st_sl: e.activation(
                        st_sl, bu[:, 0:512], AF.Copy),
                        wait=self.bank_guard(None, ("sst", nb % 2)))
                    self.bank_done(bu, self.s_act.tok())
                    at = self.s_act.tok()
                    b.emit("vector", lambda e, st_sl=st_sl, sun=sun:
                           e.tensor_mul(sun, sun, st_sl), wait=(at,))
                    self.bank_done(None, self.s_dve.tok(), ("sst", nb % 2))
                self.bank_done(None, self.s_pe.tok(), ("WR", ci % 2))
                self.bank_done(None, self.s_pe.tok(), ("WR", 2 + ci % 2))
                if ci + 2 < 8:
                    issue_gu(ci + 2)
            su_done = self.s_dve.tok()
            dsl = [self.WR[:, i * 1024:(i + 1) * 1024] for i in range(4)]

            def issue_d(mt, cg=cg, dsl=dsl, s_w=s_w, dw=dw):
                b.dma("gpsimd",
                      dsl[mt % 4].rearrange("p (kt m) -> p kt m", kt=8),
                      dw.ap()[mt, :, cg * 8:cg * 8 + 8, :],
                      wait=self.bank_guard(None, ("WR", mt % 4)), sem=s_w)

            b.wait("gpsimd", (self.s_pe.tok(),))
            for mt in range(4):
                issue_d(mt)
            for mt in range(FT):
                sl = dsl[mt % 4]
                wtk = s_w.tok()
                for nb in range(NB):
                    bank = self.pb[4 + nb % 2]
                    for c in range(8):
                        wt = ()
                        if c == 0:
                            wt = (wtk, su_done) + self.bank_guard(bank)
                        b.emit("tensor", lambda e, c=c, nb=nb, sl=sl, bank=bank,
                               st=(c == 0), sp=(c == 7): e.matmul(
                            bank[:, 0:512], sl[:, c * 128:c * 128 + 128],
                            self.A2[:, c * T + nb * 512:c * T + nb * 512 + 512],
                            start=st, stop=sp), wait=wt,
                            inc=((self.s_pe, 1),))
                    ptk = self.s_pe.tok()
                    b.emit("vector", lambda e, mt=mt, nb=nb, bank=bank:
                           e.tensor_add(
                        self.SX[:, mt * T + nb * 512:mt * T + nb * 512 + 512],
                        self.SX[:, mt * T + nb * 512:mt * T + nb * 512 + 512],
                        bank[:, 0:512]), wait=(ptk,))
                    self.bank_done(bank, self.s_dve.tok())
                self.bank_done(None, self.s_pe.tok(), ("WR", mt % 4))
                if mt + 4 < FT:
                    issue_d(mt + 4)
            b.block()

        fc1 = b.inp("fc1", (8, 128, 16384))
        fc2 = b.inp("fc2", (8, FT, 128, FT, 128))
        s_f = b.sem("dma_gpsimd")
        gtmp = self.PP[:, 2048:2048 + NB * 512]

        def wb_tiles(ex):
            ep = ex % 2
            return [self.PP[:, ep * 4096 + i * 512:ep * 4096 + (i + 1) * 512]
                    for i in range(NB)]

        def emit_wb(ex):
            ep = ex % 2
            wbt_t = wb_tiles(ex)
            s_wr = b.sem("dma_wrow%d" % ep)
            for rnd in range(0, NB, 2):
                js = [j for j in (0, 1) if rnd + j < NB]
                for j in js:
                    b.dma("sync", self.MB[0:1, (ep * 2 + j) * 512:
                                          (ep * 2 + j) * 512 + 512],
                          wrow.ap()[ex:ex + 1, (rnd + j) * 512:
                                    (rnd + j) * 512 + 512],
                          wait=self.bank_guard(None, ("wbr", ep, j)),
                          sem=s_wr)
                wbt = s_wr.tok()
                for j in js:
                    nb = rnd + j
                    bk = self.pb[4 + j]
                    b.emit("tensor", lambda e, j=j, bk=bk, ep=ep: e.matmul(
                        bk[:, 0:512], self.O1[:],
                        self.MB[0:1, (ep * 2 + j) * 512:
                                (ep * 2 + j) * 512 + 512],
                        start=True, stop=True),
                        wait=(wbt,) + self.bank_guard(bk),
                        inc=((self.s_pe, 1),))
                    ptk = self.s_pe.tok()
                    b.emit("scalar", lambda e, nb=nb, bk=bk, w=wbt_t[nb]:
                           e.activation(w, bk[:, 0:512], AF.Copy),
                        wait=(ptk,) + self.bank_guard(None, ("wbt", ep, nb)))
                    self.bank_done(bk, self.s_act.tok())
                    self.bank_done(None, self.s_act.tok(), ("wbr", ep, j))
            return self.s_act.tok()

        wb_done = emit_wb(0)
        for ex in range(8):
            wb_t = wb_tiles(ex)
            b.dma("gpsimd", self.A3[:, 0:16384], fc1.ap()[ex],
                  wait=(s_f.tok(),) + self.bank_guard(None, "A3fc1"),
                  sem=s_f)
            ftk = s_f.tok()
            a2g = self.bank_guard(None, "A2act")
            for i in range(8):
                ptoks = [None] * NB
                for nb in range(NB):
                    bank = self.pb[nb % 4]
                    for kt in range(FT):
                        wt = ()
                        if kt == 0:
                            wt = (ftk,) + self.bank_guard(bank)
                        b.emit("tensor", lambda e, i=i, kt=kt, nb=nb,
                               bank=bank, st=(kt == 0), sp=(kt == FT - 1):
                               e.matmul(
                            bank[:, 0:512],
                            self.A3[:, (i * FT + kt) * 128:
                                    (i * FT + kt) * 128 + 128],
                            self.A1[:, kt * T + nb * 512:
                                    kt * T + nb * 512 + 512],
                            start=st, stop=sp), wait=wt,
                            inc=((self.s_pe, 1),))
                    ptk = self.s_pe.tok()
                    b.emit("scalar", lambda e, i=i, nb=nb, bank=bank:
                           e.activation(
                        self.A2[:, i * 2048 + nb * 512:
                                i * 2048 + nb * 512 + 512],
                        bank[:, 0:512], AF.Silu),
                        wait=(ptk,) + (a2g if i == 0 and nb == 0 else ()))
                    self.bank_done(bank, self.s_act.tok())
                for nb in range(NB):
                    bank = self.pb[nb % 4]
                    for kt in range(FT):
                        wt = self.bank_guard(bank) if kt == 0 else ()
                        b.emit("tensor", lambda e, i=i, kt=kt, nb=nb,
                               bank=bank, st=(kt == 0), sp=(kt == FT - 1):
                               e.matmul(
                            bank[:, 0:512],
                            self.A3[:, ((i + 8) * FT + kt) * 128:
                                    ((i + 8) * FT + kt) * 128 + 128],
                            self.A1[:, kt * T + nb * 512:
                                    kt * T + nb * 512 + 512],
                            start=st, stop=sp), wait=wt,
                            inc=((self.s_pe, 1),))
                    ptk = self.s_pe.tok()
                    gsl_ = gtmp[:, nb * 512:nb * 512 + 512]
                    b.emit("scalar", lambda e, g=gsl_, bank=bank: e.activation(
                        g, bank[:, 0:512], AF.Copy),
                        wait=(ptk,) + self.bank_guard(None, ("gt", nb)))
                    self.bank_done(bank, self.s_act.tok())
                    at4 = self.s_act.tok()
                    aslot = self.A2[:, i * 2048 + nb * 512:
                                    i * 2048 + nb * 512 + 512]
                    b.emit("vector", lambda e, g=gsl_, aslot=aslot:
                           e.tensor_mul(aslot, aslot, g), wait=(at4,))
                    self.bank_done(None, self.s_dve.tok(), ("gt", nb))
                    b.emit("gpsimd", lambda e, w=wb_t[nb], aslot=aslot:
                           e.tensor_mul(aslot, aslot, w),
                           wait=(self.s_dve.tok(), wb_done))
            act_done = self.s_pool.tok()
            last_fc1 = self.s_pe.tok()
            self.bank_done(None, last_fc1, "A3fc1")
            for nb in range(NB):
                self.bank_done(None, act_done, ("wbt", ex % 2, nb))
            if ex + 1 < 8:
                wb_done = emit_wb(ex + 1)
            fsl = [self.WR[:, i * 1024:(i + 1) * 1024] for i in range(4)]

            def issue_fc2(dt, ex=ex, fsl=fsl, s_f=s_f, fc2=fc2):
                if dt % 2 == 1:
                    return
                w = (self.bank_guard(None, ("WR", dt % 4))
                     + self.bank_guard(None, ("WR", dt % 4 + 1)))
                b.dma("gpsimd",
                      self.WR[:, (dt % 4) * 1024:
                              (dt % 4 + 2) * 1024].rearrange(
                          "p (t kt m) -> p t kt m", t=2, kt=FT),
                      fc2.ap()[ex, dt:dt + 2].rearrange(
                          "t p kt m -> p t kt m"),
                      wait=w, sem=s_f)

            for dt in range(4):
                issue_fc2(dt)
            fnxt = 4
            for dt in range(FT):
                sl = fsl[dt % 4]
                wtk = s_f.tok()
                for nb in range(NB):
                    bank = self.pb[4 + nb % 2]
                    for it in range(FT):
                        wt = ()
                        if it == 0:
                            wt = (wtk, act_done) + self.bank_guard(bank)
                        b.emit("tensor", lambda e, it=it, nb=nb, sl=sl,
                               bank=bank, st=(it == 0), sp=(it == FT - 1):
                               e.matmul(
                            bank[:, 0:512], sl[:, it * 128:it * 128 + 128],
                            self.A2[:, it * 2048 + nb * 512:
                                    it * 2048 + nb * 512 + 512],
                            start=st, stop=sp), wait=wt,
                            inc=((self.s_pe, 1),))
                    ptk = self.s_pe.tok()
                    b.emit("vector", lambda e, dt=dt, nb=nb, bank=bank:
                           e.tensor_add(
                        self.SX[:, dt * T + nb * 512:dt * T + nb * 512 + 512],
                        self.SX[:, dt * T + nb * 512:dt * T + nb * 512 + 512],
                        bank[:, 0:512]), wait=(ptk,))
                    self.bank_done(bank, self.s_dve.tok())
                self.bank_done(None, self.s_pe.tok(), ("WR", dt % 4))
                while fnxt < FT and fnxt <= dt + 3:
                    issue_fc2(fnxt)
                    fnxt += 1
            self.bank_done(None, (self.s_pe.tok(), self.s_dve.tok(),
                                  self.s_pool.tok()), "A2act")
        self.emit_ln("plain")

    def emit_l1_ffn(self):
        b = self.b
        b.phase_tag = "l1ffn"
        T = self.T
        NB = self.NB
        mcd = self.b.nc.dram_tensor("mcdram", [16, 16384], BF16,
                                    kind="Internal")
        b.dma("gpsimd", self.PP[0:16, 0:4096].rearrange(
            "p (hr i) -> p hr i", hr=256),
            b.inp("makerT", (16, 4, 64, 16)).ap().rearrange(
                "j h r i -> j (h r) i"))
        b.dma("gpsimd", self.A2[0:16, 0:16384].rearrange(
            "p (hr k) -> p hr k", hr=256),
            b.inp("upT", (16, 4, 64, 64)).ap().rearrange(
                "j h r k -> j (h r) k"))
        mtk = b.sem("dma_gpsimd").tok()
        b.wait("gpsimd", (mtk,))
        b.block()
        for hr in range(256):
            bk = self.pb[hr % 2]
            b.emit("tensor", lambda e, hr=hr, bk=bk: e.matmul(
                bk[0:16, 0:64], self.PP[0:16, hr * 16:hr * 16 + 16],
                self.A2[0:16, hr * 64:hr * 64 + 64], start=True, stop=True),
                wait=self.bank_guard(bk), inc=((self.s_pe, 1),))
            ptk = self.s_pe.tok()
            b.emit("scalar", lambda e, hr=hr, bk=bk: e.activation(
                self.A3[0:16, hr * 64:hr * 64 + 64], bk[0:16, 0:64], AF.Copy),
                wait=(ptk,))
            self.bank_done(bk, self.s_act.tok())
        stg = self.s_act.tok()
        b.dma("sync", mcd.ap(), self.A3[0:16, 0:16384], wait=(stg,))
        b.wait("sync", (b.sem("dma_sync").tok(),))
        b.block()
        b.emit("gpsimd", lambda e: e.memset(self.A3[:, 0:16384], 0))
        b.wait("gpsimd", (self.s_pool.tok(),))
        b.block()
        mcv = mcd.ap().rearrange("i (h t a par k) -> i h t a par k",
                                 h=4, t=8, a=4, par=2)
        for a in range(4):
            for par in range(2):
                dst = self.A3[32 * a + 16 * par:32 * a + 16 * par + 16,
                              0:16384].rearrange(
                    "i (h t aa k) -> i h t aa k", h=4, t=8, aa=4)[
                        :, :, :, a, 64 * par:64 * par + 64]
                b.dma("sync", dst, mcv[:, :, :, a, par, :])
        b.wait("sync", (b.sem("dma_sync").tok(),))
        b.block()
        wc = b.inp("wcT", (16, FT, 128, 8, 128))
        s_w = b.sem("s_wld")
        csl = [self.WR[:, i * 1024:(i + 1) * 1024] for i in range(4)] + \
            [self.PP[:, i * 1024:(i + 1) * 1024] for i in range(8)]
        NCS = len(csl)
        for cg in range(16):
            mc_guard = self.bank_guard(None, "A2h")
            for ci in range(8):
                c = cg * 8 + ci
                pt_x = (c % 32) // 4
                for nb in range(NB):
                    bk = self.pb[2 + (ci * NB + nb) % 4]
                    w0 = self.ln_done if (cg == 0 and ci == 0 and nb == 0) \
                        else ()
                    b.emit("tensor", lambda e, c=c, nb=nb, bk=bk, pt_x=pt_x:
                           e.matmul(
                        bk[:, 0:512], self.A3[:, c * 128:c * 128 + 128],
                        self.A1[:, pt_x * T + nb * 512:pt_x * T + nb * 512 + 512],
                        start=True, stop=True),
                        wait=w0 + self.bank_guard(bk),
                        inc=((self.s_pe, 1),))
                    ptk = self.s_pe.tok()
                    b.emit("scalar", lambda e, ci=ci, nb=nb, bk=bk:
                           e.activation(
                        self.A2[:, ci * 2048 + nb * 512:
                                ci * 2048 + nb * 512 + 512],
                        bk[:, 0:512], AF.Gelu),
                        wait=(ptk,) + (mc_guard if ci == 0 and nb == 0
                                       else ()))
                    self.bank_done(bk, self.s_act.tok())
            h_done = self.s_act.tok()
            def issue_c(gi, cg=cg, s_w=s_w, wc=wc, csl=csl):
                if gi % 2 == 1:
                    return
                sidx = gi % NCS
                w = (self.bank_guard(None, ("cs", sidx))
                     + self.bank_guard(None, ("cs", sidx + 1)))
                if sidx < 4:
                    dst = self.WR[:, sidx * 1024:(sidx + 2) * 1024]
                else:
                    dst = self.PP[:, (sidx - 4) * 1024:(sidx - 2) * 1024]
                b.dma("gpsimd",
                      dst.rearrange("p (t kt m) -> p t kt m", t=2, kt=8),
                      wc.ap()[gi // FT, gi % FT:gi % FT + 2].rearrange(
                          "t p kt m -> p t kt m"),
                      wait=w, sem=s_w)

            if cg == 0:
                for gi in range(NCS):
                    issue_c(gi)
                self._cnxt = NCS
            for dt in range(FT):
                gi = cg * FT + dt
                sl = csl[gi % NCS]
                wtk = s_w.tok()
                for nb in range(NB):
                    bank = self.pb[(dt * NB + nb) % 2]
                    for ci in range(8):
                        wt = ()
                        if ci == 0:
                            wt = (wtk, h_done) + self.bank_guard(bank)
                        b.emit("tensor", lambda e, ci=ci, nb=nb, sl=sl,
                               bank=bank, st=(ci == 0), sp=(ci == 7):
                               e.matmul(
                            bank[:, 0:512], sl[:, ci * 128:ci * 128 + 128],
                            self.A2[:, ci * 2048 + nb * 512:
                                    ci * 2048 + nb * 512 + 512],
                            start=st, stop=sp), wait=wt,
                            inc=((self.s_pe, 1),))
                    ptk = self.s_pe.tok()
                    b.emit("vector", lambda e, dt=dt, nb=nb, bank=bank:
                           e.tensor_add(
                        self.SX[:, dt * T + nb * 512:dt * T + nb * 512 + 512],
                        self.SX[:, dt * T + nb * 512:dt * T + nb * 512 + 512],
                        bank[:, 0:512]), wait=(ptk,))
                    self.bank_done(bank, self.s_dve.tok())
                self.bank_done(None, self.s_pe.tok(), ("cs", gi % NCS))
                while self._cnxt < 16 * FT and self._cnxt <= gi + NCS - 1:
                    issue_c(self._cnxt)
                    self._cnxt += 1
            self.bank_done(None, self.s_pe.tok(), "A2h")
        self.emit_ln("plain")


_CACHE = {}


def _build(phases, T, colidx, ncol):
    key = (phases, T)
    if key in _CACHE:
        return _CACHE[key]
    nc = bass.Bass("TRN2", target_bir_lowering=False)
    mk = MK(nc, T, colidx)
    mk.emit_init(ncol)
    for ph in phases:
        if ph == "nop":
            pass
        elif ph == "cast":
            mk.emit_cast(mk.A1)
        elif ph == "ln":
            mk.emit_ln("plain")
        elif ph.startswith("l0u"):
            mk.emit_l0_attn(upto=int(ph[3:]))
        elif ph == "l1attn":
            mk.emit_l1_attn()
        elif ph == "l0attn":
            mk.emit_l0_attn()
        elif ph == "l0ffn":
            mk.emit_l0_ffn()
        elif ph == "l1ffn":
            mk.emit_l1_ffn()
        else:
            raise ValueError(ph)
    mk.emit_store()
    mk.b.finalize()
    names = sorted(mk.b.inputs.keys())
    _CACHE[key] = (nc, names)
    return nc, names


def _run_phases(phase_groups, host, x_fm, T):
    ncol = host["cvec"].shape[1]
    for phases in phase_groups:
        nc, names = _build(tuple(phases), T, host["_colidx"], ncol)
        in_maps = []
        for c in range(NCORES):
            m = {}
            for nname in names:
                m[nname] = x_fm[c] if nname == "xin" else host[nname]
            in_maps.append(m)
        res = run_bass_kernel_spmd(nc, in_maps, core_ids=list(range(NCORES)))
        x_fm = [np.asarray(res.results[c]["xout"]) for c in range(NCORES)]
    return x_fm


def kernel(**inputs):
    T = 2048
    host = prepare_host(inputs, T)
    x = np.asarray(inputs["L0_attn_x"], np.float32)
    x_fm = [np.ascontiguousarray(x[c].T) for c in range(NCORES)]
    groups = [["l0attn", "l0ffn", "l1attn", "l1ffn"]]
    x_fm = _run_phases(groups, host, x_fm, T)
    out = np.stack([x_fm[c].T for c in range(NCORES)], axis=0)
    return np.ascontiguousarray(out)


# version 27
# speedup vs baseline: 1.0520x; 1.0063x over previous
import contextlib

import numpy as np

import concourse.bass as bass
import concourse.mybir as mybir
from concourse.bass_utils import run_bass_kernel_spmd

F32 = mybir.dt.float32
BF16 = mybir.dt.bfloat16
AF = mybir.ActivationFunctionType
ALU = mybir.AluOpType
AX = mybir.AxisListType

NCORES = 8
D = 1024
FT = D // 128
NH = 16
HD = 64
NPAIR = 8

ENGS = ("tensor", "vector", "scalar", "gpsimd", "sync")


class Sem:
    __slots__ = ("h", "n", "name")

    def __init__(self, h, name):
        self.h = h
        self.n = 0
        self.name = name

    def tok(self):
        return (self, self.n)


class _Blk:
    def __init__(self):
        self.ops = {e: [] for e in ENGS}
        self.tag = None


class B:

    def __init__(self, nc, T):
        self.nc = nc
        self.T = T
        self.blocks = []
        self.cur = None
        self.stack = contextlib.ExitStack()
        self._sems = {}
        self.inputs = {}
        self.block()

    def sem(self, name):
        s = self._sems.get(name)
        if s is None:
            h = self.stack.enter_context(self.nc.semaphore(name))
            s = Sem(h, name)
            self._sems[name] = s
        return s

    def block(self, tag=None):
        self.cur = _Blk()
        self.cur.tag = tag or getattr(self, "phase_tag", None)
        self.blocks.append(self.cur)

    def esem(self, eng):
        return self.sem("s_" + eng)

    def emit(self, eng, fn, wait=(), inc=(), is_dma=False):
        wait = tuple(wait)
        if eng in ("vector", "scalar", "gpsimd"):
            es = self.esem(eng)
            wait = ((es, es.n),) + wait
            if not is_dma:
                inc = ((es, 1),) + tuple((s, k) for s, k in inc if s is not es)
        assert len(inc) <= 1, "one sem update per instruction"
        for s, k in inc:
            s.n += k
        self.cur.ops[eng].append((fn, tuple(wait), tuple(inc)))

    def wait(self, eng, toks):
        self.cur.ops[eng].append((None, tuple(toks), ()))

    def drain_dmas(self):
        for name, s in self._sems.items():
            if name.startswith("dma_") or name.startswith("s_wld"):
                q = name[4:] if name.startswith("dma_") else "gpsimd"
                if q not in ENGS:
                    q = "sync"
                self.wait(q, (s.tok(),))

    def dma(self, queue, dst, src, wait=(), sem=None):
        s = sem or self.sem("dma_" + queue)
        self.emit(queue, lambda e, d=dst, sr=src: e.dma_start(d, sr),
                  wait=wait, inc=((s, 16),), is_dma=True)
        return s.tok()

    def sbuf(self, name, shape, dt):
        return self.stack.enter_context(self.nc.sbuf_tensor(name, list(shape), dt))

    def psum(self, name, shape, dt):
        return self.stack.enter_context(self.nc.psum_tensor(name, list(shape), dt))

    def inp(self, name, shape, dt=F32):
        if name not in self.inputs:
            self.inputs[name] = self.nc.dram_tensor(name, list(shape), dt,
                                                    kind="ExternalInput")
        return self.inputs[name]

    def finalize(self):
        import contextlib as _ctx
        for bi, blk in enumerate(self.blocks):
            if not any(blk.ops[e] for e in ENGS):
                continue
            scope = (self.nc.named_scope("%s_%d" % (blk.tag, bi))
                     if blk.tag else _ctx.nullcontext())
            with scope, self.nc.Block() as bb:
                for ename in ENGS:
                    entries = blk.ops[ename]
                    if not entries:
                        continue

                    def replay(eng, entries=entries):
                        for fn, waits, incs in entries:
                            for s, v in waits:
                                eng.wait_ge(s.h, v)
                            if fn is None:
                                continue
                            inst = fn(eng)
                            for s, k in incs:
                                inst.then_inc(s.h, k)

                    getattr(bb, ename)(replay)
        self.stack.close()


def _tiles_mk(w):
    K, M = w.shape
    return np.ascontiguousarray(
        w.reshape(K // 128, 128, M // 128, 128).transpose(2, 1, 0, 3))


def prepare_host(inputs, T):
    h = {}
    f32 = np.float32

    n = np.arange(T, dtype=np.float64)
    g = 1.0 - 2.0 ** (-5.0 - np.arange(NH, dtype=np.float64))
    gq = np.zeros((NPAIR, 128, T), f32)
    gk = np.zeros((NPAIR, 128, T), f32)
    for p in range(NPAIR):
        for j, hh in enumerate((2 * p, 2 * p + 1)):
            gq[p, 64 * j:64 * j + 64, :] = (g[hh] ** n)[None, :].astype(f32)
            gk[p, 64 * j:64 * j + 64, :] = (g[hh] ** (-n))[None, :].astype(f32)
    h["gq"] = gq
    h["gk"] = gk
    masks = np.zeros((4, 128, 512), f32)
    ii = np.arange(128)[:, None]
    jj = np.arange(512)[None, :]
    for oi, off in enumerate((0, 128, 256, 384)):
        masks[oi] = (jj >= ii + off).astype(f32)
    h["masks"] = masks
    h["ident"] = np.eye(128, dtype=f32)
    h["ones1"] = np.ones((1, 128), f32)
    h["onesm"] = np.full((128, 1), 1.0 / D, f32)

    cols = []
    colidx = {}

    def addvec(name, v):
        v = np.asarray(v, f32).reshape(-1, 128)
        colidx[name] = len(cols)
        for t in range(v.shape[0]):
            cols.append(v[t])

    for i in range(2):
        pb = inputs["L0_attn_proj_b"][i]
        addvec(f"bq{i}", pb[0:1024])
        addvec(f"bk{i}", pb[1024:2048])
        addvec(f"bv{i}", pb[2048:3072])
        addvec(f"bxo{i}", pb[3072:4096])
        addvec(f"reto_b{i}", inputs["L0_attn_reto_b"][i])
        addvec(f"ffn1_b{i}", inputs["L0_attn_ffn1_b"][i])
        addvec(f"ffn2_b{i}", inputs["L0_attn_ffn2_b"][i])
        addvec(f"ln1_g{i}", inputs["L0_attn_ln1_g"][i])
        addvec(f"ln1_b{i}", inputs["L0_attn_ln1_b"][i])
        addvec(f"ln2_g{i}", inputs["L0_attn_ln2_g"][i])
        addvec(f"ln2_b{i}", inputs["L0_attn_ln2_b"][i])
    addvec("lnf_g", inputs["L0_attn_lnf_g"])
    addvec("lnf_b", inputs["L0_attn_lnf_b"])
    addvec("b1", inputs["L1_attn_b1"])
    addvec("b2", inputs["L1_attn_b2"])
    addvec("b4", inputs["L1_attn_b4"])
    h["cvec"] = np.ascontiguousarray(np.stack(cols, axis=1))
    h["_colidx"] = colidx
    h["b3row"] = np.asarray(inputs["L1_attn_b3"], f32).reshape(1, D)

    for i in range(2):
        pw = inputs["L0_attn_proj_w"][i]
        h[f"wq{i}"] = _tiles_mk(pw[:, 0:1024])
        h[f"wk{i}"] = _tiles_mk(pw[:, 1024:2048])
        h[f"wv{i}"] = _tiles_mk(pw[:, 2048:3072])
        h[f"wxo{i}"] = _tiles_mk(pw[:, 3072:4096])
        h[f"reto{i}"] = _tiles_mk(inputs["L0_attn_reto_w"][i])
        h[f"ffn1_{i}"] = _tiles_mk(inputs["L0_attn_ffn1_w"][i])
        h[f"ffn2_{i}"] = _tiles_mk(inputs["L0_attn_ffn2_w"][i])

    h["wrouter"] = np.ascontiguousarray(
        inputs["L0_ffn_w_router"].reshape(FT, 128, 8))
    h["fc1"] = np.stack([_tiles_mk(inputs["L0_ffn_fc1_w"][e]).transpose(
        1, 0, 2, 3).reshape(128, 16384) for e in range(8)])
    h["fc2"] = np.stack([_tiles_mk(inputs["L0_ffn_fc2_w"][e]) for e in range(8)])
    h["gatew"] = _tiles_mk(inputs["L0_ffn_gate_w"])
    h["upw"] = _tiles_mk(inputs["L0_ffn_up_w"])
    h["downw"] = _tiles_mk(inputs["L0_ffn_down_w"])

    h["w1"] = _tiles_mk(inputs["L1_attn_W1"])
    h["w2"] = _tiles_mk(inputs["L1_attn_W2"])
    h["w3"] = np.ascontiguousarray(inputs["L1_attn_W3"].reshape(FT, 128, D))
    h["w4"] = _tiles_mk(inputs["L1_attn_W4"])

    h["makerT"] = np.ascontiguousarray(
        inputs["L1_ffn_rail_maker"].transpose(3, 0, 1, 2))
    h["upT"] = np.ascontiguousarray(
        inputs["L1_ffn_rail_up_proj"].transpose(3, 0, 1, 2))
    wct = np.ascontiguousarray(inputs["L1_ffn_w_cproj"].T)
    wct = wct.reshape(16, 8, 128, 8, 128).transpose(0, 3, 2, 1, 4)
    h["wcT"] = np.ascontiguousarray(wct)
    return h


class MK:
    def __init__(self, nc, T, colidx):
        self.b = B(nc, T)
        b = self.b
        self.T = T
        self.NB = T // 512
        self.NTT = T // 128
        self.colidx = colidx
        ncol = 320
        self.SX = b.sbuf("SX", (128, FT * T), F32)
        self.A1 = b.sbuf("A1", (128, max(FT * T, 16384)), BF16)
        self.A2 = b.sbuf("A2", (128, max(FT * T, 16384)), BF16)
        self.A3 = b.sbuf("A3", (128, max(FT * T, 16384)), BF16)
        self.PP = b.sbuf("PP", (128, max(5 * T, 8192)), BF16)
        self.PT = b.sbuf("PT", (128, 1024), BF16)
        self.WR = b.sbuf("WR", (128, 4096), BF16)
        self.CB = b.sbuf("CB", (128, ncol), F32)
        self.MB = b.sbuf("MB", (128, 4 * 512), BF16)
        self.ID = b.sbuf("ID", (128, 128), BF16)
        self.O1 = b.sbuf("O1", (1, 128), BF16)
        self.OM = b.sbuf("OM", (128, 1), BF16)
        self.SS = b.sbuf("SS", (128, 64), F32)
        self.PF = b.sbuf("PF", (128, 2048), F32)
        self.pb = [b.psum(f"pb{i}", (128, 512), F32) for i in range(6)]
        self.pq = [b.psum("pq0", (128, 1024), BF16),
                   b.psum("pq1", (128, 1024), BF16)]
        self.s_pe = b.sem("s_pe")
        self.s_act = b.esem("scalar")
        self.s_dve = b.esem("vector")
        self.s_pool = b.esem("gpsimd")
        self._bank_free = {}
        self._uid = 0
        self._consts = {}

    def uid(self):
        self._uid += 1
        return self._uid

    def cv(self, name, tile):
        c = self.colidx[name] + tile
        return self.CB[:, c:c + 1]

    def bank_guard(self, bank, key=None):
        t = self._bank_free.get(key if key is not None else id(bank))
        if not t:
            return ()
        return t if isinstance(t, tuple) and t and isinstance(t[0], tuple) \
            else (t,)

    def bank_done(self, bank, tok, key=None):
        self._bank_free[key if key is not None else id(bank)] = tok

    def esem(self, eng):
        return {"scalar": self.s_act, "vector": self.s_dve,
                "gpsimd": self.s_pool, "tensor": self.s_pe}[eng]

    def const_f32(self, val):
        val = float(val)
        if val not in self._consts:
            ap = self.b.sbuf("c%d" % self.uid(), (128, 1), F32)
            self.b.emit("gpsimd", lambda e, a=ap, v=val: e.memset(a[:], v),
                        inc=((self.s_pool, 1),))
            self.b.nc.const_aps.register(F32, val, ap[:])
            self._consts[val] = ap
        return val

    def emit_init(self, cvec_ncol):
        b = self.b
        T = self.T
        xin = b.inp("xin", (D, T))
        b.dma("sync",
              self.SX[:, 0:FT * T].rearrange("p (ft t) -> p ft t", ft=FT),
              xin.ap().rearrange("(ft p) t -> p ft t", p=128))
        b.dma("sync", self.CB[:, 0:cvec_ncol],
              b.inp("cvec", (128, cvec_ncol)).ap())
        b.dma("gpsimd",
              self.MB[:, 0:4 * 512].rearrange("p (o n) -> p o n", o=4),
              b.inp("masks", (4, 128, 512)).ap().rearrange("o p n -> p o n"))
        b.dma("gpsimd", self.ID[:], b.inp("ident", (128, 128)).ap())
        b.dma("gpsimd", self.O1[:], b.inp("ones1", (1, 128)).ap())
        b.dma("gpsimd", self.OM[:], b.inp("onesm", (128, 1)).ap())
        self.const_f32(1e-5)
        self.const_f32(1.0 / 32.0)
        self.const_f32(-0.5)
        b.drain_dmas()
        b.block()

    def emit_store(self):
        b = self.b
        T = self.T
        out = self.b.nc.dram_tensor("xout", [D, T], F32, kind="ExternalOutput")
        s = b.sem("dma_out")
        b.dma("sync", out.ap().rearrange("(ft p) t -> p ft t", p=128),
              self.SX[:, 0:FT * T].rearrange("p (ft t) -> p ft t", ft=FT),
              wait=getattr(self, "ln_done", ()), sem=s)
        b.emit("sync", lambda e: e.nop(), wait=(s.tok(),))
        b.block()

    def emit_cast(self, dst=None):
        b = self.b
        T = self.T
        dst = dst if dst is not None else self.A1
        engs = ("scalar", "vector", "gpsimd")
        for ft in range(FT):
            eng = engs[ft % 3]
            sl = slice(ft * T, (ft + 1) * T)
            if eng == "scalar":
                b.emit("scalar", lambda e, s=sl: e.activation(
                    dst[:, s], self.SX[:, s], AF.Copy), inc=((self.s_act, 1),))
            else:
                b.emit(eng, lambda e, s=sl: e.tensor_copy(
                    dst[:, s], self.SX[:, s]), inc=((self.esem(eng), 1),))
        b.block()

    def warm(self, bank, wait=()):
        self.b.emit("tensor", lambda e, bank=bank: e.matmul(
            bank[0:1, 0:64], self.O1[0:1, 0:1], self.O1[0:1, 0:64],
            start=True, stop=True), wait=wait, inc=((self.s_pe, 1),))

    def emit_ln(self, kind, gname=None, bname=None, dst=None, castbuf=None,
                upto=99):
        b = self.b
        T = self.T
        cb = castbuf if castbuf is not None else self.A1
        wb = self.pb[5]
        b.block()
        apply_q = []

        def emit_apply(nb, at, mtile, rtile):
            for ft in range(FT):
                s = slice(ft * T + nb * 512, ft * T + nb * 512 + 512)
                eng = ("vector", "gpsimd")[ft % 2]
                sem = self.esem(eng)
                tgt = self.SX if kind == "plain" else dst
                b.emit(eng, lambda e, s=s, tgt=tgt, mtile=mtile: e.tensor_add(
                    tgt[:, s], self.SX[:, s], mtile), wait=(at,),
                    inc=((sem, 1),))
                b.emit(eng, lambda e, s=s, tgt=tgt, rtile=rtile: e.tensor_mul(
                    tgt[:, s], tgt[:, s], rtile), inc=((sem, 1),))
                if kind == "plain":
                    eng2 = ("gpsimd", "vector")[ft % 2]
                    b.emit(eng2, lambda e, s=s: e.tensor_copy(
                        self.A1[:, s], self.SX[:, s]), wait=(sem.tok(),))
                if kind == "affine":
                    et = sem.tok()
                    b.emit("scalar", lambda e, s=s, ft=ft: e.activation(
                        dst[:, s], dst[:, s], AF.Identity,
                        bias=self.cv(bname, ft), scale=self.cv(gname, ft)),
                        wait=(et,), inc=((self.s_act, 1),))
            par = nb % 2
            ldone = (self.s_dve.tok(), self.s_pool.tok(), self.s_act.tok())
            self.bank_done(None, ldone, ("PTm", par))
            self.bank_done(None, ldone, ("PTr", par))

        for nb in range(self.NB):
            par = nb % 2
            pmean = self.pb[0 + 2 * par]
            psq = self.pb[1 + 2 * par]
            srm = self.MB[0:1, par * 1024:par * 1024 + 512]
            srr = self.MB[0:1, par * 1024 + 512:par * 1024 + 1024]
            scr = self.PF[0:1, par * 1024 + 512:par * 1024 + 1024]
            mtile = self.PF[:, par * 1024:par * 1024 + 512]
            rtile = self.PF[:, par * 1024 + 512:par * 1024 + 1024]
            sq = [self.PT[:, 0:512], self.PT[:, 512:1024]]
            sqtok = [None] * FT
            for ft in range(FT):
                s = slice(ft * T + nb * 512, ft * T + nb * 512 + 512)
                ceng = ("scalar", "vector", "gpsimd")[ft % 3]
                csem = self.esem(ceng)
                if ceng == "scalar":
                    b.emit("scalar", lambda e, s=s: e.activation(
                        cb[:, s], self.SX[:, s], AF.Copy), inc=((csem, 1),))
                else:
                    b.emit(ceng, lambda e, s=s: e.tensor_copy(
                        cb[:, s], self.SX[:, s]), inc=((csem, 1),))
                ct = csem.tok()
                if ft >= 2:
                    w2 = (sqtok[ft - 2],) + (self.bank_guard(psq)
                                             if ft == 2 else ())
                    b.emit("tensor", lambda e, q=sq[ft % 2], ft=ft,
                           psq=psq: e.matmul(
                        psq[0:1, 0:512], self.OM[:], q,
                        start=(ft == 2), stop=False),
                        wait=w2, inc=((self.s_pe, 1),))
                    self.bank_done(None, self.s_pe.tok(), ("sq", ft % 2))
                qs = sq[ft % 2]
                b.emit("scalar", lambda e, s=s, q=qs: e.activation(
                    q, cb[:, s], AF.Square),
                    wait=(ct,) + self.bank_guard(qs, ("sq", ft % 2)))
                sqtok[ft] = self.s_act.tok()
                w1 = (ct,) + (self.bank_guard(pmean) if ft == 0 else ())
                b.emit("tensor", lambda e, s=s, ft=ft, pmean=pmean: e.matmul(
                    pmean[0:1, 0:512], self.OM[:], cb[:, s],
                    start=(ft == 0), stop=(ft == FT - 1)),
                    wait=w1, inc=((self.s_pe, 1),))
            for ft in (FT - 2, FT - 1):
                b.emit("tensor", lambda e, q=sq[ft % 2], ft=ft, psq=psq:
                       e.matmul(
                    psq[0:1, 0:512], self.OM[:], q,
                    start=False, stop=(ft == FT - 1)),
                    wait=(sqtok[ft],), inc=((self.s_pe, 1),))
                self.bank_done(None, self.s_pe.tok(), ("sq", ft % 2))
            pet = self.s_pe.tok()
            b.emit("scalar", lambda e, pmean=pmean, scr=scr: e.activation(
                scr, pmean[0:1, 0:512], AF.Square),
                wait=(pet,) + self.bank_guard(None, ("SC", par))
                + self.bank_guard(None, ("PTr", par)))
            sq_t = self.s_act.tok()
            self.warm(wb, (sq_t,))
            b.emit("vector", lambda e, psq=psq, scr=scr: e.tensor_sub(
                psq[0:1, 0:512], psq[0:1, 0:512], scr), wait=(sq_t,))
            self.bank_done(None, self.s_dve.tok(), ("SC", par))
            b.emit("vector", lambda e, pmean=pmean, srm=srm:
                   e.tensor_scalar_mul(srm, pmean[0:1, 0:512], -1.0),
                   wait=self.bank_guard(None, ("SRm", par)))
            dv_t = self.s_dve.tok()
            b.emit("scalar", lambda e, psq=psq: e.activation(
                psq[0:1, 0:512], psq[0:1, 0:512], AF.Ln, bias=1e-5),
                wait=(dv_t,))
            b.emit("scalar", lambda e, psq=psq, srr=srr: e.activation(
                srr, psq[0:1, 0:512], AF.Exp, scale=-0.5),
                wait=self.bank_guard(None, ("SRr", par)))
            rt = (dv_t, self.s_act.tok())
            self.bank_done(pmean, rt[1])
            self.bank_done(psq, rt[1])
            self.warm(wb, (rt[1],))
            b.emit("tensor", lambda e, pmean=pmean, srm=srm: e.matmul(
                pmean[:, 0:512], self.O1[:], srm, start=True, stop=True),
                wait=rt, inc=((self.s_pe, 1),))
            self.bank_done(None, self.s_pe.tok(), ("SRm", par))
            b.emit("tensor", lambda e, psq=psq, srr=srr: e.matmul(
                psq[:, 0:512], self.O1[:], srr, start=True, stop=True),
                wait=rt, inc=((self.s_pe, 1),))
            self.bank_done(None, self.s_pe.tok(), ("SRr", par))
            bt = self.s_pe.tok()
            b.emit("scalar", lambda e, pmean=pmean, mtile=mtile: e.activation(
                mtile, pmean[:, 0:512], AF.Copy),
                wait=(bt,) + self.bank_guard(None, ("PTm", par)))
            b.emit("scalar", lambda e, psq=psq, rtile=rtile: e.activation(
                rtile, psq[:, 0:512], AF.Copy),
                wait=self.bank_guard(None, ("PTr", par)))
            at = self.s_act.tok()
            self.bank_done(pmean, at)
            self.bank_done(psq, at)
            self.warm(wb, (at,))
            apply_q.append((nb, at, mtile, rtile))
            if len(apply_q) > 1:
                emit_apply(*apply_q.pop(0))
        while apply_q:
            emit_apply(*apply_q.pop(0))
        self.ln_done = (self.s_dve.tok(), self.s_pool.tok(),
                        self.s_act.tok())

    def proj(self, wname, Mt, Kt, src_bf, evac, wdram=None, banks=None,
             nb_range=None, src_off=0, src_T=None, mt_range=None,
             slot_range=None, wsem="s_wld", first_wait=()):
        b = self.b
        T = self.T
        sT = src_T if src_T is not None else T
        nbs = list(range(self.NB)) if nb_range is None else list(nb_range)
        mts = list(range(Mt)) if mt_range is None else list(mt_range)
        wdram = wdram if wdram is not None else b.inp(wname, (Mt, 128, Kt, 128))
        wsz = Kt * 128
        srange = list(slot_range) if slot_range is not None else \
            list(range(max(1, min(4, 4096 // wsz))))
        nslot = len(srange)
        assert nslot >= 2 or len(mts) == 1, "weight strip too large for ring"
        slots = [self.WR[:, i * wsz:(i + 1) * wsz] for i in srange]
        banks = banks or self.pb[0:4]
        s_w = b.sem(wsem)
        dtok = [None] * (len(mts) if mt_range is not None else Mt)
        pair_ok = (mt_range is None and nslot % 2 == 0)

        def issue_dma(i):
            if pair_ok:
                if i % 2 == 1:
                    return
                w = (self.bank_guard(None, ("WR", srange[i % nslot]))
                     + self.bank_guard(None, ("WR", srange[(i + 1) % nslot])))
                if i + 1 < len(mts) and mts[i + 1] == mts[i] + 1 \
                        and (i % nslot) + 1 == (i + 1) % nslot:
                    dst = self.WR[:, srange[i % nslot] * wsz:
                                  (srange[i % nslot] + 2) * wsz].rearrange(
                        "p (t kt m) -> p t kt m", t=2, kt=Kt)
                    dtok[i] = b.dma(
                        "gpsimd", dst,
                        wdram.ap()[mts[i]:mts[i] + 2].rearrange(
                            "t p kt m -> p t kt m"),
                        wait=w, sem=s_w)
                    dtok[i + 1] = dtok[i]
                    return
            w = self.bank_guard(None, ("WR", srange[i % nslot]))
            dtok[i] = b.dma(
                "gpsimd",
                slots[i % nslot].rearrange("p (kt m) -> p kt m", kt=Kt),
                wdram.ap()[mts[i]], wait=w, sem=s_w)

        nxt = min(nslot, len(mts))
        for i in range(nxt):
            issue_dma(i)
        for i, mt in enumerate(mts):
            sl = slots[i % nslot]
            for kt in range(Kt):
                for j, nb in enumerate(nbs):
                    bank = banks[j % len(banks)]
                    wt = ()
                    if kt == 0 and j == 0:
                        wt += (s_w.tok(),)
                        if i == 0:
                            wt += tuple(first_wait)
                    if kt == 0:
                        wt += self.bank_guard(bank)
                    rhs = src_bf[:, src_off + kt * sT + nb * 512:
                                 src_off + kt * sT + nb * 512 + 512]
                    b.emit("tensor", lambda e, sl=sl, kt=kt, rhs=rhs, bank=bank,
                           st=(kt == 0), sp=(kt == Kt - 1): e.matmul(
                        bank[:, 0:512], sl[:, kt * 128:kt * 128 + 128], rhs,
                        start=st, stop=sp),
                        wait=wt, inc=((self.s_pe, 1),))
            self.bank_done(None, self.s_pe.tok(), ("WR", srange[i % nslot]))
            lim = i + nslot - (1 if pair_ok else 0)
            while nxt < len(mts) and nxt <= lim:
                issue_dma(nxt)
                nxt += 1
            
            for j, nb in enumerate(nbs):
                evac(mt, nb, banks[j % len(banks)], self.s_pe.tok())

    def emit_l1_attn(self):
        b = self.b
        b.phase_tag = "l1attn"
        T = self.T
        NTT = self.NTT
        NB = self.NB
        SXb = self.SX[:, 0:FT * T].bitcast(BF16)
        P_P = 0
        P_PT = 4 * T
        P_AO = 8 * T
        if not hasattr(self, "xpark"):
            self.xpark = self.b.nc.dram_tensor("xpark", [128, FT * T], F32,
                                               kind="Internal")
        xpark = self.xpark
        pk = b.dma("sync", xpark.ap(), self.SX[:, 0:FT * T],
                   wait=getattr(self, "ln_done", ()))
        b.wait("sync", (pk,))
        dt0 = b.dma("gpsimd", self.MB[0:1, 0:D], b.inp("b3row", (1, D)).ap())
        for half in range(2):
            bk = self.pb[4]
            b.emit("tensor", lambda e, h=half, bk=bk: e.matmul(
                bk[:, 0:512], self.O1[:],
                self.MB[0:1, h * 512:h * 512 + 512],
                start=True, stop=True),
                wait=(dt0,) + self.bank_guard(bk), inc=((self.s_pe, 1),))
            b.emit("scalar", lambda e, h=half, bk=bk: e.activation(
                self.PF[:, h * 512:h * 512 + 512], bk[:, 0:512], AF.Copy),
                wait=(self.s_pe.tok(),), inc=((self.s_act, 1),))
            self.bank_done(bk, self.s_act.tok())
        b.block()
        w3 = b.inp("w3", (FT, 128, D))
        s_w = b.sem("s_wld")
        for vh in range(2):
            wv = None
            for kt in range(FT):
                sl = self.WR[:, kt * 512:(kt + 1) * 512]
                wv = b.dma("gpsimd", sl,
                           w3.ap()[kt, :, vh * 512:vh * 512 + 512], sem=s_w)
            for tt in range(NTT):
                bk = self.pb[tt % 2]
                for kt in range(FT):
                    wt = ()
                    if kt == 0:
                        wt = (wv,) + self.bank_guard(bk)
                        if vh == 0 and tt == 0:
                            wt += self.ln_done
                    b.emit("tensor", lambda e, kt=kt, tt=tt, bk=bk,
                           st=(kt == 0), sp=(kt == FT - 1): e.matmul(
                        bk[:, 0:512],
                        self.A1[:, kt * T + tt * 128:kt * T + tt * 128 + 128],
                        self.WR[:, kt * 512:kt * 512 + 512],
                        start=st, stop=sp), wait=wt, inc=((self.s_pe, 1),))
                pt = self.s_pe.tok()
                b.emit("vector", lambda e, tt=tt, vh=vh, bk=bk: e.tensor_add(
                    self.A2[:, tt * D + vh * 512:tt * D + vh * 512 + 512],
                    bk[:, 0:512], self.PF[:, vh * 512:vh * 512 + 512]),
                    wait=(pt,), inc=((self.s_dve, 1),))
                self.bank_done(bk, self.s_dve.tok())
            b.block()

        def evac_k(mt, nb, bank, tok):
            b.emit("scalar", lambda e, mt=mt, nb=nb, bank=bank: e.activation(
                self.A3[:, mt * T + nb * 512:mt * T + nb * 512 + 512],
                bank[:, 0:512], AF.Identity, bias=self.cv("b2", mt)),
                wait=(tok,), inc=((self.s_act, 1),))
            self.bank_done(bank, self.s_act.tok())

        self.proj("w2", FT, FT, self.A1, evac_k)
        b.block()

        w1 = b.inp("w1", (FT, 128, FT, 128))
        w4 = b.inp("w4", (FT, 128, FT, 128))
        for ns in range(NB):
            def evac_q(mt, nb, bank, tok):
                qg = self.bank_guard(None, "PPq") if mt == 0 else ()
                b.emit("scalar", lambda e, mt=mt, bank=bank: e.activation(
                    self.PP[:, mt * 512:mt * 512 + 512],
                    bank[:, 0:512], AF.Identity, bias=self.cv("b1", mt)),
                    wait=(tok,) + qg, inc=((self.s_act, 1),))
                self.bank_done(bank, self.s_act.tok())

            self.proj("w1", FT, FT, self.A1, evac_q, wdram=w1,
                      banks=self.pb[4:6], nb_range=(ns,), slot_range=(0, 1),
                      wsem="s_wldq")
            q_done = self.s_act.tok()
            for ntl in range(4):
                sbanks = [self.pb[i] for i in range(4)]
                for mb in range(NB):
                    bk = sbanks[mb % 4]
                    for kt in range(FT):
                        wt = ((q_done,) + self.bank_guard(bk)) \
                            if kt == 0 else ()
                        b.emit("tensor", lambda e, kt=kt, mb=mb, bk=bk,
                               ntl=ntl, st=(kt == 0), sp=(kt == FT - 1):
                               e.matmul(
                            bk[:, 0:512],
                            self.PP[:, kt * 512 + ntl * 128:
                                    kt * 512 + ntl * 128 + 128],
                            self.A3[:, kt * T + mb * 512:
                                    kt * T + mb * 512 + 512],
                            start=st, stop=sp),
                            wait=wt, inc=((self.s_pe, 1),))
                spt = self.s_pe.tok()
                for mb in range(NB):
                    b.emit("vector", lambda e, mb=mb: e.reduce_max(
                        self.SS[:, 8 + mb:9 + mb], sbanks[mb % 4][:, 0:512],
                        AX.X), wait=(spt,), inc=((self.s_dve, 1),))
                b.emit("vector", lambda e, NB=NB: e.tensor_reduce(
                    self.SS[:, 0:1], self.SS[:, 8:8 + NB], AX.X, ALU.max),
                    inc=((self.s_dve, 1),))
                b.emit("vector", lambda e: e.tensor_scalar_mul(
                    self.SS[:, 0:1], self.SS[:, 0:1], -1.0 / 32.0),
                    inc=((self.s_dve, 1),))
                mt_ = self.s_dve.tok()
                for mb in range(NB):
                    pw = self.bank_guard(None, "pSXb") \
                        if (ntl == 0 and mb == 0) else ()
                    b.emit("scalar", lambda e, mb=mb, ntl=ntl: e.activation(
                        SXb[:, P_P + ntl * T + mb * 512:
                            P_P + ntl * T + mb * 512 + 512],
                        sbanks[mb % 4][:, 0:512], AF.Exp,
                        bias=self.SS[:, 0:1], scale=1.0 / 32.0,
                        accum_out=self.SS[:, 16 + mb:17 + mb]),
                        wait=(mt_,) + pw, inc=((self.s_act, 1),))
                    self.bank_done(sbanks[mb % 4], self.s_act.tok())
                et = self.s_act.tok()
                b.emit("vector", lambda e, NB=NB: e.tensor_reduce(
                    self.SS[:, 1:2], self.SS[:, 16:16 + NB], AX.X, ALU.add),
                    wait=(et,), inc=((self.s_dve, 1),))
                b.emit("vector", lambda e: e.reciprocal(
                    self.SS[:, 2:3], self.SS[:, 1:2]), inc=((self.s_dve, 1),))
                rt = self.s_dve.tok()
                for mb in range(NB):
                    b.emit("vector", lambda e, mb=mb, ntl=ntl:
                           e.tensor_scalar_mul(
                        SXb[:, P_P + ntl * T + mb * 512:
                            P_P + ntl * T + mb * 512 + 512],
                        SXb[:, P_P + ntl * T + mb * 512:
                            P_P + ntl * T + mb * 512 + 512],
                        self.SS[:, 2:3]), wait=(rt,), inc=((self.s_dve, 1),))
            self.bank_done(None, self.s_pe.tok(), "PPq")
            psc = self.s_dve.tok()
            for mc in range(NTT):
                pqv = self.pq[mc % 2][:, 0:512]
                for ntl in range(4):
                    w0 = ((psc,) if (mc == 0 and ntl == 0) else ())
                    b.emit("tensor", lambda e, mc=mc, ntl=ntl, pqv=pqv:
                           e.transpose(
                        pqv[:, ntl * 128:ntl * 128 + 128],
                        SXb[:, P_P + ntl * T + mc * 128:
                            P_P + ntl * T + mc * 128 + 128],
                        self.ID[:]),
                        wait=w0 + (self.bank_guard(pqv, ("pq", mc % 2))
                                   if ntl == 0 else ()),
                        inc=((self.s_pe, 1),))
                tt_ = self.s_pe.tok()
                b.emit("vector", lambda e, mc=mc, pqv=pqv: e.tensor_copy(
                    SXb[:, P_PT + mc * 512:P_PT + mc * 512 + 512], pqv),
                    wait=(tt_,) + (self.bank_guard(None, "pTSXb")
                                   if mc == 0 else ()),
                    inc=((self.s_dve, 1),))
                self.bank_done(pqv, self.s_dve.tok(), ("pq", mc % 2))
            ptev = self.s_dve.tok()
            self.bank_done(None, self.s_pe.tok(), "pSXb")
            for dt_ in range(FT):
                bk = self.pb[4 + (dt_ % 2)]
                for mc in range(NTT):
                    wt = ((ptev,) + self.bank_guard(bk)) \
                        if mc == 0 else ()
                    b.emit("tensor", lambda e, dt_=dt_, mc=mc, bk=bk,
                           st=(mc == 0), sp=(mc == NTT - 1): e.matmul(
                        bk[:, 0:512],
                        self.A2[:, mc * D + dt_ * 128:mc * D + dt_ * 128 + 128],
                        SXb[:, P_PT + mc * 512:P_PT + mc * 512 + 512],
                        start=st, stop=sp), wait=wt, inc=((self.s_pe, 1),))
                pt = self.s_pe.tok()
                b.emit("scalar", lambda e, dt_=dt_, bk=bk: e.activation(
                    self.PP[:, 8 * 512 + dt_ * 512:8 * 512 + dt_ * 512 + 512],
                    bk[:, 0:512], AF.Copy),
                    wait=(pt,) + (self.bank_guard(None, "PPr")
                                  if dt_ == 0 else ()),
                    inc=((self.s_act, 1),))
                self.bank_done(bk, self.s_act.tok())
            retev = self.s_act.tok()
            self.bank_done(None, self.s_pe.tok(), "pTSXb")

            def evac_o(mt, nb, bank, tok, ns=ns):
                b.emit("scalar", lambda e, mt=mt, bank=bank: e.activation(
                    SXb[:, P_AO + mt * T + ns * 512:
                        P_AO + mt * T + ns * 512 + 512],
                    bank[:, 0:512], AF.Identity, bias=self.cv("b4", mt)),
                    wait=(tok,), inc=((self.s_act, 1),))
                self.bank_done(bank, self.s_act.tok())

            self.proj("w4", FT, FT, self.PP, evac_o, wdram=w4,
                      banks=self.pb[4:6], nb_range=(0,), src_off=8 * 512,
                      src_T=512, slot_range=(2, 3), first_wait=(retev,))
            self.bank_done(None, self.s_pe.tok(), "PPr")
        A2f = self.A2[:, 0:FT * T].bitcast(F32)
        A3f = self.A3[:, 0:FT * T].bitcast(F32)
        b.wait("sync", (self.s_pe.tok(),))
        b.dma("sync", A2f, self.xpark.ap()[:, 0:4 * T])
        b.dma("sync", A3f, self.xpark.ap()[:, 4 * T:8 * T])
        rl = b.sem("dma_sync").tok()
        ao_done = self.s_act.tok()
        for ft in range(FT):
            eng = ("vector", "gpsimd")[ft % 2]
            wt = (rl, ao_done, self.s_dve.tok(), self.s_pool.tok())
            b.emit(eng, lambda e, ft=ft: e.tensor_copy(
                self.SX[:, ft * T:ft * T + T],
                SXb[:, P_AO + ft * T:P_AO + ft * T + T]),
                wait=wt, inc=((self.esem(eng), 1),))
        cpt = (self.s_dve.tok(), self.s_pool.tok())
        for ft in range(FT):
            eng = ("vector", "gpsimd")[ft % 2]
            b.emit(eng, lambda e, ft=ft: e.tensor_add(
                self.SX[:, ft * T:ft * T + T],
                self.SX[:, ft * T:ft * T + T],
                (A2f if ft < 4 else A3f)[:, (ft % 4) * T:(ft % 4) * T + T]),
                wait=cpt, inc=((self.esem(eng), 1),))
        b.block()
        self.emit_ln("plain")

    def emit_l0_attn(self, upto=99):
        b = self.b
        b.phase_tag = "l0attn"
        T = self.T
        NB = self.NB
        NTT = self.NTT
        if not hasattr(self, "xpark"):
            self.xpark = self.b.nc.dram_tensor("xpark", [128, FT * T], F32,
                                               kind="Internal")
        pk = b.dma("sync", self.xpark.ap(), self.SX[:, 0:FT * T],
                   wait=getattr(self, "ln_done", ()))
        b.wait("sync", (pk,))
        b.block()
        P0 = self.PP[:, 0 * T:1 * T]
        P1 = self.PP[:, 1 * T:2 * T]
        P2 = self.PP[:, 2 * T:3 * T]
        P3 = self.PP[:, 3 * T:4 * T]
        P4 = self.PP[:, 4 * T:5 * T]
        gq = b.inp("gq", (NPAIR, 128, T))
        gk = b.inp("gk", (NPAIR, 128, T))
        for li in range(2):
            self.emit_ln("affine", "ln1_g%d" % li, "ln1_b%d" % li, dst=self.A1)
            if upto <= 1:
                return
            wq = b.inp("wq%d" % li, (FT, 128, FT, 128))
            wk = b.inp("wk%d" % li, (FT, 128, FT, 128))
            wv = b.inp("wv%d" % li, (FT, 128, FT, 128))
            wxo = b.inp("wxo%d" % li, (FT, 128, FT, 128))
            s_msk = b.sem("dma_mask")
            b.dma("sync",
                  self.PF[:, 0:2048].rearrange("p (o n) -> p o n", o=4),
                  b.inp("masks", (4, 128, 512)).ap().rearrange(
                      "o p n -> p o n"),
                  wait=(s_msk.tok(),) + self.ln_done
                  + self.bank_guard(None, "PFmask"),
                  sem=s_msk)
            mskf = s_msk.tok()
            self.bank_done(None, mskf, "PFmask")
            gq_tok = [None] * NPAIR
            gk_tok = [None] * NPAIR

            def issue_gq(p):
                s = b.sem("dma_gq")
                b.dma("gpsimd", P0, gq.ap()[p],
                      wait=self.bank_guard(P0, "P0"), sem=s)
                gq_tok[p] = s.tok()

            def issue_gk(p):
                s = b.sem("dma_gk")
                b.dma("gpsimd", P1, gk.ap()[p],
                      wait=self.bank_guard(P1, "P1"), sem=s)
                gk_tok[p] = s.tok()

            issue_gq(0)
            issue_gk(0)
            for p in range(NPAIR):

                def evac_q(mt, nb, bank, tok, li=li, p=p):
                    tm = self.PT[:, (nb % 2) * 512:(nb % 2) * 512 + 512]
                    b.emit("scalar", lambda e, bank=bank, tm=tm: e.activation(
                        tm, bank[:, 0:512], AF.Identity,
                        bias=self.cv("bq%d" % li, p)),
                        wait=(tok,) + self.bank_guard(None, ("sT", nb % 2)))
                    att = self.s_act.tok()
                    self.bank_done(bank, att)
                    b.emit("vector", lambda e, nb=nb, tm=tm: e.tensor_mul(
                        P2[:, nb * 512:nb * 512 + 512], tm,
                        P0[:, nb * 512:nb * 512 + 512]),
                        wait=(att, gq_tok[p]) + self.bank_guard(P2, "P2"))
                    self.bank_done(None, self.s_dve.tok(), ("sT", nb % 2))

                self.proj("", FT, FT, self.A1, evac_q, wdram=wq,
                          mt_range=(p,), slot_range=(0,), wsem="s_wld0",
                          first_wait=(self.ln_done if p == 0 else ()))
                self.bank_done(None, self.s_dve.tok(), "P2")

                def evac_k(mt, nb, bank, tok, li=li, p=p):
                    tm = self.PT[:, (nb % 2) * 512:(nb % 2) * 512 + 512]
                    b.emit("scalar", lambda e, bank=bank, tm=tm: e.activation(
                        tm, bank[:, 0:512], AF.Identity,
                        bias=self.cv("bk%d" % li, p)),
                        wait=(tok,) + self.bank_guard(None, ("sT", nb % 2)))
                    att = self.s_act.tok()
                    self.bank_done(bank, att)
                    b.emit("vector", lambda e, nb=nb, tm=tm: e.tensor_mul(
                        P3[:, nb * 512:nb * 512 + 512], tm,
                        P1[:, nb * 512:nb * 512 + 512]),
                        wait=(att, gk_tok[p]) + self.bank_guard(P3, "P3"))
                    self.bank_done(None, self.s_dve.tok(), ("sT", nb % 2))

                self.proj("", FT, FT, self.A1, evac_k, wdram=wk,
                          mt_range=(p,), slot_range=(1,), wsem="s_wld1")
                self.bank_done(None, self.s_dve.tok(), "P3")
                qk_done = self.s_dve.tok()

                def evac_v(mt, nb, bank, tok, li=li, p=p, qk=qk_done):
                    b.emit("scalar", lambda e, nb=nb, bank=bank: e.activation(
                        P0[:, nb * 512:nb * 512 + 512], bank[:, 0:512],
                        AF.Identity, bias=self.cv("bv%d" % li, p)),
                        wait=(tok, qk))
                    self.bank_done(bank, self.s_act.tok())

                self.proj("", FT, FT, self.A1, evac_v, wdram=wv,
                          mt_range=(p,), slot_range=(2,), wsem="s_wld2")
                vfm_done = self.s_act.tok()
                self.bank_done(None, vfm_done, "P0")

                def evac_x(mt, nb, bank, tok, li=li, p=p):
                    b.emit("scalar", lambda e, nb=nb, bank=bank: e.activation(
                        P4[:, nb * 512:nb * 512 + 512], bank[:, 0:512],
                        AF.Silu, bias=self.cv("bxo%d" % li, p)),
                        wait=(tok,) + self.bank_guard(P4, "P4"))
                    self.bank_done(bank, self.s_act.tok())

                self.proj("", FT, FT, self.A1, evac_x, wdram=wxo,
                          mt_range=(p,), slot_range=(3,), wsem="s_wld3")
                self.bank_done(None, self.s_act.tok(), "P4")
                if upto <= 2:
                    continue

                for tt in range(NTT):
                    pqv = self.pq[tt % 2][:, 0:128]
                    b.emit("tensor", lambda e, tt=tt, pqv=pqv: e.transpose(
                        pqv, P0[:, tt * 128:tt * 128 + 128], self.ID[:]),
                        wait=(vfm_done, qk_done)
                        + self.bank_guard(pqv, ("pq", tt % 2)),
                        inc=((self.s_pe, 1),))
                    ttk = self.s_pe.tok()
                    b.emit("vector", lambda e, tt=tt, pqv=pqv: e.tensor_copy(
                        P1[:, tt * 128:tt * 128 + 128], pqv), wait=(ttk,))
                    self.bank_done(pqv, self.s_dve.tok(), ("pq", tt % 2))
                vtm_done = self.s_dve.tok()
                self.bank_done(None, self.s_pe.tok(), "P0")
                if p + 1 < NPAIR:
                    issue_gq(p + 1)
                if upto <= 3:
                    continue
                qk_attn = (qk_done, self.s_act.tok())

                sT = [self.MB[:, i * 512:(i + 1) * 512] for i in range(4)]
                for nb in range(NB):
                    rbank = self.pb[4 + nb % 2]
                    M = 4 * nb + 4
                    evtok = [[None, None] for _ in range(M)]

                    def emit_mm2(mc, rbank=rbank, nb=nb, M=M):
                        for h in range(2):
                            b.emit("tensor", lambda e, h=h, mc=mc,
                                   rbank=rbank: e.matmul(
                                rbank[64 * h:64 * h + 64, 0:512],
                                P1[:, mc * 128 + 64 * h:
                                   mc * 128 + 64 * h + 64],
                                sT[2 * (mc % 2) + h],
                                start=(mc == 0), stop=(mc == M - 1),
                                skip_group_check=True),
                                wait=(evtok[mc][h],)
                                + (self.bank_guard(rbank)
                                   if mc == 0 else ()),
                                inc=((self.s_pe, 1),))
                        self.bank_done(None, self.s_pe.tok(),
                                       ("sTp", mc % 2))

                    for mc in range(M):
                        for h in range(2):
                            sb = self.pb[2 * (mc % 2) + h]
                            w0 = ()
                            if nb == 0 and mc == 0 and h == 0:
                                w0 = qk_attn + (vtm_done,)
                            b.emit("tensor", lambda e, h=h, mc=mc, nb=nb,
                                   sb=sb: e.matmul(
                                sb[:, 0:512],
                                P3[64 * h:64 * h + 64,
                                   mc * 128:mc * 128 + 128],
                                P2[64 * h:64 * h + 64,
                                   nb * 512:nb * 512 + 512],
                                start=True, stop=True),
                                wait=w0 + self.bank_guard(sb),
                                inc=((self.s_pe, 1),))
                        mmt = self.s_pe.tok()
                        if mc >= 2:
                            emit_mm2(mc - 2)
                        off = mc - 4 * nb
                        stg = self.bank_guard(None, ("sTp", mc % 2))
                        for h in range(2):
                            sb = self.pb[2 * (mc % 2) + h]
                            st = sT[2 * (mc % 2) + h]
                            if off < 0:
                                b.emit("scalar", lambda e, sb=sb, st=st:
                                       e.activation(st, sb[:, 0:512],
                                                    AF.Copy),
                                       wait=(mmt,) + (stg if h == 0 else ()))
                                self.bank_done(sb, self.s_act.tok())
                                evtok[mc][h] = self.s_act.tok()
                            else:
                                b.emit("vector", lambda e, sb=sb, st=st,
                                       off=off: e.tensor_mul(
                                    st, sb[:, 0:512],
                                    self.PF[:, off * 512:off * 512 + 512]),
                                    wait=(mmt, mskf)
                                    + (stg if h == 0 else ()))
                                self.bank_done(sb, self.s_dve.tok())
                                evtok[mc][h] = self.s_dve.tok()
                    for mc in range(max(0, M - 2), M):
                        emit_mm2(mc)
                    rt = self.s_pe.tok()
                    tm = self.PT[:, (nb % 2) * 512:(nb % 2) * 512 + 512]
                    b.emit("scalar", lambda e, tm=tm, rbank=rbank: e.activation(
                        tm, rbank[:, 0:512], AF.Copy),
                        wait=(rt,) + self.bank_guard(None, ("sT", nb % 2)))
                    att2 = self.s_act.tok()
                    self.bank_done(rbank, att2)
                    b.emit("vector", lambda e, nb=nb, p=p, tm=tm: e.tensor_mul(
                        self.A2[:, p * T + nb * 512:p * T + nb * 512 + 512],
                        tm, P4[:, nb * 512:nb * 512 + 512]),
                        wait=(att2,))
                    self.bank_done(None, self.s_dve.tok(), ("sT", nb % 2))
                self.bank_done(None, self.s_pe.tok(), "P2")
                self.bank_done(None, self.s_pe.tok(), "P3")
                self.bank_done(None, self.s_pe.tok(), "P1")
                self.bank_done(None, self.s_dve.tok(), "P4")
                if p + 1 < NPAIR:
                    issue_gk(p + 1)
            if upto <= 4:
                return

            retg_done = self.s_dve.tok()

            def evac_r(mt, nb, bank, tok, li=li):
                b.emit("vector", lambda e, mt=mt, nb=nb, bank=bank:
                       e.scalar_tensor_tensor(
                    self.SX[:, mt * T + nb * 512:mt * T + nb * 512 + 512],
                    bank[:, 0:512], self.cv("reto_b%d" % li, mt),
                    self.SX[:, mt * T + nb * 512:mt * T + nb * 512 + 512],
                    ALU.add, ALU.add), wait=(tok,))
                self.bank_done(bank, self.s_dve.tok())

            self.proj("reto%d" % li, FT, FT, self.A2, evac_r,
                      first_wait=(retg_done,))
            if upto <= 5:
                return

            self.emit_ln("affine", "ln2_g%d" % li, "ln2_b%d" % li, dst=self.A1)
            f1 = b.inp("ffn1_%d" % li, (32, 128, FT, 128))
            f2 = b.inp("ffn2_%d" % li, (FT, 128, 32, 128))
            for cg in range(4):
                def evac_g(mt, nb, bank, tok, cg=cg, li=li):
                    c = mt - cg * 8
                    a2w = self.bank_guard(None, "A2ffn") \
                        if (c == 0 and nb == 0) else ()
                    b.emit("scalar", lambda e, c=c, nb=nb, bank=bank, mt=mt:
                           e.activation(
                        self.A2[:, c * T + nb * 512:c * T + nb * 512 + 512],
                        bank[:, 0:512], AF.Gelu,
                        bias=self.cv("ffn1_b%d" % li, mt)),
                        wait=(tok,) + a2w)
                    self.bank_done(bank, self.s_act.tok())

                self.proj("", 32, FT, self.A1, evac_g, wdram=f1,
                          mt_range=range(cg * 8, cg * 8 + 8),
                          first_wait=(self.ln_done if cg == 0 else ()))
                act_done = self.s_act.tok()
                wsz = 8 * 128
                slots = [self.WR[:, i * wsz:(i + 1) * wsz] for i in range(4)]
                s_w = b.sem("s_wld")

                def issue_f2(mt, cg=cg, slots=slots, s_w=s_w, f2=f2):
                    b.dma("gpsimd",
                          slots[mt % 4].rearrange("p (kt m) -> p kt m", kt=8),
                          f2.ap()[mt, :, cg * 8:cg * 8 + 8, :],
                          wait=self.bank_guard(None, ("WR", mt % 4)), sem=s_w)

                for mt in range(4):
                    issue_f2(mt)
                for mt in range(FT):
                    sl = slots[mt % 4]
                    wtk = s_w.tok()
                    for nb in range(NB):
                        bank = self.pb[4 + nb % 2]
                        for c in range(8):
                            wt = ()
                            if c == 0:
                                wt = (wtk, act_done) + self.bank_guard(bank)
                            b.emit("tensor", lambda e, c=c, nb=nb, sl=sl,
                                   bank=bank, st=(c == 0), sp=(c == 7):
                                   e.matmul(
                                bank[:, 0:512], sl[:, c * 128:c * 128 + 128],
                                self.A2[:, c * T + nb * 512:
                                        c * T + nb * 512 + 512],
                                start=st, stop=sp),
                                wait=wt, inc=((self.s_pe, 1),))
                        ptk = self.s_pe.tok()
                        if cg == 0:
                            b.emit("vector", lambda e, mt=mt, nb=nb, bank=bank,
                                   li=li: e.scalar_tensor_tensor(
                                self.SX[:, mt * T + nb * 512:
                                        mt * T + nb * 512 + 512],
                                bank[:, 0:512],
                                self.cv("ffn2_b%d" % li, mt),
                                self.SX[:, mt * T + nb * 512:
                                        mt * T + nb * 512 + 512],
                                ALU.add, ALU.add), wait=(ptk,))
                        else:
                            b.emit("vector", lambda e, mt=mt, nb=nb, bank=bank:
                                   e.tensor_add(
                                self.SX[:, mt * T + nb * 512:
                                        mt * T + nb * 512 + 512],
                                self.SX[:, mt * T + nb * 512:
                                        mt * T + nb * 512 + 512],
                                bank[:, 0:512]), wait=(ptk,))
                        self.bank_done(bank, self.s_dve.tok())
                    self.bank_done(None, self.s_pe.tok(), ("WR", mt % 4))
                    if mt + 4 < FT:
                        issue_f2(mt + 4)
                self.bank_done(None, self.s_pe.tok(), "A2ffn")
        self.emit_ln("affine", "lnf_g", "lnf_b", dst=self.A1)
        A2f = self.A2[:, 0:FT * T].bitcast(F32)
        A3f = self.A3[:, 0:FT * T].bitcast(F32)
        b.dma("sync", A2f, self.xpark.ap()[:, 0:4 * T], wait=self.ln_done)
        b.dma("sync", A3f, self.xpark.ap()[:, 4 * T:8 * T])
        rl = b.sem("dma_sync").tok()
        for ft in range(FT):
            eng = ("vector", "gpsimd")[ft % 2]
            b.emit(eng, lambda e, ft=ft: e.tensor_add(
                self.SX[:, ft * T:ft * T + T],
                (A2f if ft < 4 else A3f)[:, (ft % 4) * T:(ft % 4) * T + T],
                self.A1[:, ft * T:ft * T + T]),
                wait=(rl,) + self.ln_done)
        b.block()
        self.emit_ln("plain")


    def emit_l0_ffn(self):
        b = self.b
        b.phase_tag = "l0ffn"
        T = self.T
        NB = self.NB
        NTT = self.NTT
        wrt = self.sbufs_wrt if hasattr(self, "sbufs_wrt") else None
        if wrt is None:
            self.sbufs_wrt = b.sbuf("WRT", (128, 64), BF16)
            wrt = self.sbufs_wrt
        b.dma("gpsimd", wrt[:].rearrange("p (kt e) -> p kt e", kt=FT),
              b.inp("wrouter", (FT, 128, 8)).ap().rearrange("kt p e -> p kt e"))
        wrow = self.b.nc.dram_tensor("wrow", [8, T], BF16, kind="Internal")
        rt = b.sem("dma_gpsimd").tok()
        b.wait("gpsimd", (rt,))
        b.block()
        SS = self.SS
        for tt in range(NTT):
            bk = self.pb[4 + tt % 2]
            for kt in range(FT):
                w0 = self.ln_done if (tt == 0 and kt == 0) else ()
                b.emit("tensor", lambda e, tt=tt, kt=kt, bk=bk,
                       st=(kt == 0), sp=(kt == FT - 1): e.matmul(
                    bk[:, 0:8],
                    self.A1[:, kt * T + tt * 128:kt * T + tt * 128 + 128],
                    wrt[:, kt * 8:kt * 8 + 8], start=st, stop=sp),
                    wait=w0 + (self.bank_guard(bk) if kt == 0 else ()),
                    inc=((self.s_pe, 1),))
            pt = self.s_pe.tok()
            b.emit("vector", lambda e, bk=bk: e.tensor_copy(
                SS[:, 0:8], bk[:, 0:8]), wait=(pt,))
            self.bank_done(bk, self.s_dve.tok())
            b.emit("vector", lambda e: e.reduce_max(
                SS[:, 8:9], SS[:, 0:8], AX.X))
            b.emit("vector", lambda e: e.tensor_scalar_mul(
                SS[:, 9:10], SS[:, 8:9], -1.0))
            b.emit("vector", lambda e: e.tensor_scalar(
                SS[:, 16:24], SS[:, 0:8], SS[:, 8:9], None, ALU.is_equal))
            b.emit("vector", lambda e: e.scalar_tensor_tensor(
                SS[:, 24:32], SS[:, 16:24], -1e30, SS[:, 0:8],
                ALU.mult, ALU.add))
            b.emit("vector", lambda e: e.reduce_max(
                SS[:, 10:11], SS[:, 24:32], AX.X))
            dv = self.s_dve.tok()
            b.emit("scalar", lambda e: e.activation(
                SS[:, 11:12], SS[:, 10:11], AF.Exp, bias=SS[:, 9:10]),
                wait=(dv,))
            at = self.s_act.tok()
            b.emit("vector", lambda e: e.tensor_scalar_add(
                SS[:, 12:13], SS[:, 11:12], 1.0), wait=(at,))
            b.emit("vector", lambda e: e.reciprocal(
                SS[:, 13:14], SS[:, 12:13]))
            b.emit("vector", lambda e: e.tensor_scalar(
                SS[:, 32:40], SS[:, 0:8], SS[:, 10:11], None, ALU.is_ge))
            dv2 = self.s_dve.tok()
            b.emit("scalar", lambda e: e.activation(
                SS[:, 40:48], SS[:, 0:8], AF.Exp, bias=SS[:, 9:10]),
                wait=(dv2,))
            at2 = self.s_act.tok()
            b.emit("vector", lambda e: e.scalar_tensor_tensor(
                SS[:, 48:56], SS[:, 40:48], SS[:, 13:14], SS[:, 32:40],
                ALU.mult, ALU.mult), wait=(at2,))
            dv3 = self.s_dve.tok()
            b.emit("scalar", lambda e: e.activation(
                self.PT[:, 896:904], SS[:, 48:56], AF.Copy), wait=(dv3,))
            at3 = self.s_act.tok()
            b.emit("tensor", lambda e, tt=tt: e.transpose(
                self.pq[0][0:8, 0:128], self.PT[:, 896:904], self.ID[:]),
                wait=(at3,) + self.bank_guard(None, "pqw"),
                inc=((self.s_pe, 1),))
            pt2 = self.s_pe.tok()
            b.emit("vector", lambda e: e.tensor_copy(
                self.PT[0:8, 512:640], self.pq[0][0:8, 0:128]),
                wait=(pt2,) + self.bank_guard(None, "wstg"))
            self.bank_done(None, self.s_dve.tok(), "pqw")
            dv4 = self.s_dve.tok()
            b.dma("sync", wrow.ap()[:, tt * 128:tt * 128 + 128],
                  self.PT[0:8, 512:640], wait=(dv4,))
            self.bank_done(None, b.sem("dma_sync").tok(), "wstg")
        b.wait("sync", (b.sem("dma_sync").tok(),))
        b.block()

        gw = b.inp("gatew", (16, 128, FT, 128))
        uw = b.inp("upw", (16, 128, FT, 128))
        dw = b.inp("downw", (FT, 128, 16, 128))
        s_w = b.sem("s_wld")
        gsl = [self.WR[:, i * 1024:(i + 1) * 1024] for i in range(2)]
        usl = [self.WR[:, 2048 + i * 1024:2048 + (i + 1) * 1024]
               for i in range(2)]
        for cg in range(2):
            def issue_gu(ci, cg=cg):
                c = cg * 8 + ci
                b.dma("gpsimd",
                      gsl[ci % 2].rearrange("p (kt m) -> p kt m", kt=FT),
                      gw.ap()[c],
                      wait=self.bank_guard(None, ("WR", ci % 2)), sem=s_w)
                b.dma("gpsimd",
                      usl[ci % 2].rearrange("p (kt m) -> p kt m", kt=FT),
                      uw.ap()[c],
                      wait=self.bank_guard(None, ("WR", 2 + ci % 2)), sem=s_w)

            issue_gu(0)
            issue_gu(1)
            for ci in range(8):
                wtk = s_w.tok()
                for nb in range(NB):
                    bg = self.pb[0 + nb % 2]
                    bu = self.pb[2 + nb % 2]
                    for kt in range(FT):
                        wt1 = ((wtk,) + self.bank_guard(bg)) if kt == 0 else ()
                        wt2 = ((wtk,) + self.bank_guard(bu)) if kt == 0 else ()
                        b.emit("tensor", lambda e, kt=kt, nb=nb, bg=bg,
                               sl=gsl[ci % 2], st=(kt == 0),
                               sp=(kt == FT - 1): e.matmul(
                            bg[:, 0:512], sl[:, kt * 128:kt * 128 + 128],
                            self.A1[:, kt * T + nb * 512:kt * T + nb * 512 + 512],
                            start=st, stop=sp), wait=wt1,
                            inc=((self.s_pe, 1),))
                        b.emit("tensor", lambda e, kt=kt, nb=nb, bu=bu,
                               sl=usl[ci % 2], st=(kt == 0),
                               sp=(kt == FT - 1): e.matmul(
                            bu[:, 0:512], sl[:, kt * 128:kt * 128 + 128],
                            self.A1[:, kt * T + nb * 512:kt * T + nb * 512 + 512],
                            start=st, stop=sp), wait=wt2,
                            inc=((self.s_pe, 1),))
                    ptk = self.s_pe.tok()
                    st_sl = self.PT[:, (nb % 2) * 512:(nb % 2) * 512 + 512]
                    sun = self.A2[:, ci * T + nb * 512:ci * T + nb * 512 + 512]
                    b.emit("scalar", lambda e, bg=bg, sun=sun: e.activation(
                        sun, bg[:, 0:512], AF.Silu), wait=(ptk,))
                    self.bank_done(bg, self.s_act.tok())
                    b.emit("scalar", lambda e, bu=bu, st_sl=st_sl: e.activation(
                        st_sl, bu[:, 0:512], AF.Copy),
                        wait=self.bank_guard(None, ("sst", nb % 2)))
                    self.bank_done(bu, self.s_act.tok())
                    at = self.s_act.tok()
                    b.emit("vector", lambda e, st_sl=st_sl, sun=sun:
                           e.tensor_mul(sun, sun, st_sl), wait=(at,))
                    self.bank_done(None, self.s_dve.tok(), ("sst", nb % 2))
                self.bank_done(None, self.s_pe.tok(), ("WR", ci % 2))
                self.bank_done(None, self.s_pe.tok(), ("WR", 2 + ci % 2))
                if ci + 2 < 8:
                    issue_gu(ci + 2)
            su_done = self.s_dve.tok()
            dsl = [self.WR[:, i * 1024:(i + 1) * 1024] for i in range(4)]

            def issue_d(mt, cg=cg, dsl=dsl, s_w=s_w, dw=dw):
                b.dma("gpsimd",
                      dsl[mt % 4].rearrange("p (kt m) -> p kt m", kt=8),
                      dw.ap()[mt, :, cg * 8:cg * 8 + 8, :],
                      wait=self.bank_guard(None, ("WR", mt % 4)), sem=s_w)

            b.wait("gpsimd", (self.s_pe.tok(),))
            for mt in range(4):
                issue_d(mt)
            for mt in range(FT):
                sl = dsl[mt % 4]
                wtk = s_w.tok()
                for nb in range(NB):
                    bank = self.pb[4 + nb % 2]
                    for c in range(8):
                        wt = ()
                        if c == 0:
                            wt = (wtk, su_done) + self.bank_guard(bank)
                        b.emit("tensor", lambda e, c=c, nb=nb, sl=sl, bank=bank,
                               st=(c == 0), sp=(c == 7): e.matmul(
                            bank[:, 0:512], sl[:, c * 128:c * 128 + 128],
                            self.A2[:, c * T + nb * 512:c * T + nb * 512 + 512],
                            start=st, stop=sp), wait=wt,
                            inc=((self.s_pe, 1),))
                    ptk = self.s_pe.tok()
                    b.emit("vector", lambda e, mt=mt, nb=nb, bank=bank:
                           e.tensor_add(
                        self.SX[:, mt * T + nb * 512:mt * T + nb * 512 + 512],
                        self.SX[:, mt * T + nb * 512:mt * T + nb * 512 + 512],
                        bank[:, 0:512]), wait=(ptk,))
                    self.bank_done(bank, self.s_dve.tok())
                self.bank_done(None, self.s_pe.tok(), ("WR", mt % 4))
                if mt + 4 < FT:
                    issue_d(mt + 4)
            b.block()

        fc1 = b.inp("fc1", (8, 128, 16384))
        fc2 = b.inp("fc2", (8, FT, 128, FT, 128))
        s_f = b.sem("dma_gpsimd")
        gtmp = self.PP[:, 2048:2048 + NB * 512]

        def wb_tiles(ex):
            ep = ex % 2
            return [self.PP[:, ep * 4096 + i * 512:ep * 4096 + (i + 1) * 512]
                    for i in range(NB)]

        def emit_wb(ex):
            ep = ex % 2
            wbt_t = wb_tiles(ex)
            s_wr = b.sem("dma_wrow%d" % ep)
            for rnd in range(0, NB, 2):
                js = [j for j in (0, 1) if rnd + j < NB]
                for j in js:
                    b.dma("sync", self.MB[0:1, (ep * 2 + j) * 512:
                                          (ep * 2 + j) * 512 + 512],
                          wrow.ap()[ex:ex + 1, (rnd + j) * 512:
                                    (rnd + j) * 512 + 512],
                          wait=self.bank_guard(None, ("wbr", ep, j)),
                          sem=s_wr)
                wbt = s_wr.tok()
                for j in js:
                    nb = rnd + j
                    bk = self.pb[4 + j]
                    b.emit("tensor", lambda e, j=j, bk=bk, ep=ep: e.matmul(
                        bk[:, 0:512], self.O1[:],
                        self.MB[0:1, (ep * 2 + j) * 512:
                                (ep * 2 + j) * 512 + 512],
                        start=True, stop=True),
                        wait=(wbt,) + self.bank_guard(bk),
                        inc=((self.s_pe, 1),))
                    ptk = self.s_pe.tok()
                    b.emit("scalar", lambda e, nb=nb, bk=bk, w=wbt_t[nb]:
                           e.activation(w, bk[:, 0:512], AF.Copy),
                        wait=(ptk,) + self.bank_guard(None, ("wbt", ep, nb)))
                    self.bank_done(bk, self.s_act.tok())
                    self.bank_done(None, self.s_act.tok(), ("wbr", ep, j))
            return self.s_act.tok()

        wb_done = emit_wb(0)
        for ex in range(8):
            wb_t = wb_tiles(ex)
            b.dma("gpsimd", self.A3[:, 0:16384], fc1.ap()[ex],
                  wait=(s_f.tok(),) + self.bank_guard(None, "A3fc1"),
                  sem=s_f)
            ftk = s_f.tok()
            a2g = self.bank_guard(None, "A2act")
            for i in range(8):
                ptoks = [None] * NB
                for nb in range(NB):
                    bank = self.pb[nb % 4]
                    for kt in range(FT):
                        wt = ()
                        if kt == 0:
                            wt = (ftk,) + self.bank_guard(bank)
                        b.emit("tensor", lambda e, i=i, kt=kt, nb=nb,
                               bank=bank, st=(kt == 0), sp=(kt == FT - 1):
                               e.matmul(
                            bank[:, 0:512],
                            self.A3[:, (i * FT + kt) * 128:
                                    (i * FT + kt) * 128 + 128],
                            self.A1[:, kt * T + nb * 512:
                                    kt * T + nb * 512 + 512],
                            start=st, stop=sp), wait=wt,
                            inc=((self.s_pe, 1),))
                    ptk = self.s_pe.tok()
                    b.emit("scalar", lambda e, i=i, nb=nb, bank=bank:
                           e.activation(
                        self.A2[:, i * 2048 + nb * 512:
                                i * 2048 + nb * 512 + 512],
                        bank[:, 0:512], AF.Silu),
                        wait=(ptk,) + (a2g if i == 0 and nb == 0 else ()))
                    self.bank_done(bank, self.s_act.tok())
                for nb in range(NB):
                    bank = self.pb[nb % 4]
                    for kt in range(FT):
                        wt = self.bank_guard(bank) if kt == 0 else ()
                        b.emit("tensor", lambda e, i=i, kt=kt, nb=nb,
                               bank=bank, st=(kt == 0), sp=(kt == FT - 1):
                               e.matmul(
                            bank[:, 0:512],
                            self.A3[:, ((i + 8) * FT + kt) * 128:
                                    ((i + 8) * FT + kt) * 128 + 128],
                            self.A1[:, kt * T + nb * 512:
                                    kt * T + nb * 512 + 512],
                            start=st, stop=sp), wait=wt,
                            inc=((self.s_pe, 1),))
                    ptk = self.s_pe.tok()
                    gsl_ = gtmp[:, nb * 512:nb * 512 + 512]
                    b.emit("scalar", lambda e, g=gsl_, bank=bank: e.activation(
                        g, bank[:, 0:512], AF.Copy),
                        wait=(ptk,) + self.bank_guard(None, ("gt", nb)))
                    self.bank_done(bank, self.s_act.tok())
                    at4 = self.s_act.tok()
                    aslot = self.A2[:, i * 2048 + nb * 512:
                                    i * 2048 + nb * 512 + 512]
                    b.emit("vector", lambda e, g=gsl_, aslot=aslot:
                           e.tensor_mul(aslot, aslot, g), wait=(at4,))
                    self.bank_done(None, self.s_dve.tok(), ("gt", nb))
                    b.emit("gpsimd", lambda e, w=wb_t[nb], aslot=aslot:
                           e.tensor_mul(aslot, aslot, w),
                           wait=(self.s_dve.tok(), wb_done))
            act_done = self.s_pool.tok()
            last_fc1 = self.s_pe.tok()
            self.bank_done(None, last_fc1, "A3fc1")
            for nb in range(NB):
                self.bank_done(None, act_done, ("wbt", ex % 2, nb))
            if ex + 1 < 8:
                wb_done = emit_wb(ex + 1)
            fsl = [self.WR[:, i * 1024:(i + 1) * 1024] for i in range(4)]

            def issue_fc2(dt, ex=ex, fsl=fsl, s_f=s_f, fc2=fc2):
                if dt % 2 == 1:
                    return
                w = (self.bank_guard(None, ("WR", dt % 4))
                     + self.bank_guard(None, ("WR", dt % 4 + 1)))
                b.dma("gpsimd",
                      self.WR[:, (dt % 4) * 1024:
                              (dt % 4 + 2) * 1024].rearrange(
                          "p (t kt m) -> p t kt m", t=2, kt=FT),
                      fc2.ap()[ex, dt:dt + 2].rearrange(
                          "t p kt m -> p t kt m"),
                      wait=w, sem=s_f)

            for dt in range(4):
                issue_fc2(dt)
            fnxt = 4
            for dt in range(FT):
                sl = fsl[dt % 4]
                wtk = s_f.tok()
                for nb in range(NB):
                    bank = self.pb[4 + nb % 2]
                    for it in range(FT):
                        wt = ()
                        if it == 0:
                            wt = (wtk, act_done) + self.bank_guard(bank)
                        b.emit("tensor", lambda e, it=it, nb=nb, sl=sl,
                               bank=bank, st=(it == 0), sp=(it == FT - 1):
                               e.matmul(
                            bank[:, 0:512], sl[:, it * 128:it * 128 + 128],
                            self.A2[:, it * 2048 + nb * 512:
                                    it * 2048 + nb * 512 + 512],
                            start=st, stop=sp), wait=wt,
                            inc=((self.s_pe, 1),))
                    ptk = self.s_pe.tok()
                    b.emit("vector", lambda e, dt=dt, nb=nb, bank=bank:
                           e.tensor_add(
                        self.SX[:, dt * T + nb * 512:dt * T + nb * 512 + 512],
                        self.SX[:, dt * T + nb * 512:dt * T + nb * 512 + 512],
                        bank[:, 0:512]), wait=(ptk,))
                    self.bank_done(bank, self.s_dve.tok())
                self.bank_done(None, self.s_pe.tok(), ("WR", dt % 4))
                while fnxt < FT and fnxt <= dt + 3:
                    issue_fc2(fnxt)
                    fnxt += 1
            self.bank_done(None, (self.s_pe.tok(), self.s_dve.tok(),
                                  self.s_pool.tok()), "A2act")
        self.emit_ln("plain")

    def emit_l1_ffn(self):
        b = self.b
        b.phase_tag = "l1ffn"
        T = self.T
        NB = self.NB
        mcd = self.b.nc.dram_tensor("mcdram", [16, 16384], BF16,
                                    kind="Internal")
        b.dma("gpsimd", self.PP[0:16, 0:4096].rearrange(
            "p (hr i) -> p hr i", hr=256),
            b.inp("makerT", (16, 4, 64, 16)).ap().rearrange(
                "j h r i -> j (h r) i"))
        b.dma("gpsimd", self.A2[0:16, 0:16384].rearrange(
            "p (hr k) -> p hr k", hr=256),
            b.inp("upT", (16, 4, 64, 64)).ap().rearrange(
                "j h r k -> j (h r) k"))
        mtk = b.sem("dma_gpsimd").tok()
        b.wait("gpsimd", (mtk,))
        b.block()
        for hr in range(256):
            bk = self.pb[hr % 2]
            b.emit("tensor", lambda e, hr=hr, bk=bk: e.matmul(
                bk[0:16, 0:64], self.PP[0:16, hr * 16:hr * 16 + 16],
                self.A2[0:16, hr * 64:hr * 64 + 64], start=True, stop=True),
                wait=self.bank_guard(bk), inc=((self.s_pe, 1),))
            ptk = self.s_pe.tok()
            b.emit("scalar", lambda e, hr=hr, bk=bk: e.activation(
                self.A3[0:16, hr * 64:hr * 64 + 64], bk[0:16, 0:64], AF.Copy),
                wait=(ptk,))
            self.bank_done(bk, self.s_act.tok())
        stg = self.s_act.tok()
        b.dma("sync", mcd.ap(), self.A3[0:16, 0:16384], wait=(stg,))
        b.wait("sync", (b.sem("dma_sync").tok(),))
        b.block()
        b.emit("gpsimd", lambda e: e.memset(self.A3[:, 0:16384], 0))
        b.wait("gpsimd", (self.s_pool.tok(),))
        b.block()
        mcv = mcd.ap().rearrange("i (h t a par k) -> i h t a par k",
                                 h=4, t=8, a=4, par=2)
        for a in range(4):
            for par in range(2):
                dst = self.A3[32 * a + 16 * par:32 * a + 16 * par + 16,
                              0:16384].rearrange(
                    "i (h t aa k) -> i h t aa k", h=4, t=8, aa=4)[
                        :, :, :, a, 64 * par:64 * par + 64]
                b.dma("sync", dst, mcv[:, :, :, a, par, :])
        b.wait("sync", (b.sem("dma_sync").tok(),))
        b.block()
        wc = b.inp("wcT", (16, FT, 128, 8, 128))
        s_w = b.sem("s_wld")
        csl = [self.WR[:, i * 1024:(i + 1) * 1024] for i in range(4)] + \
            [self.PP[:, i * 1024:(i + 1) * 1024] for i in range(8)]
        NCS = len(csl)
        for cg in range(16):
            mc_guard = self.bank_guard(None, "A2h")
            for ci in range(8):
                c = cg * 8 + ci
                pt_x = (c % 32) // 4
                for nb in range(NB):
                    bk = self.pb[2 + (ci * NB + nb) % 4]
                    w0 = self.ln_done if (cg == 0 and ci == 0 and nb == 0) \
                        else ()
                    b.emit("tensor", lambda e, c=c, nb=nb, bk=bk, pt_x=pt_x:
                           e.matmul(
                        bk[:, 0:512], self.A3[:, c * 128:c * 128 + 128],
                        self.A1[:, pt_x * T + nb * 512:pt_x * T + nb * 512 + 512],
                        start=True, stop=True),
                        wait=w0 + self.bank_guard(bk),
                        inc=((self.s_pe, 1),))
                    ptk = self.s_pe.tok()
                    b.emit("scalar", lambda e, ci=ci, nb=nb, bk=bk:
                           e.activation(
                        self.A2[:, ci * 2048 + nb * 512:
                                ci * 2048 + nb * 512 + 512],
                        bk[:, 0:512], AF.Gelu),
                        wait=(ptk,) + (mc_guard if ci == 0 and nb == 0
                                       else ()))
                    self.bank_done(bk, self.s_act.tok())
            h_done = self.s_act.tok()
            def issue_c(gi, cg=cg, s_w=s_w, wc=wc, csl=csl):
                if gi % 2 == 1:
                    return
                sidx = gi % NCS
                w = (self.bank_guard(None, ("cs", sidx))
                     + self.bank_guard(None, ("cs", sidx + 1)))
                if sidx < 4:
                    dst = self.WR[:, sidx * 1024:(sidx + 2) * 1024]
                else:
                    dst = self.PP[:, (sidx - 4) * 1024:(sidx - 2) * 1024]
                b.dma("gpsimd",
                      dst.rearrange("p (t kt m) -> p t kt m", t=2, kt=8),
                      wc.ap()[gi // FT, gi % FT:gi % FT + 2].rearrange(
                          "t p kt m -> p t kt m"),
                      wait=w, sem=s_w)

            if cg == 0:
                for gi in range(NCS):
                    issue_c(gi)
                self._cnxt = NCS
            for dt in range(FT):
                gi = cg * FT + dt
                sl = csl[gi % NCS]
                wtk = s_w.tok()
                for nb in range(NB):
                    bank = self.pb[(dt * NB + nb) % 2]
                    for ci in range(8):
                        wt = ()
                        if ci == 0:
                            wt = (wtk, h_done) + self.bank_guard(bank)
                        b.emit("tensor", lambda e, ci=ci, nb=nb, sl=sl,
                               bank=bank, st=(ci == 0), sp=(ci == 7):
                               e.matmul(
                            bank[:, 0:512], sl[:, ci * 128:ci * 128 + 128],
                            self.A2[:, ci * 2048 + nb * 512:
                                    ci * 2048 + nb * 512 + 512],
                            start=st, stop=sp), wait=wt,
                            inc=((self.s_pe, 1),))
                    ptk = self.s_pe.tok()
                    b.emit("vector", lambda e, dt=dt, nb=nb, bank=bank:
                           e.tensor_add(
                        self.SX[:, dt * T + nb * 512:dt * T + nb * 512 + 512],
                        self.SX[:, dt * T + nb * 512:dt * T + nb * 512 + 512],
                        bank[:, 0:512]), wait=(ptk,))
                    self.bank_done(bank, self.s_dve.tok())
                self.bank_done(None, self.s_pe.tok(), ("cs", gi % NCS))
                while self._cnxt < 16 * FT and self._cnxt <= gi + NCS - 1:
                    issue_c(self._cnxt)
                    self._cnxt += 1
            self.bank_done(None, self.s_pe.tok(), "A2h")
        self.emit_ln("plain")


_CACHE = {}


def _build(phases, T, colidx, ncol):
    key = (phases, T)
    if key in _CACHE:
        return _CACHE[key]
    nc = bass.Bass("TRN2", target_bir_lowering=False)
    mk = MK(nc, T, colidx)
    mk.emit_init(ncol)
    for ph in phases:
        if ph == "nop":
            pass
        elif ph == "cast":
            mk.emit_cast(mk.A1)
        elif ph == "ln":
            mk.emit_ln("plain")
        elif ph.startswith("l0u"):
            mk.emit_l0_attn(upto=int(ph[3:]))
        elif ph == "l1attn":
            mk.emit_l1_attn()
        elif ph == "l0attn":
            mk.emit_l0_attn()
        elif ph == "l0ffn":
            mk.emit_l0_ffn()
        elif ph == "l1ffn":
            mk.emit_l1_ffn()
        else:
            raise ValueError(ph)
    mk.emit_store()
    mk.b.finalize()
    names = sorted(mk.b.inputs.keys())
    _CACHE[key] = (nc, names)
    return nc, names


def _run_phases(phase_groups, host, x_fm, T):
    ncol = host["cvec"].shape[1]
    for phases in phase_groups:
        nc, names = _build(tuple(phases), T, host["_colidx"], ncol)
        in_maps = []
        for c in range(NCORES):
            m = {}
            for nname in names:
                m[nname] = x_fm[c] if nname == "xin" else host[nname]
            in_maps.append(m)
        res = run_bass_kernel_spmd(nc, in_maps, core_ids=list(range(NCORES)))
        x_fm = [np.asarray(res.results[c]["xout"]) for c in range(NCORES)]
    return x_fm


def kernel(**inputs):
    T = 2048
    host = prepare_host(inputs, T)
    x = np.asarray(inputs["L0_attn_x"], np.float32)
    x_fm = [np.ascontiguousarray(x[c].T) for c in range(NCORES)]
    groups = [["l0attn", "l0ffn", "l1attn", "l1ffn"]]
    x_fm = _run_phases(groups, host, x_fm, T)
    out = np.stack([x_fm[c].T for c in range(NCORES)], axis=0)
    return np.ascontiguousarray(out)
